# version 12
# baseline (speedup 1.0000x reference)
import numpy as np
import ml_dtypes
import concourse.bass as bass
import concourse.mybir as mybir
from concourse.bass_utils import run_bass_kernel_spmd

F32 = mybir.dt.float32
BF16 = mybir.dt.bfloat16
I32 = mybir.dt.int32
AX = mybir.AxisListType.X
ALU = mybir.AluOpType
ACT = mybir.ActivationFunctionType
BF = ml_dtypes.bfloat16

S, D, H, DH, I3 = 2048, 1024, 16, 64, 3072
NT, ND, NI = S // 128, D // 128, I3 // 128
NCORES = 8
QMAX = 32767.0
EPS12 = 1e-12

ENGS = ("tensor", "vector", "scalar", "gpsimd", "sync")


class Sched:

    def __init__(self, nc, es):
        self.nc = nc
        self.es = es
        self.steps = {e: [] for e in ENGS}
        names = list(ENGS) + ["cc"]
        self.sems = {n: es.enter_context(nc.semaphore("sem_" + n)) for n in names}
        self.cnt = {n: 0 for n in names}
        self.wm = {}

    def now(self, name):
        return (name, self.cnt[name])

    def stream(self, name):
        if name not in self.sems:
            self.sems[name] = self.es.enter_context(self.nc.semaphore("sem_" + name))
            self.cnt[name] = 0
        return name

    def add(self, eng, fn=None, waits=(), inc=None, n=1):
        todo = []
        for t in waits:
            if t is None:
                continue
            s, v = t
            key = (eng, s)
            if self.wm.get(key, 0) < v:
                self.wm[key] = v
                todo.append((self.sems[s], v))
        tok, sem = None, None
        if inc is not None:
            self.cnt[inc] += n
            tok = (inc, self.cnt[inc])
            sem = self.sems[inc]
        amt = n

        def thunk(obj, todo=todo, fn=fn, sem=sem, amt=amt):
            for (sm, v) in todo:
                obj.wait_ge(sm, v)
            ins = fn(obj) if fn is not None else None
            if sem is not None:
                ins.then_inc(sem, amt)

        self.steps[eng].append(thunk)
        return tok

    def dma(self, qeng, stream, out, in_, waits=()):
        return self.add(qeng, lambda o, out=out, in_=in_: o.dma_start(out=out, in_=in_),
                        waits=waits, inc=self.stream(stream), n=16)

    def dmat(self, stream, out, in_, waits=()):
        return self.add("sync",
                        lambda o, out=out, in_=in_: o.dma_start(out=out, in_=in_, transpose=True),
                        waits=waits, inc=self.stream(stream), n=16)

    def emit(self, block):
        for e in ENGS:
            thunks = self.steps[e]

            def body(obj, thunks=thunks):
                for t in thunks:
                    t(obj)
            getattr(block, e)(body)


class Mem:
    def __init__(self, nc):
        self.nc = nc
        self.guards = {}
        self.order = {}
        self.n = 0

    def _reg(self, name, g):
        t = g.__enter__()
        self.guards[name] = g
        self.order[name] = self.n
        self.n += 1
        return t

    def sb(self, name, shape, dt, side=None):
        return self._reg(name, self.nc.sbuf_tensor(name, shape, dt, side=side))

    def ps(self, name, shape, dt=F32):
        return self._reg(name, self.nc.psum_tensor(name, shape, dt))

    def free(self, *names):
        for n in sorted(names, key=lambda n: self.order[n], reverse=True):
            self.guards.pop(n).__exit__(None, None, None)


def build_nc():
    nc = bass.Bass("TRN2", target_bir_lowering=False)

    x_d = nc.dram_tensor("x", [S, D], F32, kind="ExternalInput")
    wT_d = {n: nc.dram_tensor("w" + n + "T", [D, D], F32, kind="ExternalInput")
            for n in ("q", "k", "v", "o")}
    wup_d = nc.dram_tensor("wup", [I3, D], F32, kind="ExternalInput")
    wgate_d = nc.dram_tensor("wgate", [I3, D], F32, kind="ExternalInput")
    wdown_d = nc.dram_tensor("wdown", [D, I3], F32, kind="ExternalInput")
    ident_d = nc.dram_tensor("ident", [128, 128], BF16, kind="ExternalInput")
    identf_d = nc.dram_tensor("identf", [128, 128], F32, kind="ExternalInput")
    dmask_d = nc.dram_tensor("dmask", [128, 2048], BF16, kind="ExternalInput")
    ones1_d = nc.dram_tensor("ones1", [1, 128], BF16, kind="ExternalInput")
    ones1f_d = nc.dram_tensor("ones1f", [1, 128], F32, kind="ExternalInput")
    onesc_d = nc.dram_tensor("onesc", [128, 1], BF16, kind="ExternalInput")
    out_d = nc.dram_tensor("out", [S, D], F32, kind="ExternalOutput")
    cc_in = [nc.dram_tensor(f"cc_in{r}", [1, 16], F32) for r in range(4)]
    cc_out = [nc.dram_tensor(f"cc_out{r}", [1, 16], F32) for r in range(4)]
    wdnT_d = nc.dram_tensor("wdnT", [I3, D], BF16)
    up_d = nc.dram_tensor("upsp", [I3, S], BF16)
    attn_d = nc.dram_tensor("attnsp", [S, D], F32)
    den_d = [nc.dram_tensor(f"densp{i}", [1, 512], F32) for i in range(4)]

    from contextlib import ExitStack
    es = ExitStack()
    sc = Sched(nc, es)
    mem = Mem(nc)

    ident = mem.sb("s_ident", [128, 128], BF16)
    identf = mem.sb("s_identf", [128, 128], F32)
    colws = mem.sb("s_colws", [128, 128], F32)
    ptre = mem.ps("s_ptre", [128, 512], F32)
    dmask = mem.sb("s_dmask", [128, 2048], BF16)
    ones1 = mem.sb("s_ones1", [1, 128], BF16)
    ones1f = mem.sb("s_ones1f", [1, 128], F32)
    onesc = mem.sb("s_onesc", [128, 1], BF16)
    trs = mem.sb("s_trs", [32, 32], F32)
    trs2 = mem.sb("s_trs2", [32, 32], F32)
    scal = mem.sb("scal", [128, 64], F32)
    ccs = mem.sb("ccs", [1, 64], F32)
    ccb = mem.sb("ccb", [128, 64], F32)
    cmx = mem.sb("cmx", [128, 8], F32)

    sc.dma("sync", "const", ident[:], ident_d[:, :])
    sc.dma("sync", "const", identf[:], identf_d[:, :])
    sc.dma("sync", "const", dmask[:], dmask_d[:, :])
    sc.dma("sync", "const", ones1[:], ones1_d[:, :])
    sc.dma("sync", "const", ones1f[:], ones1f_d[:, :])
    t_const = sc.dma("sync", "const", onesc[:], onesc_d[:, :])
    t_id = t_dm = t_o1 = t_o1f = t_const
    t_cmx0 = sc.add("vector", lambda v: v.memset(cmx[:], 0.0), inc="vector")
    t_trs0 = sc.add("vector", lambda v: v.memset(trs[:], 0.0), inc="vector")
    t_ccs0 = sc.add("vector", lambda v: v.memset(ccs[:], 0.0), inc="vector")
    t_cw0 = sc.add("vector", lambda v: v.memset(colws[:], 0.0), inc="vector")

    tree_busy = [None]

    def tree(col_ap, out_ap, op, p0, waits):
        t1 = sc.add("vector", lambda v, col_ap=col_ap, p0=p0: v.tensor_copy(
            out=colws[0:p0, 0:1], in_=col_ap),
            waits=tuple(waits) + (tree_busy[0], t_cw0), inc="vector")
        t2 = sc.add("tensor", lambda te: te.transpose(ptre[:, 0:128], colws[:], identf[:]),
                    waits=(t1, t_id, tree_busy[0]), inc="tensor")
        tok = sc.add("vector", lambda v, out_ap=out_ap, op=op, p0=p0: v.tensor_reduce(
            out=out_ap, in_=ptre[0:1, 0:p0], axis=AX, op=op), waits=(t2,), inc="vector")
        tree_busy[0] = tok
        return tok

    def allreduce(r, t_staged):
        t_w = sc.dma("gpsimd", "gcc", cc_in[r][:, :], ccs[0:1, 16 * r:16 * r + 16],
                     waits=(t_staged, t_ccs0))
        t_ar = sc.add("gpsimd", lambda g, r=r: g.collective_compute(
            "AllReduce", ALU.max, replica_groups=[list(range(NCORES))],
            ins=[cc_in[r].ap().opt()], outs=[cc_out[r].ap().opt()]),
            waits=(t_w,), inc="cc")
        return sc.dma("sync", "ccb", ccb[:, 16 * r:16 * r + 16],
                      cc_out[r].ap().partition_broadcast(128), waits=(t_ar,))

    xnT = mem.sb("xnT", [128, ND * S], BF16, side="right")
    xst = [mem.sb(f"xst{i}", [128, D], F32) for i in range(2)]
    xnst = [mem.sb(f"xnst{i}", [128, D], BF16) for i in range(2)]
    lns = mem.sb("lns", [128, 64], F32)
    sqd = mem.sb("sqd", [128, D], F32)
    ptr = [mem.ps(f"ptr{i}", [128, 128], BF16) for i in range(2)]

    xst_free = [None, None]
    xnst_free = [None, None]
    ptr_free = {}
    t_xn_done = None
    t_xnmax = None
    t_sums = {}
    for tt in range(NT):
        i = tt % 2
        t_ld = sc.dma("sync", f"xst{i}", xst[i][:], x_d[tt * 128:(tt + 1) * 128, :],
                      waits=(xst_free[i],))
        x_ap = xst[i][:]
        t1 = sc.add("vector", lambda v, x_ap=x_ap, tt=tt: v.reduce_sum(
            out=lns[:, tt:tt + 1], in_=x_ap, axis=AX), waits=(t_ld,), inc="vector")
        t2 = sc.add("vector", lambda v, x_ap=x_ap, tt=tt: v.scalar_tensor_tensor(
            out=sqd[:], in0=x_ap, scalar=1.0, in1=x_ap, op0=ALU.mult, op1=ALU.mult,
            accum_out=lns[:, 16 + tt:17 + tt]), waits=(t1,), inc="vector")
        t_sums[tt] = t2
        xst_free[tt % 2] = t2
    tb = sc.add("vector", lambda v: v.tensor_scalar(
        out=lns[:, 0:16], in0=lns[:, 0:16], scalar1=1.0 / D, scalar2=None,
        op0=ALU.mult), waits=(t_sums[NT - 1],), inc="vector")
    tb = sc.add("vector", lambda v: v.tensor_scalar(
        out=lns[:, 16:32], in0=lns[:, 16:32], scalar1=1.0 / D, scalar2=None,
        op0=ALU.mult), waits=(tb,), inc="vector")
    tb = sc.add("vector", lambda v: v.scalar_tensor_tensor(
        out=lns[:, 32:48], in0=lns[:, 0:16], scalar=1.0, in1=lns[:, 0:16],
        op0=ALU.mult, op1=ALU.mult), waits=(tb,), inc="vector")
    tb = sc.add("vector", lambda v: v.tensor_tensor(
        out=lns[:, 32:48], in0=lns[:, 16:32], in1=lns[:, 32:48],
        op=ALU.subtract), waits=(tb,), inc="vector")
    tb = sc.add("vector", lambda v: v.tensor_scalar(
        out=lns[:, 32:48], in0=lns[:, 32:48], scalar1=1e-5, scalar2=None,
        op0=ALU.add), waits=(tb,), inc="vector")
    tb = sc.add("scalar", lambda s: s.activation(
        out=lns[:, 48:64], in_=lns[:, 32:48], func=ACT.Sqrt), waits=(tb,), inc="scalar")
    tb = sc.add("vector", lambda v: v.reciprocal(
        out=lns[:, 32:48], in_=lns[:, 48:64]), waits=(tb,), inc="vector")
    t_stats = sc.add("vector", lambda v: v.scalar_tensor_tensor(
        out=lns[:, 48:64], in0=lns[:, 0:16], scalar=-1.0, in1=lns[:, 32:48],
        op0=ALU.mult, op1=ALU.mult), waits=(tb,), inc="vector")
    for tt in range(NT):
        i = tt % 2
        t_ld = sc.dma("sync", f"xst{i}", xst[i][:], x_d[tt * 128:(tt + 1) * 128, :],
                      waits=(xst_free[i],))
        x_ap = xst[i][:]
        n_ap = xnst[i][:]
        t7 = sc.add("vector", lambda v, x_ap=x_ap, n_ap=n_ap, tt=tt: v.tensor_scalar(
            out=n_ap, in0=x_ap, scalar1=lns[:, 32 + tt:33 + tt],
            scalar2=lns[:, 48 + tt:49 + tt], op0=ALU.mult, op1=ALU.add),
            waits=(t_ld, t_stats, xnst_free[i]), inc="vector")
        xst_free[i] = t7
        t8 = sc.add("vector", lambda v, n_ap=n_ap: v.reduce_max(
            out=lns[:, 0:1], in_=n_ap, axis=AX, apply_absolute_value=True),
            waits=(t7, t_cmx0), inc="vector")
        t_xnmax = sc.add("vector", lambda v: v.tensor_tensor(
            out=cmx[:, 0:1], in0=cmx[:, 0:1], in1=lns[:, 0:1], op=ALU.max),
            waits=(t8,), inc="vector")
        last_tr = None
        for dt in range(ND):
            pp = ptr[dt % 2]
            tmm = sc.add("tensor", lambda te, n_ap=n_ap, dt=dt, pp=pp: te.transpose(
                pp[:], n_ap[:, dt * 128:(dt + 1) * 128], ident[:]),
                waits=(t7, t_id, ptr_free.get(dt % 2)), inc="tensor")
            tev = sc.add("vector", lambda v, pp=pp, dt=dt, tt=tt: v.tensor_copy(
                out=xnT[:, dt * S + tt * 128: dt * S + (tt + 1) * 128], in_=pp[:]),
                waits=(tmm,), inc="vector")
            ptr_free[dt % 2] = tev
            last_tr = tmm
        xnst_free[i] = last_tr
        t_xn_done = tev
    mem.free("xst0", "xst1", "xnst0", "xnst1", "lns", "sqd", "ptr0", "ptr1")

    t_st0 = tree(cmx[:, 0:1], ccs[0:1, 0:1], ALU.max, 128, (t_xnmax,))
    t_bc0 = allreduce(0, t_st0)
    t_sx = sc.add("vector", lambda v: v.tensor_scalar(out=scal[:, 0:1], in0=ccb[:, 0:1],
                  scalar1=1.0 / QMAX, scalar2=EPS12, op0=ALU.mult, op1=ALU.max),
                  waits=(t_bc0,), inc="vector")
    t_rx = sc.add("vector", lambda v: v.reciprocal(out=scal[:, 1:2], in_=scal[:, 0:1]),
                  waits=(t_sx,), inc="vector")

    jw1 = (sc.now("vector"), sc.now("scalar"), sc.now("tensor"))
    wsb = {n: mem.sb("w" + n, [128, ND * D], BF16, side="right") for n in ("q", "k", "v")}
    t_wld = {}
    for n in ("q", "k", "v"):
        tk = None
        for kd in range(ND):
            tk = sc.dma("gpsimd", "wcast", wsb[n][:, kd * D:(kd + 1) * D],
                        wT_d[n][kd * 128:(kd + 1) * 128, :], waits=jw1)
        t_wld[n] = tk
    for n in ("q", "k", "v"):
        t_wld[n] = tk

    kT2 = mem.sb("kT2", [128, S], BF16)
    qT2 = mem.sb("qT2", [128, S], BF16)
    vx = [mem.sb(f"vx{i}", [128, NT * 65], BF16) for i in range(2)]
    nsc = mem.sb("nsc", [128, S], BF16)
    expw = [mem.sb(f"expw{i}", [128, NT * 512], BF16) for i in range(2)]
    zT2 = mem.sb("zT2", [128, ND * S], BF16)
    zTo = mem.sb("zTo", [64, S], BF16)
    zsc = [mem.sb(f"zsc{i}", [65, 512], F32) for i in range(4)]
    denbc = [mem.sb(f"denbc{i}", [64, 512], F32) for i in range(4)]
    denbr = [mem.sb(f"denbr{i}", [64, 512], F32) for i in range(4)]
    ps_ = [mem.ps(f"ps{i}", [128, 1024]) for i in range(2)]
    pzx = mem.ps("pzx", [128, 1024])
    pzb = [pzx[0:65, 0:512], ptre[0:65, 0:512]]
    pnr = pzx[0:1, 512:1024]
    pmx = mem.ps("pmx", [128, 512])
    prb = pmx[0:64, 0:512]

    t_vones = [sc.add("vector", lambda v, i=i: v.memset(
        vx[i][:].rearrange("p (t c) -> p t c", c=65)[:, :, 64:65], 1.0), inc="vector")
        for i in range(2)]

    qkv_hist = []
    vx_free = [None, None]
    prev_exp = None
    pz_free = {0: None, 1: None}
    prb_free = None
    zn_prev = [None, None, None, None]
    t_qkv_last = None
    t_attn_last_te = None
    pmx_free = None
    hp_state = {}
    zto_free = None
    t_zrepack = {}

    def qkv_slot(idx):
        return pzx[0:128, 0:512] if idx % 2 == 0 else pmx[0:128, 0:512]

    for hp in range(H // 2):
        for wnm, dst in (("k", kT2), ("q", qT2)):
            for c in range(4):
                pp = qkv_slot(len(qkv_hist))
                gate = qkv_hist[-2] if len(qkv_hist) >= 2 else None
                tmm = None
                for kd in range(ND):
                    tmm = sc.add("tensor", lambda te, pp=pp, wnm=wnm, kd=kd, hp=hp, c=c: te.matmul(
                        pp, wsb[wnm][:, kd * D + hp * 128: kd * D + hp * 128 + 128],
                        xnT[:, kd * S + c * 512: kd * S + (c + 1) * 512],
                        start=(kd == 0), stop=(kd == ND - 1)),
                        waits=(t_xn_done, t_wld[wnm], gate,
                               pz_free[0] if c == 0 and wnm == "k" else None,
                               pz_free[1] if c == 0 and wnm == "k" else None,
                               pmx_free if c == 0 and wnm == "k" else None,
                               hp_state.get("score_last") if c == 0 and wnm == "k" else None),
                        inc="tensor" if kd == ND - 1 else None)
                tev = sc.add("vector", lambda v, pp=pp, c=c, dst=dst: v.tensor_copy(
                    out=dst[:, c * 512:(c + 1) * 512], in_=pp),
                    waits=(tmm,), inc="vector")
                qkv_hist.append(tev)
        for tt in range(NT):
            pp = qkv_slot(len(qkv_hist))
            gate = qkv_hist[-2] if len(qkv_hist) >= 2 else None
            tmm = None
            for kd in range(ND):
                tmm = sc.add("tensor", lambda te, pp=pp, kd=kd, tt=tt, hp=hp: te.matmul(
                    pp[:, 0:128], xnT[:, kd * S + tt * 128: kd * S + (tt + 1) * 128],
                    wsb["v"][:, kd * D + hp * 128: kd * D + hp * 128 + 128],
                    start=(kd == 0), stop=(kd == ND - 1)),
                    waits=(t_xn_done, t_wld["v"], gate,
                           vx_free[0] if tt == 0 else None,
                           vx_free[1] if tt == 0 else None),
                    inc="tensor" if kd == ND - 1 else None)
            tev = None
            for half in range(2):
                tev = sc.add("vector", lambda v, pp=pp, half=half, tt=tt: v.tensor_copy(
                    out=vx[half][:, tt * 65: tt * 65 + 64],
                    in_=pp[:, 64 * half:64 * half + 64]),
                    waits=(tmm, t_vones[half]), inc="vector")
            qkv_hist.append(tev)
        t_qkv_last = tmm

        bias_tok = {}
        pmx_busy = hp_state.get("pmx_busy")
        nsc_reader = hp_state.get("nsc_busy")
        nrm_prev = hp_state.get("nrm_prev")
        nrm_tok = {}
        for j, wch in enumerate(("q", "k")):
            srcT = qT2 if wch == "q" else kT2
            tsq = sc.add("vector", lambda v, srcT=srcT: v.tensor_tensor(
                out=nsc[:], in0=srcT[:], in1=srcT[:], op=ALU.mult),
                waits=(qkv_hist[-1], prev_exp, nsc_reader), inc="vector")
            for half in range(2):
                sl = 8 + 2 * half + j
                tred = None
                for c in range(4):
                    tm = sc.add("tensor", lambda te, c=c, half=half: te.matmul(
                        pnr, onesc[64 * half:64 * half + 64, :],
                        nsc[64 * half:64 * half + 64, c * 512:(c + 1) * 512],
                        start=True, stop=True),
                        waits=(tsq, t_o1f, tred, pmx_busy, hp_state.get("pnr_busy")),
                        inc="tensor")
                    tred = sc.add("vector", lambda v, c=c: v.reduce_max(
                        out=scal[0:1, 30 + c:31 + c], in_=pnr, axis=AX),
                        waits=(tm, nrm_prev), inc="vector")
                nsc_reader = tm
                hp_state["pnr_busy"] = tred
                tok = sc.add("vector", lambda v: v.tensor_tensor(
                    out=scal[0:1, 30:31], in0=scal[0:1, 30:31], in1=scal[0:1, 31:32],
                    op=ALU.max), waits=(tred,), inc="vector")
                tok = sc.add("vector", lambda v: v.tensor_tensor(
                    out=scal[0:1, 32:33], in0=scal[0:1, 32:33], in1=scal[0:1, 33:34],
                    op=ALU.max), waits=(tok,), inc="vector")
                nrm_tok[(wch, half)] = nrm_prev = sc.add(
                    "vector", lambda v, sl=sl: v.tensor_tensor(
                        out=scal[0:1, sl:sl + 1], in0=scal[0:1, 30:31],
                        in1=scal[0:1, 32:33], op=ALU.max), waits=(tok,), inc="vector")
        hp_state["nsc_busy"] = nsc_reader
        hp_state["nrm_prev"] = nrm_prev
        for half in range(2):
            toks = []
            for j, wch in enumerate(("q", "k")):
                sl = 8 + 2 * half + j
                tok = sc.add("tensor", lambda te, sl=sl, half=half, j=j: te.matmul(
                    pmx[:, 2 + 2 * half + j:3 + 2 * half + j], ones1f[:],
                    scal[0:1, sl:sl + 1], start=True, stop=True),
                    waits=(nrm_tok[(wch, half)], t_o1f, prb_free, pmx_busy,
                           qkv_hist[-1]), inc="tensor")
                toks.append(tok)
            tok = sc.add("vector", lambda v, half=half: v.tensor_copy(
                out=scal[:, 34:35], in_=pmx[:, 2 + 2 * half:3 + 2 * half]),
                waits=tuple(toks), inc="vector")
            tok = sc.add("vector", lambda v, half=half: v.tensor_tensor(
                out=scal[:, 16 + half:17 + half], in0=scal[:, 34:35],
                in1=pmx[:, 3 + 2 * half:4 + 2 * half], op=ALU.mult),
                waits=(tok,), inc="vector")
            pmx_busy = tok
            pmx_free = tok
            tok = sc.add("scalar", lambda s2, half=half: s2.activation(
                out=scal[:, 18 + half:19 + half], in_=scal[:, 16 + half:17 + half],
                func=ACT.Sqrt), waits=(tok,), inc="scalar")
            bias_tok[half] = sc.add("vector", lambda v, half=half: v.tensor_scalar(
                out=scal[:, 20 + half:21 + half], in0=scal[:, 18 + half:19 + half],
                scalar1=-0.125, scalar2=None, op0=ALU.mult), waits=(tok,), inc="vector")
        hp_state["pmx_busy"] = pmx_busy

        pending = None

        def emit_av_chunk_tok(st):
            nonlocal t_attn_last_te
            p_hp, p_qb, p_half, p_nkt, p_gates, pos = st[:6]
            if pos >= p_nkt:
                return st[7]
            b = p_half
            end = min(pos + 2, p_nkt)
            tav = None
            for kt in range(pos, end):
                last = (kt == p_nkt - 1)
                tav = sc.add("tensor", lambda te, kt=kt, p_half=p_half, p_nkt=p_nkt, b=b: te.matmul(
                    pzb[b], vx[p_half][:, kt * 65:(kt + 1) * 65],
                    expw[p_half][:, kt * 512:(kt + 1) * 512],
                    start=(kt == 0), stop=(kt == p_nkt - 1)),
                    waits=(p_gates[kt], pz_free[b] if kt == 0 else None),
                    inc="tensor")
                if last:
                    t_attn_last_te = tav
                    hp_state["av_last_%d" % p_half] = tav
                    st[6] = tav
            st[5] = end
            st[7] = tav
            return tav

        def finish_av(st):
            nonlocal prb_free, zto_free
            while st[5] < st[3]:
                emit_av_chunk_tok(st)
            p_hp, p_qb, p_half, p_nkt, p_gates, pos, tav, _ = st
            ui = hp_state.get("ui", 0)
            hp_state["ui"] = ui + 1
            w = ui % 4
            b = p_half
            t_ev = sc.add("vector", lambda v, w=w, b=b: v.tensor_copy(
                out=zsc[w][:], in_=pzb[b]), waits=(tav, zn_prev[w]), inc="vector")
            pz_free[b] = t_ev
            t_dw = sc.dma("gpsimd", "gdwr%d" % w, den_d[w][:, :], zsc[w][64:65, :],
                          waits=(t_ev,))
            t_db = sc.dma("gpsimd", "gdbc%d" % w, denbc[w][:],
                          den_d[w].ap().partition_broadcast(64), waits=(t_dw,))
            t_rc = sc.add("vector", lambda v, w=w: v.reciprocal(
                out=denbr[w][:], in_=denbc[w][:]), waits=(t_db, zn_prev[w]),
                inc="vector")
            if p_half == 0:
                out_ap = zT2[0:64, p_hp * S + p_qb * 512: p_hp * S + (p_qb + 1) * 512]
                t_zn = sc.add("vector", lambda v, out_ap=out_ap, w=w: v.tensor_tensor(
                    out=out_ap, in0=zsc[w][0:64, :], in1=denbr[w][:], op=ALU.mult),
                    waits=(t_rc,), inc="vector")
            else:
                out_ap = zTo[:, p_qb * 512:(p_qb + 1) * 512]
                t_zn = sc.add("vector", lambda v, out_ap=out_ap, w=w: v.tensor_tensor(
                    out=out_ap, in0=zsc[w][0:64, :], in1=denbr[w][:], op=ALU.mult),
                    waits=(t_rc, zto_free), inc="vector")
                if p_qb == 3:
                    zto_free = sc.dma("gpsimd", "gzto",
                                      zT2[64:128, p_hp * S:(p_hp + 1) * S], zTo[:],
                                      waits=(t_zn,))
                    t_zrepack[p_hp] = zto_free
            zn_prev[w] = t_zn
            prb_free = t_zn
            return t_zn

        exp_prev2 = [hp_state.get("exp_prev_0"), hp_state.get("exp_prev_1")]
        for qb in range(4):
            nkt = 4 * qb + 4
            exp_toks = {0: {}, 1: {}}
            gates2 = {0: {}, 1: {}}
            tmm_last = {}
            for g in range(nkt // 2):
                for half in range(2):
                    p0, p1_ = 64 * half, 64 * half + 64
                    tmm2 = None
                    for kt in (2 * g, 2 * g + 1):
                        slot = kt % 2
                        tmm2 = sc.add("tensor",
                            lambda te, kt=kt, qb=qb, p0=p0, p1_=p1_, half=half, slot=slot: te.matmul(
                                ps_[half][:, slot * 512:(slot + 1) * 512],
                                kT2[p0:p1_, kt * 128:(kt + 1) * 128],
                                qT2[p0:p1_, qb * 512:(qb + 1) * 512],
                                start=True, stop=True),
                            waits=(exp_toks[half].get(g - 1), qkv_hist[-1],
                                   exp_prev2[half] if g == 0 else None,
                                   hp_state.get("av_last_%d" % half) if g == 0 else None),
                            inc="tensor" if kt % 2 == 1 else None)
                    tmm_last[half] = tmm2
                ctoks = [None, None]
                if pending is not None:
                    ctoks[0] = emit_av_chunk_tok(pending[0])
                    ctoks[1] = emit_av_chunk_tok(pending[1])
                for half in range(2):
                    bias_ap = scal[:, 20 + half:21 + half]
                    texp = sc.add("scalar",
                        lambda s2, g=g, half=half, bias_ap=bias_ap: s2.activation(
                            out=expw[half][:, g * 1024:(g + 1) * 1024],
                            in_=ps_[half][:, 0:1024], func=ACT.Exp,
                            scale=0.125, bias=bias_ap),
                        waits=(tmm_last[half], bias_tok[half], ctoks[half]),
                        inc="scalar")
                    exp_toks[half][g] = texp
                    for kt in (2 * g, 2 * g + 1):
                        d = kt - 4 * qb
                        if d >= 0:
                            gates2[half][kt] = sc.add("gpsimd",
                                lambda g2, half=half, kt=kt, d=d: g2.tensor_tensor(
                                    out=expw[half][:, kt * 512:(kt + 1) * 512],
                                    in0=expw[half][:, kt * 512:(kt + 1) * 512],
                                    in1=dmask[:, d * 512:(d + 1) * 512], op=ALU.mult),
                                waits=(texp, t_dm), inc="gpsimd")
                        else:
                            gates2[half][kt] = texp
            prev_exp = exp_toks[1][nkt // 2 - 1]
            exp_prev2 = [exp_toks[0][nkt // 2 - 1], exp_toks[1][nkt // 2 - 1]]
            hp_state["score_last"] = tmm_last[1]
            if pending is not None:
                finish_av(pending[0])
                finish_av(pending[1])
            pending = [[hp, qb, 0, nkt, gates2[0], 0, None, None],
                       [hp, qb, 1, nkt, gates2[1], 0, None, None]]
        finish_av(pending[0])
        t_zn_last = finish_av(pending[1])
        hp_state["exp_prev_0"] = exp_prev2[0]
        hp_state["exp_prev_1"] = exp_prev2[1]
        vx_free = [t_attn_last_te, t_attn_last_te]
    t_attn_z = t_zn_last

    mem.free("wq", "wk", "wv")
    wo2 = mem.sb("wo2", [128, ND * D], BF16, side="right")
    tk = None
    for kd in range(ND):
        tk = sc.dma("gpsimd", "wcast", wo2[:, kd * D:(kd + 1) * D],
                    wT_d["o"][kd * 128:(kd + 1) * 128, :], waits=(qkv_hist[-1],))
    t_wo_ld = tk
    t_zrp = tuple(t_zrepack[p] for p in range(H // 2))
    wos = [mem.sb(f"wos{i}", [128, D], F32) for i in range(2)]
    wos_free = [None, None]
    wo_sp_fin = [None, None]
    t_wo_sp = None
    for tt in range(NT):
        i = tt % 2
        t_ev = None
        for c in range(2):
            pp = qkv_slot(len(qkv_hist))
            gate = qkv_hist[-2] if len(qkv_hist) >= 2 else None
            tmm = None
            for kd in range(ND):
                tmm = sc.add("tensor", lambda te, pp=pp, kd=kd, tt=tt, c=c: te.matmul(
                    pp, zT2[:, kd * S + tt * 128: kd * S + (tt + 1) * 128],
                    wo2[:, kd * D + c * 512: kd * D + (c + 1) * 512],
                    start=(kd == 0), stop=(kd == ND - 1)),
                    waits=(t_attn_z, t_wo_ld, gate) + t_zrp,
                    inc="tensor" if kd == ND - 1 else None)
            t_ev = sc.add("scalar", lambda s2, pp=pp, i=i, c=c: s2.activation(
                out=wos[i][:, c * 512:(c + 1) * 512], in_=pp, func=ACT.Copy),
                waits=(tmm, wos_free[i]), inc="scalar")
            qkv_hist.append(t_ev)
        t_wo_sp = sc.dma("sync", f"wos{i}", attn_d[tt * 128:(tt + 1) * 128, :],
                         wos[i][:], waits=(t_ev,))
        wos_free[i] = t_wo_sp
        wo_sp_fin[i] = t_wo_sp

    mem.free("kT2", "qT2", "vx0", "vx1", "nsc", "expw0", "expw1", "zT2", "zTo",
             "zsc0", "zsc1", "zsc2", "zsc3", "denbc0", "denbc1", "denbc2",
             "denbc3", "denbr0", "denbr1", "denbr2", "denbr3", "wo2", "wos0", "wos1",
             "ps0", "ps1", "pzx", "pmx")

    jw2 = (sc.now("vector"), sc.now("scalar"), sc.now("tensor"),
           sc.now("wos0") if "wos0" in sc.cnt else None,
           sc.now("wos1") if "wos1" in sc.cnt else None)
    BUFN = mem.sb("BUF", [128, NI * S], BF16)
    iscr = mem.sb("iscr", [128, I3], I32)
    smx = mem.sb("smx", [128, 96], F32)
    wnat = [mem.sb(f"wnat{i}", [128, D], F32, side="right") for i in range(2)]
    wdqo = [mem.sb(f"wdqo{i}", [128, D], BF16, side="right") for i in range(2)]
    wnatD = [mem.sb("wnatD", [128, I3], F32, side="right")]
    wdqoD = [mem.sb("wdqoD", [128, I3], BF16, side="right")]
    bmx = mem.sb("bmx", [128, 96], F32, side="right")
    rix = mem.sb("rix", [128, 96], F32, side="right")
    wtst = [mem.sb("wtst0", [128, I3], BF16, side="right")]
    ptr2 = [mem.ps(f"ptr2_{i}", [128, 128], BF16) for i in range(2)]

    wnat_free = [None, None]
    wdqo_free = [None, None]
    wnatD_free = [None]
    wdqoD_free = [None]
    wtst_free = [None]
    UG_BUFS = (wnat, wdqo, wnat_free, wdqo_free, "wnat")
    DN_BUFS = (wnatD, wdqoD, wnatD_free, wdqoD_free, "wnatD")
    ptr2_free = {}
    prev_fscr = None

    def dequant_tile(idx, src_ap, w, waits=(), bufs=None):
        wnat_, wdqo_, wnat_free_, wdqo_free_, strm = bufs
        i = idx % len(wnat_)
        nb = w // 32
        t_l = sc.dma("sync", strm + str(i), wnat_[i][:, 0:w], src_ap,
                     waits=(wnat_free_[i],) + jw2)
        w3 = wnat_[i][:, 0:w].rearrange("p (n b) -> p n b", b=32)
        t_b = sc.add("vector", lambda v, w3=w3, nb=nb: v.reduce_max(
            out=bmx[:, 0:nb], in_=w3, axis=AX, apply_absolute_value=True),
            waits=(t_l, dq_state["prev_q"]), inc="vector")
        t_s = sc.add("vector", lambda v, nb=nb: v.tensor_scalar(
            out=smx[:, 0:nb], in0=bmx[:, 0:nb], scalar1=1.0 / 7.0, scalar2=EPS12,
            op0=ALU.mult, op1=ALU.max), waits=(t_b,), inc="vector")
        t_r = sc.add("vector", lambda v, nb=nb: v.reciprocal(
            out=rix[:, 0:nb], in_=smx[:, 0:nb]), waits=(t_s,), inc="vector")
        rb = rix[:, 0:nb].unsqueeze(2).broadcast_to([128, nb, 32])
        sb_ = smx[:, 0:nb].unsqueeze(2).broadcast_to([128, nb, 32])
        t_q = sc.add("vector", lambda v, w3=w3, rb=rb, w=w: v.tensor_tensor(
            out=iscr[:, 0:w].rearrange("p (n b) -> p n b", b=32), in0=w3, in1=rb,
            op=ALU.mult), waits=(t_r, dq_state["prev_q"]), inc="vector")
        dq_state["prev_q"] = t_q
        t_w2 = sc.add("vector", lambda v, i=i, sb_=sb_, w=w, wdqo_=wdqo_: v.tensor_tensor(
            out=wdqo_[i][:, 0:w].rearrange("p (n b) -> p n b", b=32),
            in0=iscr[:, 0:w].rearrange("p (n b) -> p n b", b=32), in1=sb_,
            op=ALU.mult), waits=(t_q, wdqo_free_[i]) + tuple(waits), inc="vector")
        wnat_free_[i] = t_w2
        return i, t_w2

    dq_state = {"prev_q": None}
    t_dq_down = None

    def down_dq_job(j):
        nonlocal t_dq_down
        i, t_w2 = dequant_tile(j, wdown_d[j * 128:(j + 1) * 128, :], I3,
                               waits=(wtst_free[0],), bufs=DN_BUFS)
        tevT = None
        for k in range(NI):
            pp = ptr2[k % 2]
            tmmT = sc.add("tensor", lambda te, pp=pp, i=i, k=k: te.transpose(
                pp[:], wdqoD[i][:, k * 128:(k + 1) * 128], ident[:]),
                waits=(t_w2, t_id, ptr2_free.get(k % 2)), inc="tensor")
            tevT = sc.add("scalar", lambda s2, pp=pp, i=i, k=k: s2.activation(
                out=wtst[0][:, k * 128:(k + 1) * 128], in_=pp[:], func=ACT.Copy),
                waits=(tmmT, wtst_free[0]), inc="scalar")
            ptr2_free[k % 2] = tevT
        wdqoD_free[i] = tevT
        t_dq_down = sc.dma(
            "sync", "wtsp0",
            wdnT_d[:, j * 128:(j + 1) * 128].rearrange("(t p) c -> p t c", p=128),
            wtst[0][:].rearrange("p (t c) -> p t c", c=128), waits=(tevT,))
        wtst_free[0] = t_dq_down

    t_xq = {}
    prev_xq = None
    for dt in range(ND):
        sl = xnT[:, dt * S:(dt + 1) * S]
        ti = sc.add("vector", lambda v, sl=sl: v.tensor_scalar(
            out=iscr[:, 0:S], in0=sl, scalar1=scal[:, 1:2], scalar2=None,
            op0=ALU.mult), waits=(t_rx, t_qkv_last, prev_xq), inc="vector")
        t_xq[dt] = prev_xq = sc.add("vector", lambda v, sl=sl: v.tensor_scalar(
            out=sl, in0=iscr[:, 0:S], scalar1=scal[:, 0:1], scalar2=None,
            op0=ALU.mult), waits=(ti,), inc="vector")
    t_xq_all = t_xq[ND - 1]

    upst = [mem.sb(f"upst{i}", [128, S], BF16) for i in range(2)]
    wstU = [mem.sb(f"wstU{i}", [128, D], BF16) for i in range(2)]
    wstG = [mem.sb(f"wstG{i}", [128, D], BF16) for i in range(2)]
    pug = [mem.ps(f"pug{i}", [128, 512]) for i in range(2)]

    wst_free = {"up": [None, None], "gate": [None, None]}
    upst_free = [None, None]
    t_mx_prev = None
    deferred_mx = None
    pug_hist = []
    t_upmax = None
    t_gatemax = None
    dq_state["prev_q"] = t_xq_all
    for it in range(NI):
        i = it % 2
        wtoks = {}
        dq2 = {}
        for nmi, (nm, wbuf, srcw_d) in enumerate(
                (("up", wstU, wup_d), ("gate", wstG, wgate_d))):
            dqi, t_w2 = dequant_tile(2 * it + nmi, srcw_d[it * 128:(it + 1) * 128, :],
                                     D, bufs=UG_BUFS)
            dq2[nm] = (dqi, t_w2)
        if it % 3 == 0:
            down_dq_job(it // 3)
        for nmi, (nm, wbuf, srcw_d) in enumerate(
                (("up", wstU, wup_d), ("gate", wstG, wgate_d))):
            dqi, t_w2 = dq2[nm]
            tevT = None
            for kd in range(ND):
                pp = ptr2[kd % 2]
                tmmT = sc.add("tensor", lambda te, pp=pp, dqi=dqi, kd=kd: te.transpose(
                    pp[:], wdqo[dqi][:, kd * 128:(kd + 1) * 128], ident[:]),
                    waits=(t_w2, t_id, ptr2_free.get(kd % 2)), inc="tensor")
                tevT = sc.add("scalar", lambda s2, pp=pp, wbuf=wbuf, i=i, kd=kd: s2.activation(
                    out=wbuf[i][:, kd * 128:(kd + 1) * 128], in_=pp[:], func=ACT.Copy),
                    waits=(tmmT, wst_free[nm][i]), inc="scalar")
                ptr2_free[kd % 2] = tevT
            wdqo_free[dqi] = tevT
            wtoks[nm] = tevT
        for nm, wbuf in (("up", wstU), ("gate", wstG)):
            ev_t = None
            for c in range(4):
                pp = pug[len(pug_hist) % 2]
                gate = pug_hist[-2] if len(pug_hist) >= 2 else None
                tmm = None
                for kd in range(ND):
                    tmm = sc.add("tensor", lambda te, pp=pp, wbuf=wbuf, i=i, kd=kd, c=c: te.matmul(
                        pp[:], wbuf[i][:, kd * 128:(kd + 1) * 128],
                        xnT[:, kd * S + c * 512: kd * S + (c + 1) * 512],
                        start=(kd == 0), stop=(kd == ND - 1)),
                        waits=(wtoks[nm], t_xq_all, gate),
                        inc="tensor" if kd == ND - 1 else None)
                if nm == "up":
                    out_ap = upst[i][:, c * 512:(c + 1) * 512]
                    ev_t = sc.add("scalar", lambda s2, pp=pp, out_ap=out_ap: s2.activation(
                        out=out_ap, in_=pp[:], func=ACT.Copy),
                        waits=(tmm, upst_free[i] if c == 0 else None),
                        inc="scalar")
                else:
                    out_ap = BUFN[:, it * S + c * 512: it * S + (c + 1) * 512]
                    ev_t = sc.add("scalar", lambda s2, pp=pp, out_ap=out_ap: s2.activation(
                        out=out_ap, in_=pp[:], func=ACT.Copy), waits=(tmm,), inc="scalar")
                pug_hist.append(ev_t)
            wst_free[nm][i] = ev_t
            if deferred_mx is not None:
                (d_src, d_cslot, d_ev) = deferred_mx
                t_red = sc.add("vector", lambda v, d_src=d_src: v.reduce_max(
                    out=cmx[:, 7:8], in_=d_src, axis=AX, apply_absolute_value=True),
                    waits=(d_ev, t_mx_prev), inc="vector")
                t_mx = sc.add("vector", lambda v, d_cslot=d_cslot: v.tensor_tensor(
                    out=cmx[:, d_cslot:d_cslot + 1], in0=cmx[:, d_cslot:d_cslot + 1],
                    in1=cmx[:, 7:8], op=ALU.max), waits=(t_red, t_cmx0), inc="vector")
                t_mx_prev = t_mx
                if d_cslot == 1:
                    t_upmax = t_mx
                else:
                    t_gatemax = t_mx
            src2 = upst[i][:] if nm == "up" else BUFN[:, it * S:(it + 1) * S]
            deferred_mx = (src2, 1 if nm == "up" else 2, ev_t)
            if nm == "up":
                upst_free[i] = sc.dma("sync", f"upst{i}", up_d[it * 128:(it + 1) * 128, :],
                                      upst[i][:], waits=(ev_t,))
    (d_src, d_cslot, d_ev) = deferred_mx
    t_red = sc.add("vector", lambda v, d_src=d_src: v.reduce_max(
        out=cmx[:, 7:8], in_=d_src, axis=AX, apply_absolute_value=True),
        waits=(d_ev, t_mx_prev), inc="vector")
    t_gatemax = sc.add("vector", lambda v, d_cslot=d_cslot: v.tensor_tensor(
        out=cmx[:, d_cslot:d_cslot + 1], in0=cmx[:, d_cslot:d_cslot + 1],
        in1=cmx[:, 7:8], op=ALU.max), waits=(t_red, t_cmx0), inc="vector")
    t_dq_down_all = (wtst_free[0], wtst_free[0])
    mem.free("wstU0", "wstU1", "wstG0", "wstG1")
    mem.free("wtst0", "rix", "bmx", "wdqoD", "wnatD", "wdqo0", "wdqo1",
             "wnat0", "wnat1", "xnT")

    t_u1 = tree(cmx[:, 1:2], ccs[0:1, 16:17], ALU.max, 128, (t_upmax,))
    t_g1 = tree(cmx[:, 2:3], ccs[0:1, 17:18], ALU.max, 128, (t_gatemax, t_u1))
    t_bc1 = allreduce(1, t_g1)
    t_su = sc.add("vector", lambda v: v.tensor_scalar(out=scal[:, 2:3], in0=ccb[:, 16:17],
                  scalar1=1.0 / QMAX, scalar2=EPS12, op0=ALU.mult, op1=ALU.max),
                  waits=(t_bc1,), inc="vector")
    t_ru = sc.add("vector", lambda v: v.reciprocal(out=scal[:, 3:4], in_=scal[:, 2:3]),
                  waits=(t_su,), inc="vector")
    t_sg = sc.add("vector", lambda v: v.tensor_scalar(out=scal[:, 4:5], in0=ccb[:, 17:18],
                  scalar1=1.0 / QMAX, scalar2=EPS12, op0=ALU.mult, op1=ALU.max),
                  waits=(t_bc1,), inc="vector")
    t_rg = sc.add("vector", lambda v: v.reciprocal(out=scal[:, 5:6], in_=scal[:, 4:5]),
                  waits=(t_sg,), inc="vector")

    iA = [mem.sb(f"iA{i}", [128, S], I32) for i in range(2)]
    iB = [mem.sb(f"iB{i}", [128, S], I32) for i in range(2)]
    fsg = [mem.sb(f"fsg{i}", [128, S], F32) for i in range(2)]

    iA_free = [None, None]
    iB_free = [None, None]
    fsg_free = [None, None]
    t_pmax = None
    for it in range(NI):
        i = it % 2
        buf_sl = BUFN[:, it * S:(it + 1) * S]
        tGo = sc.add("scalar", lambda s2, buf_sl=buf_sl, i=i: s2.activation(
            out=iA[i][:], in_=buf_sl, func=ACT.Copy, scale=scal[:, 5:6]),
            waits=(t_rg, iA_free[i], t_dq_down_all[0], t_dq_down_all[1]), inc="scalar")
        tsg_ = sc.add("scalar", lambda s2, i=i: s2.activation(
            out=fsg[i][:], in_=iA[i][:], func=ACT.Sigmoid, scale=scal[:, 4:5]),
            waits=(tGo, fsg_free[i]), inc="scalar")
        tSo = sc.add("scalar", lambda s2, i=i: s2.activation(
            out=iB[i][:], in_=fsg[i][:], func=ACT.Copy, scale=65536.0),
            waits=(tsg_, iB_free[i]), inc="scalar")
        fsg_free[i] = tSo
        tP = sc.add("vector", lambda v, buf_sl=buf_sl, i=i: v.tensor_tensor(
            out=buf_sl, in0=iA[i][:], in1=iB[i][:], op=ALU.mult),
            waits=(tSo,), inc="vector")
        iA_free[i] = tP
        iB_free[i] = tP
        t_red = sc.add("vector", lambda v, buf_sl=buf_sl: v.reduce_max(
            out=cmx[:, 7:8], in_=buf_sl, axis=AX, apply_absolute_value=True),
            waits=(tP,), inc="vector")
        t_pmax = sc.add("vector", lambda v: v.tensor_tensor(
            out=cmx[:, 3:4], in0=cmx[:, 3:4], in1=cmx[:, 7:8], op=ALU.max),
            waits=(t_red,), inc="vector")

    t_p1 = tree(cmx[:, 3:4], ccs[0:1, 32:33], ALU.max, 128, (t_pmax,))
    t_bc2 = allreduce(2, t_p1)
    t_a = sc.add("vector", lambda v: v.tensor_tensor(out=scal[:, 22:23],
                 in0=ccb[:, 32:33], in1=scal[:, 4:5], op=ALU.mult),
                 waits=(t_bc2,), inc="vector")
    t_b2 = sc.add("vector", lambda v: v.tensor_scalar(out=scal[:, 6:7], in0=scal[:, 22:23],
                  scalar1=1.0 / (65536.0 * QMAX), scalar2=EPS12, op0=ALU.mult,
                  op1=ALU.max), waits=(t_a,), inc="vector")
    t_c = sc.add("vector", lambda v: v.reciprocal(out=scal[:, 23:24], in_=scal[:, 6:7]),
                 waits=(t_b2,), inc="vector")
    t_d2 = sc.add("vector", lambda v: v.tensor_scalar(out=scal[:, 24:25], in0=scal[:, 4:5],
                  scalar1=1.0 / 65536.0, scalar2=None, op0=ALU.mult), inc="vector")
    t_r2 = sc.add("vector", lambda v: v.tensor_tensor(out=scal[:, 7:8], in0=scal[:, 24:25],
                  in1=scal[:, 23:24], op=ALU.mult), waits=(t_c, t_d2), inc="vector")

    t_omax = None
    for it in range(NI):
        i = it % 2
        buf_sl = BUFN[:, it * S:(it + 1) * S]
        t_ul = sc.dma("sync", f"upst{i}", upst[i][:], up_d[it * 128:(it + 1) * 128, :],
                      waits=(upst_free[i],))
        tR2 = sc.add("scalar", lambda s2, buf_sl=buf_sl, i=i: s2.activation(
            out=iA[i][:], in_=buf_sl, func=ACT.Copy, scale=scal[:, 7:8]),
            waits=(t_r2, iA_free[i]), inc="scalar")
        tU = sc.add("scalar", lambda s2, i=i: s2.activation(
            out=iB[i][:], in_=upst[i][:], func=ACT.Copy, scale=scal[:, 3:4]),
            waits=(t_ru, t_ul, iB_free[i]), inc="scalar")
        tO = sc.add("vector", lambda v, buf_sl=buf_sl, i=i: v.tensor_tensor(
            out=buf_sl, in0=iA[i][:], in1=iB[i][:], op=ALU.mult),
            waits=(tR2, tU), inc="vector")
        iA_free[i] = tO
        iB_free[i] = tO
        upst_free[i] = tO
        t_red = sc.add("vector", lambda v, buf_sl=buf_sl: v.reduce_max(
            out=cmx[:, 7:8], in_=buf_sl, axis=AX, apply_absolute_value=True),
            waits=(tO,), inc="vector")
        t_omax = sc.add("vector", lambda v: v.tensor_tensor(
            out=cmx[:, 4:5], in0=cmx[:, 4:5], in1=cmx[:, 7:8], op=ALU.max),
            waits=(t_red,), inc="vector")

    t_o1t = tree(cmx[:, 4:5], ccs[0:1, 48:49], ALU.max, 128, (t_omax,))
    t_bc3 = allreduce(3, t_o1t)
    t_e = sc.add("vector", lambda v: v.tensor_tensor(out=scal[:, 25:26], in0=scal[:, 6:7],
                 in1=scal[:, 2:3], op=ALU.mult), waits=(t_b2, t_su), inc="vector")
    t_f = sc.add("vector", lambda v: v.tensor_tensor(out=scal[:, 22:23], in0=ccb[:, 48:49],
                 in1=scal[:, 25:26], op=ALU.mult), waits=(t_bc3, t_e), inc="vector")
    t_g = sc.add("vector", lambda v: v.tensor_scalar(out=scal[:, 26:27], in0=scal[:, 22:23],
                 scalar1=1.0 / QMAX, scalar2=EPS12, op0=ALU.mult, op1=ALU.max),
                 waits=(t_f,), inc="vector")
    t_h = sc.add("vector", lambda v: v.reciprocal(out=scal[:, 27:28], in_=scal[:, 26:27]),
                 waits=(t_g,), inc="vector")
    t_r3 = sc.add("vector", lambda v: v.tensor_tensor(out=scal[:, 28:29], in0=scal[:, 25:26],
                  in1=scal[:, 27:28], op=ALU.mult), waits=(t_h,), inc="vector")

    t_oq = {}
    for it in range(NI):
        i = it % 2
        buf_sl = BUFN[:, it * S:(it + 1) * S]
        tq1 = sc.add("scalar", lambda s2, buf_sl=buf_sl, i=i: s2.activation(
            out=iA[i][:], in_=buf_sl, func=ACT.Copy, scale=scal[:, 28:29]),
            waits=(t_r3, iA_free[i]), inc="scalar")
        t_oq[it] = sc.add("vector", lambda v, buf_sl=buf_sl, i=i: v.tensor_copy(
            out=buf_sl, in_=iA[i][:]), waits=(tq1,), inc="vector")
        iA_free[i] = t_oq[it]

    mem.free("iA0", "iA1", "iB0", "iB1", "fsg0", "fsg1", "iscr", "smx",
             "upst0", "upst1", "pug0", "pug1")

    jw6 = (sc.now("vector"), sc.now("scalar"))
    wdn = mem.sb("wdn", [128, 2 * NI * 512], BF16)
    xrl = [mem.sb(f"xrl{i}", [128, D], F32) for i in range(2)]
    arl = [mem.sb(f"arl{i}", [128, D], F32) for i in range(2)]
    ost = [mem.sb(f"ost{i}", [128, D], F32) for i in range(2)]
    pdn = [mem.ps(f"pdn{i}", [128, 512]) for i in range(2)]

    t_wdn = None
    for dc in range(2):
        for it in range(NI):
            t_wdn = sc.dma(
                "sync", "wdn",
                wdn[:, dc * NI * 512 + it * 512: dc * NI * 512 + (it + 1) * 512],
                wdnT_d[it * 128:(it + 1) * 128, dc * 512:(dc + 1) * 512],
                waits=t_dq_down_all + jw6)

    xrl_free = [None, None]
    arl_free = [None, None]
    ost_free = [None, None]
    pdn_hist = []
    out_fin = [None, None]
    for tt in range(NT):
        i = tt % 2
        t_xl = sc.dma("sync", f"xrl{i}", xrl[i][:], x_d[tt * 128:(tt + 1) * 128, :],
                      waits=(xrl_free[i],) + jw6)
        t_al = sc.dma("sync", f"arl{i}", arl[i][:], attn_d[tt * 128:(tt + 1) * 128, :],
                      waits=(arl_free[i], wo_sp_fin[0], wo_sp_fin[1]) + jw6)
        for dc in range(2):
            pp = pdn[len(pdn_hist) % 2]
            gate = pdn_hist[-2] if len(pdn_hist) >= 2 else None
            tmm = None
            for it in range(NI):
                tmm = sc.add("tensor", lambda te, pp=pp, it=it, tt=tt, dc=dc: te.matmul(
                    pp[:], BUFN[:, it * S + tt * 128: it * S + (tt + 1) * 128],
                    wdn[:, dc * NI * 512 + it * 512: dc * NI * 512 + (it + 1) * 512],
                    start=(it == 0), stop=(it == NI - 1)),
                    waits=(t_oq[it], t_wdn, gate),
                    inc="tensor" if it == NI - 1 else None)
            t_ev = sc.add("scalar", lambda s2, pp=pp, i=i, dc=dc: s2.activation(
                out=ost[i][:, dc * 512:(dc + 1) * 512], in_=pp[:], func=ACT.Copy,
                scale=scal[:, 26:27]),
                waits=(tmm, t_g, ost_free[i] if dc == 0 else None), inc="scalar")
            pdn_hist.append(t_ev)
            t_a1 = sc.add("vector", lambda v, i=i, dc=dc: v.tensor_tensor(
                out=ost[i][:, dc * 512:(dc + 1) * 512],
                in0=ost[i][:, dc * 512:(dc + 1) * 512],
                in1=xrl[i][:, dc * 512:(dc + 1) * 512], op=ALU.add),
                waits=(t_ev, t_xl), inc="vector")
            t_a2 = sc.add("vector", lambda v, i=i, dc=dc: v.tensor_tensor(
                out=ost[i][:, dc * 512:(dc + 1) * 512],
                in0=ost[i][:, dc * 512:(dc + 1) * 512],
                in1=arl[i][:, dc * 512:(dc + 1) * 512], op=ALU.add),
                waits=(t_a1, t_al), inc="vector")
        t_st = sc.dma("sync", f"ost{i}", out_d[tt * 128:(tt + 1) * 128, :], ost[i][:],
                      waits=(t_a2,))
        xrl_free[i] = t_st
        arl_free[i] = t_st
        ost_free[i] = t_st
        out_fin[i] = t_st

    sc.add("gpsimd", fn=None, waits=(out_fin[0], out_fin[1]))

    with nc.Block() as block:
        sc.emit(block)
    es.close()
    return nc


_NC_CACHE = [None]


def kernel(L0_attn_x, L0_attn_W_K, L0_attn_W_Q, L0_attn_W_V, L0_attn_W_O,
           L0_ffn_w_gate, L0_ffn_w_up, L0_ffn_w_down):
    f = np.float32
    x = np.ascontiguousarray(np.asarray(L0_attn_x, f))
    wq = np.ascontiguousarray(np.asarray(L0_attn_W_Q, f).transpose(2, 0, 1).reshape(D, D))
    wk = np.ascontiguousarray(np.asarray(L0_attn_W_K, f).transpose(2, 0, 1).reshape(D, D))
    wv = np.ascontiguousarray(np.asarray(L0_attn_W_V, f).transpose(2, 0, 1).reshape(D, D))
    wo = np.ascontiguousarray(np.asarray(L0_attn_W_O, f).T)
    wupn = np.ascontiguousarray(np.asarray(L0_ffn_w_up, f))
    wgn = np.ascontiguousarray(np.asarray(L0_ffn_w_gate, f))
    wdn_ = np.ascontiguousarray(np.asarray(L0_ffn_w_down, f))
    ident = np.eye(128, dtype=BF)
    identf = np.eye(128, dtype=f)
    kk = np.arange(128)[:, None]
    dmsk = np.concatenate(
        [((d * 128 + kk) <= np.arange(512)[None, :]) for d in range(4)],
        axis=1).astype(BF)
    ones1 = np.ones((1, 128), dtype=BF)
    ones1f = np.ones((1, 128), dtype=f)
    onesc = np.ones((128, 1), dtype=BF)

    if _NC_CACHE[0] is None:
        _NC_CACHE[0] = build_nc()
    nc = _NC_CACHE[0]
    in_maps = []
    for c in range(NCORES):
        in_maps.append({"x": np.ascontiguousarray(x[c]), "wqT": wq, "wkT": wk,
                        "wvT": wv, "woT": wo, "wup": wupn, "wgate": wgn,
                        "wdown": wdn_, "ident": ident, "dmask": dmsk,
                        "ones1": ones1, "ones1f": ones1f, "onesc": onesc, "identf": identf})
    res = run_bass_kernel_spmd(nc, in_maps, core_ids=list(range(NCORES)))
    return np.stack([res.results[c]["out"] for c in range(NCORES)], axis=0)


# version 13
# speedup vs baseline: 1.0102x; 1.0102x over previous
import numpy as np
import ml_dtypes
import concourse.bass as bass
import concourse.mybir as mybir
from concourse.bass_utils import run_bass_kernel_spmd

F32 = mybir.dt.float32
BF16 = mybir.dt.bfloat16
I32 = mybir.dt.int32
AX = mybir.AxisListType.X
ALU = mybir.AluOpType
ACT = mybir.ActivationFunctionType
BF = ml_dtypes.bfloat16

S, D, H, DH, I3 = 2048, 1024, 16, 64, 3072
NT, ND, NI = S // 128, D // 128, I3 // 128
NCORES = 8
QMAX = 32767.0
EPS12 = 1e-12

ENGS = ("tensor", "vector", "scalar", "gpsimd", "sync")


class Sched:

    def __init__(self, nc, es):
        self.nc = nc
        self.es = es
        self.steps = {e: [] for e in ENGS}
        names = list(ENGS) + ["cc"]
        self.sems = {n: es.enter_context(nc.semaphore("sem_" + n)) for n in names}
        self.cnt = {n: 0 for n in names}
        self.wm = {}

    def now(self, name):
        return (name, self.cnt[name])

    def stream(self, name):
        if name not in self.sems:
            self.sems[name] = self.es.enter_context(self.nc.semaphore("sem_" + name))
            self.cnt[name] = 0
        return name

    def add(self, eng, fn=None, waits=(), inc=None, n=1):
        todo = []
        for t in waits:
            if t is None:
                continue
            s, v = t
            key = (eng, s)
            if self.wm.get(key, 0) < v:
                self.wm[key] = v
                todo.append((self.sems[s], v))
        tok, sem = None, None
        if inc is not None:
            self.cnt[inc] += n
            tok = (inc, self.cnt[inc])
            sem = self.sems[inc]
        amt = n

        def thunk(obj, todo=todo, fn=fn, sem=sem, amt=amt):
            for (sm, v) in todo:
                obj.wait_ge(sm, v)
            ins = fn(obj) if fn is not None else None
            if sem is not None:
                ins.then_inc(sem, amt)

        self.steps[eng].append(thunk)
        return tok

    def dma(self, qeng, stream, out, in_, waits=()):
        return self.add(qeng, lambda o, out=out, in_=in_: o.dma_start(out=out, in_=in_),
                        waits=waits, inc=self.stream(stream), n=16)

    def dmat(self, stream, out, in_, waits=()):
        return self.add("sync",
                        lambda o, out=out, in_=in_: o.dma_start(out=out, in_=in_, transpose=True),
                        waits=waits, inc=self.stream(stream), n=16)

    def emit(self, block):
        for e in ENGS:
            thunks = self.steps[e]

            def body(obj, thunks=thunks):
                for t in thunks:
                    t(obj)
            getattr(block, e)(body)


class Mem:
    def __init__(self, nc):
        self.nc = nc
        self.guards = {}
        self.order = {}
        self.n = 0

    def _reg(self, name, g):
        t = g.__enter__()
        self.guards[name] = g
        self.order[name] = self.n
        self.n += 1
        return t

    def sb(self, name, shape, dt, side=None):
        return self._reg(name, self.nc.sbuf_tensor(name, shape, dt, side=side))

    def ps(self, name, shape, dt=F32):
        return self._reg(name, self.nc.psum_tensor(name, shape, dt))

    def free(self, *names):
        for n in sorted(names, key=lambda n: self.order[n], reverse=True):
            self.guards.pop(n).__exit__(None, None, None)


def build_nc():
    nc = bass.Bass("TRN2", target_bir_lowering=False)

    x_d = nc.dram_tensor("x", [S, D], F32, kind="ExternalInput")
    wT_d = {n: nc.dram_tensor("w" + n + "T", [D, D], F32, kind="ExternalInput")
            for n in ("q", "k", "v", "o")}
    wup_d = nc.dram_tensor("wup", [I3, D], F32, kind="ExternalInput")
    wgate_d = nc.dram_tensor("wgate", [I3, D], F32, kind="ExternalInput")
    wdown_d = nc.dram_tensor("wdown", [D, I3], F32, kind="ExternalInput")
    ident_d = nc.dram_tensor("ident", [128, 128], BF16, kind="ExternalInput")
    identf_d = nc.dram_tensor("identf", [128, 128], F32, kind="ExternalInput")
    dmask_d = nc.dram_tensor("dmask", [128, 2048], BF16, kind="ExternalInput")
    ones1_d = nc.dram_tensor("ones1", [1, 128], BF16, kind="ExternalInput")
    ones1f_d = nc.dram_tensor("ones1f", [1, 128], F32, kind="ExternalInput")
    onesc_d = nc.dram_tensor("onesc", [128, 1], BF16, kind="ExternalInput")
    out_d = nc.dram_tensor("out", [S, D], F32, kind="ExternalOutput")
    cc_in = [nc.dram_tensor(f"cc_in{r}", [1, 16], F32) for r in range(4)]
    cc_out = [nc.dram_tensor(f"cc_out{r}", [1, 16], F32) for r in range(4)]
    wdnT_d = nc.dram_tensor("wdnT", [I3, D], BF16)
    up_d = nc.dram_tensor("upsp", [I3, S], BF16)
    attn_d = nc.dram_tensor("attnsp", [S, D], F32)
    den_d = [nc.dram_tensor(f"densp{i}", [1, 512], F32) for i in range(4)]

    from contextlib import ExitStack
    es = ExitStack()
    sc = Sched(nc, es)
    mem = Mem(nc)

    ident = mem.sb("s_ident", [128, 128], BF16)
    identf = mem.sb("s_identf", [128, 128], F32)
    colws = mem.sb("s_colws", [128, 128], F32)
    ptre = mem.ps("s_ptre", [128, 512], F32)
    dmask = mem.sb("s_dmask", [128, 2048], BF16)
    ones1 = mem.sb("s_ones1", [1, 128], BF16)
    ones1f = mem.sb("s_ones1f", [1, 128], F32)
    onesc = mem.sb("s_onesc", [128, 1], BF16)
    trs = mem.sb("s_trs", [32, 32], F32)
    trs2 = mem.sb("s_trs2", [32, 32], F32)
    scal = mem.sb("scal", [128, 64], F32)
    ccs = mem.sb("ccs", [1, 64], F32)
    ccb = mem.sb("ccb", [128, 64], F32)
    cmx = mem.sb("cmx", [128, 8], F32)

    sc.dma("sync", "const", ident[:], ident_d[:, :])
    sc.dma("sync", "const", identf[:], identf_d[:, :])
    sc.dma("sync", "const", dmask[:], dmask_d[:, :])
    sc.dma("sync", "const", ones1[:], ones1_d[:, :])
    sc.dma("sync", "const", ones1f[:], ones1f_d[:, :])
    t_const = sc.dma("sync", "const", onesc[:], onesc_d[:, :])
    t_id = t_dm = t_o1 = t_o1f = t_const
    t_cmx0 = sc.add("vector", lambda v: v.memset(cmx[:], 0.0), inc="vector")
    t_trs0 = sc.add("vector", lambda v: v.memset(trs[:], 0.0), inc="vector")
    t_ccs0 = sc.add("vector", lambda v: v.memset(ccs[:], 0.0), inc="vector")
    t_cw0 = sc.add("vector", lambda v: v.memset(colws[:], 0.0), inc="vector")

    tree_busy = [None]

    def tree(col_ap, out_ap, op, p0, waits):
        t1 = sc.add("vector", lambda v, col_ap=col_ap, p0=p0: v.tensor_copy(
            out=colws[0:p0, 0:1], in_=col_ap),
            waits=tuple(waits) + (tree_busy[0], t_cw0), inc="vector")
        t2 = sc.add("tensor", lambda te: te.transpose(ptre[:, 0:128], colws[:], identf[:]),
                    waits=(t1, t_id, tree_busy[0]), inc="tensor")
        tok = sc.add("vector", lambda v, out_ap=out_ap, op=op, p0=p0: v.tensor_reduce(
            out=out_ap, in_=ptre[0:1, 0:p0], axis=AX, op=op), waits=(t2,), inc="vector")
        tree_busy[0] = tok
        return tok

    def allreduce(r, t_staged):
        t_w = sc.dma("gpsimd", "gcc", cc_in[r][:, :], ccs[0:1, 16 * r:16 * r + 16],
                     waits=(t_staged, t_ccs0))
        t_ar = sc.add("gpsimd", lambda g, r=r: g.collective_compute(
            "AllReduce", ALU.max, replica_groups=[list(range(NCORES))],
            ins=[cc_in[r].ap().opt()], outs=[cc_out[r].ap().opt()]),
            waits=(t_w,), inc="cc")
        return sc.dma("sync", "ccb", ccb[:, 16 * r:16 * r + 16],
                      cc_out[r].ap().partition_broadcast(128), waits=(t_ar,))

    xnT = mem.sb("xnT", [128, ND * S], BF16, side="right")
    xst = [mem.sb(f"xst{i}", [128, D], F32) for i in range(2)]
    xnst = [mem.sb(f"xnst{i}", [128, D], BF16) for i in range(2)]
    lns = mem.sb("lns", [128, 64], F32)
    sqd = mem.sb("sqd", [128, D], F32)
    ptr = [mem.ps(f"ptr{i}", [128, 128], BF16) for i in range(2)]

    xst_free = [None, None]
    xnst_free = [None, None]
    ptr_free = {}
    t_xn_done = None
    t_xnmax = None
    t_sums = {}
    for tt in range(NT):
        i = tt % 2
        t_ld = sc.dma("sync", f"xst{i}", xst[i][:], x_d[tt * 128:(tt + 1) * 128, :],
                      waits=(xst_free[i],))
        x_ap = xst[i][:]
        t1 = sc.add("vector", lambda v, x_ap=x_ap, tt=tt: v.reduce_sum(
            out=lns[:, tt:tt + 1], in_=x_ap, axis=AX), waits=(t_ld,), inc="vector")
        t2 = sc.add("vector", lambda v, x_ap=x_ap, tt=tt: v.scalar_tensor_tensor(
            out=sqd[:], in0=x_ap, scalar=1.0, in1=x_ap, op0=ALU.mult, op1=ALU.mult,
            accum_out=lns[:, 16 + tt:17 + tt]), waits=(t1,), inc="vector")
        t_sums[tt] = t2
        xst_free[tt % 2] = t2
    tb = sc.add("vector", lambda v: v.tensor_scalar(
        out=lns[:, 0:16], in0=lns[:, 0:16], scalar1=1.0 / D, scalar2=None,
        op0=ALU.mult), waits=(t_sums[NT - 1],), inc="vector")
    tb = sc.add("vector", lambda v: v.tensor_scalar(
        out=lns[:, 16:32], in0=lns[:, 16:32], scalar1=1.0 / D, scalar2=None,
        op0=ALU.mult), waits=(tb,), inc="vector")
    tb = sc.add("vector", lambda v: v.scalar_tensor_tensor(
        out=lns[:, 32:48], in0=lns[:, 0:16], scalar=1.0, in1=lns[:, 0:16],
        op0=ALU.mult, op1=ALU.mult), waits=(tb,), inc="vector")
    tb = sc.add("vector", lambda v: v.tensor_tensor(
        out=lns[:, 32:48], in0=lns[:, 16:32], in1=lns[:, 32:48],
        op=ALU.subtract), waits=(tb,), inc="vector")
    tb = sc.add("vector", lambda v: v.tensor_scalar(
        out=lns[:, 32:48], in0=lns[:, 32:48], scalar1=1e-5, scalar2=None,
        op0=ALU.add), waits=(tb,), inc="vector")
    tb = sc.add("scalar", lambda s: s.activation(
        out=lns[:, 48:64], in_=lns[:, 32:48], func=ACT.Sqrt), waits=(tb,), inc="scalar")
    tb = sc.add("vector", lambda v: v.reciprocal(
        out=lns[:, 32:48], in_=lns[:, 48:64]), waits=(tb,), inc="vector")
    t_stats = sc.add("vector", lambda v: v.scalar_tensor_tensor(
        out=lns[:, 48:64], in0=lns[:, 0:16], scalar=-1.0, in1=lns[:, 32:48],
        op0=ALU.mult, op1=ALU.mult), waits=(tb,), inc="vector")
    for tt in range(NT):
        i = tt % 2
        t_ld = sc.dma("sync", f"xst{i}", xst[i][:], x_d[tt * 128:(tt + 1) * 128, :],
                      waits=(xst_free[i],))
        x_ap = xst[i][:]
        n_ap = xnst[i][:]
        t7 = sc.add("vector", lambda v, x_ap=x_ap, n_ap=n_ap, tt=tt: v.tensor_scalar(
            out=n_ap, in0=x_ap, scalar1=lns[:, 32 + tt:33 + tt],
            scalar2=lns[:, 48 + tt:49 + tt], op0=ALU.mult, op1=ALU.add),
            waits=(t_ld, t_stats, xnst_free[i]), inc="vector")
        xst_free[i] = t7
        t8 = sc.add("vector", lambda v, n_ap=n_ap: v.reduce_max(
            out=lns[:, 0:1], in_=n_ap, axis=AX, apply_absolute_value=True),
            waits=(t7, t_cmx0), inc="vector")
        t_xnmax = sc.add("vector", lambda v: v.tensor_tensor(
            out=cmx[:, 0:1], in0=cmx[:, 0:1], in1=lns[:, 0:1], op=ALU.max),
            waits=(t8,), inc="vector")
        last_tr = None
        for dt in range(ND):
            pp = ptr[dt % 2]
            tmm = sc.add("tensor", lambda te, n_ap=n_ap, dt=dt, pp=pp: te.transpose(
                pp[:], n_ap[:, dt * 128:(dt + 1) * 128], ident[:]),
                waits=(t7, t_id, ptr_free.get(dt % 2)), inc="tensor")
            tev = sc.add("vector", lambda v, pp=pp, dt=dt, tt=tt: v.tensor_copy(
                out=xnT[:, dt * S + tt * 128: dt * S + (tt + 1) * 128], in_=pp[:]),
                waits=(tmm,), inc="vector")
            ptr_free[dt % 2] = tev
            last_tr = tmm
        xnst_free[i] = last_tr
        t_xn_done = tev
    mem.free("xst0", "xst1", "xnst0", "xnst1", "lns", "sqd", "ptr0", "ptr1")

    t_st0 = tree(cmx[:, 0:1], ccs[0:1, 0:1], ALU.max, 128, (t_xnmax,))
    t_bc0 = allreduce(0, t_st0)
    t_sx = sc.add("vector", lambda v: v.tensor_scalar(out=scal[:, 0:1], in0=ccb[:, 0:1],
                  scalar1=1.0 / QMAX, scalar2=EPS12, op0=ALU.mult, op1=ALU.max),
                  waits=(t_bc0,), inc="vector")
    t_rx = sc.add("vector", lambda v: v.reciprocal(out=scal[:, 1:2], in_=scal[:, 0:1]),
                  waits=(t_sx,), inc="vector")

    jw1 = (sc.now("vector"), sc.now("scalar"), sc.now("tensor"))
    wsb = {n: mem.sb("w" + n, [128, ND * D], BF16, side="right") for n in ("q", "k", "v")}
    t_wld = {}
    for n in ("q", "k", "v"):
        tk = None
        for kd in range(ND):
            tk = sc.dma("gpsimd", "wcast", wsb[n][:, kd * D:(kd + 1) * D],
                        wT_d[n][kd * 128:(kd + 1) * 128, :], waits=jw1)
        t_wld[n] = tk
    for n in ("q", "k", "v"):
        t_wld[n] = tk

    kT2 = mem.sb("kT2", [128, S], BF16)
    qT2 = mem.sb("qT2", [128, S], BF16)
    vx = [mem.sb(f"vx{i}", [128, NT * 65], BF16) for i in range(2)]
    nsc = mem.sb("nsc", [128, S], BF16)
    expw = [mem.sb(f"expw{i}", [128, NT * 512], BF16) for i in range(2)]
    zT2 = mem.sb("zT2", [128, ND * S], BF16)
    zTo = mem.sb("zTo", [64, S], BF16)
    zsc = [mem.sb(f"zsc{i}", [65, 512], F32) for i in range(4)]
    denbc = [mem.sb(f"denbc{i}", [64, 512], F32) for i in range(4)]
    denbr = [mem.sb(f"denbr{i}", [64, 512], F32) for i in range(4)]
    ps_ = [mem.ps(f"ps{i}", [128, 1024]) for i in range(2)]
    pzx = mem.ps("pzx", [128, 1024])
    pzb = [pzx[0:65, 0:512], ptre[0:65, 0:512]]
    pnr = pzx[0:1, 512:1024]
    pmx = mem.ps("pmx", [128, 512])
    prb = pmx[0:64, 0:512]

    t_vones = [sc.add("vector", lambda v, i=i: v.memset(
        vx[i][:].rearrange("p (t c) -> p t c", c=65)[:, :, 64:65], 1.0), inc="vector")
        for i in range(2)]

    qkv_hist = []
    vx_free = [None, None]
    prev_exp = None
    pz_free = {0: None, 1: None}
    prb_free = None
    zn_prev = [None, None, None, None]
    t_qkv_last = None
    t_attn_last_te = None
    pmx_free = None
    hp_state = {}
    zto_free = None
    t_zrepack = {}

    def qkv_slot(idx):
        return pzx[0:128, 0:512] if idx % 2 == 0 else pmx[0:128, 0:512]

    for hp in range(H // 2):
        for wnm, dst in (("k", kT2), ("q", qT2)):
            for c in range(4):
                pp = qkv_slot(len(qkv_hist))
                gate = qkv_hist[-2] if len(qkv_hist) >= 2 else None
                tmm = None
                for kd in range(ND):
                    tmm = sc.add("tensor", lambda te, pp=pp, wnm=wnm, kd=kd, hp=hp, c=c: te.matmul(
                        pp, wsb[wnm][:, kd * D + hp * 128: kd * D + hp * 128 + 128],
                        xnT[:, kd * S + c * 512: kd * S + (c + 1) * 512],
                        start=(kd == 0), stop=(kd == ND - 1)),
                        waits=(t_xn_done, t_wld[wnm], gate,
                               pz_free[0] if c == 0 and wnm == "k" else None,
                               pz_free[1] if c == 0 and wnm == "k" else None,
                               pmx_free if c == 0 and wnm == "k" else None,
                               hp_state.get("score_last") if c == 0 and wnm == "k" else None),
                        inc="tensor" if kd == ND - 1 else None)
                tev = sc.add("vector", lambda v, pp=pp, c=c, dst=dst: v.tensor_copy(
                    out=dst[:, c * 512:(c + 1) * 512], in_=pp),
                    waits=(tmm,), inc="vector")
                qkv_hist.append(tev)
        for tt in range(NT):
            pp = qkv_slot(len(qkv_hist))
            gate = qkv_hist[-2] if len(qkv_hist) >= 2 else None
            tmm = None
            for kd in range(ND):
                tmm = sc.add("tensor", lambda te, pp=pp, kd=kd, tt=tt, hp=hp: te.matmul(
                    pp[:, 0:128], xnT[:, kd * S + tt * 128: kd * S + (tt + 1) * 128],
                    wsb["v"][:, kd * D + hp * 128: kd * D + hp * 128 + 128],
                    start=(kd == 0), stop=(kd == ND - 1)),
                    waits=(t_xn_done, t_wld["v"], gate,
                           vx_free[0] if tt == 0 else None,
                           vx_free[1] if tt == 0 else None),
                    inc="tensor" if kd == ND - 1 else None)
            tev = None
            for half in range(2):
                tev = sc.add("vector", lambda v, pp=pp, half=half, tt=tt: v.tensor_copy(
                    out=vx[half][:, tt * 65: tt * 65 + 64],
                    in_=pp[:, 64 * half:64 * half + 64]),
                    waits=(tmm, t_vones[half]), inc="vector")
            qkv_hist.append(tev)
        t_qkv_last = tmm

        bias_tok = {}
        pmx_busy = hp_state.get("pmx_busy")
        nsc_reader = hp_state.get("nsc_busy")
        nrm_prev = hp_state.get("nrm_prev")
        nrm_tok = {}
        for j, wch in enumerate(("q", "k")):
            srcT = qT2 if wch == "q" else kT2
            tsq = sc.add("vector", lambda v, srcT=srcT: v.tensor_tensor(
                out=nsc[:], in0=srcT[:], in1=srcT[:], op=ALU.mult),
                waits=(qkv_hist[-1], prev_exp, nsc_reader), inc="vector")
            for half in range(2):
                sl = 8 + 2 * half + j
                tred = None
                for c in range(4):
                    tm = sc.add("tensor", lambda te, c=c, half=half: te.matmul(
                        pnr, onesc[64 * half:64 * half + 64, :],
                        nsc[64 * half:64 * half + 64, c * 512:(c + 1) * 512],
                        start=True, stop=True),
                        waits=(tsq, t_o1f, tred, pmx_busy, hp_state.get("pnr_busy")),
                        inc="tensor")
                    tred = sc.add("vector", lambda v, c=c: v.reduce_max(
                        out=scal[0:1, 30 + c:31 + c], in_=pnr, axis=AX),
                        waits=(tm, nrm_prev), inc="vector")
                nsc_reader = tm
                hp_state["pnr_busy"] = tred
                tok = sc.add("vector", lambda v: v.tensor_tensor(
                    out=scal[0:1, 30:31], in0=scal[0:1, 30:31], in1=scal[0:1, 31:32],
                    op=ALU.max), waits=(tred,), inc="vector")
                tok = sc.add("vector", lambda v: v.tensor_tensor(
                    out=scal[0:1, 32:33], in0=scal[0:1, 32:33], in1=scal[0:1, 33:34],
                    op=ALU.max), waits=(tok,), inc="vector")
                nrm_tok[(wch, half)] = nrm_prev = sc.add(
                    "vector", lambda v, sl=sl: v.tensor_tensor(
                        out=scal[0:1, sl:sl + 1], in0=scal[0:1, 30:31],
                        in1=scal[0:1, 32:33], op=ALU.max), waits=(tok,), inc="vector")
        hp_state["nsc_busy"] = nsc_reader
        hp_state["nrm_prev"] = nrm_prev
        for half in range(2):
            toks = []
            for j, wch in enumerate(("q", "k")):
                sl = 8 + 2 * half + j
                tok = sc.add("tensor", lambda te, sl=sl, half=half, j=j: te.matmul(
                    pmx[:, 2 + 2 * half + j:3 + 2 * half + j], ones1f[:],
                    scal[0:1, sl:sl + 1], start=True, stop=True),
                    waits=(nrm_tok[(wch, half)], t_o1f, prb_free, pmx_busy,
                           qkv_hist[-1]), inc="tensor")
                toks.append(tok)
            tok = sc.add("vector", lambda v, half=half: v.tensor_copy(
                out=scal[:, 34:35], in_=pmx[:, 2 + 2 * half:3 + 2 * half]),
                waits=tuple(toks), inc="vector")
            tok = sc.add("vector", lambda v, half=half: v.tensor_tensor(
                out=scal[:, 16 + half:17 + half], in0=scal[:, 34:35],
                in1=pmx[:, 3 + 2 * half:4 + 2 * half], op=ALU.mult),
                waits=(tok,), inc="vector")
            pmx_busy = tok
            pmx_free = tok
            tok = sc.add("scalar", lambda s2, half=half: s2.activation(
                out=scal[:, 18 + half:19 + half], in_=scal[:, 16 + half:17 + half],
                func=ACT.Sqrt), waits=(tok,), inc="scalar")
            bias_tok[half] = sc.add("vector", lambda v, half=half: v.tensor_scalar(
                out=scal[:, 20 + half:21 + half], in0=scal[:, 18 + half:19 + half],
                scalar1=-0.125, scalar2=None, op0=ALU.mult), waits=(tok,), inc="vector")
        hp_state["pmx_busy"] = pmx_busy

        pending = None

        def av_burst(st):
            nonlocal t_attn_last_te
            p_hp, p_qb, p_half, p_nkt, p_gates = st[:5]
            b = p_half
            tav = None
            for kt in range(p_nkt):
                tav = sc.add("tensor", lambda te, kt=kt, p_half=p_half, p_nkt=p_nkt, b=b: te.matmul(
                    pzb[b], vx[p_half][:, kt * 65:(kt + 1) * 65],
                    expw[p_half][:, kt * 512:(kt + 1) * 512],
                    start=(kt == 0), stop=(kt == p_nkt - 1)),
                    waits=(p_gates[kt], pz_free[b] if kt == 0 else None),
                    inc="tensor" if kt == p_nkt - 1 else None)
            t_attn_last_te = tav
            hp_state["av_last_%d" % p_half] = tav
            st[6] = tav
            return tav

        def finish_av(st):
            nonlocal prb_free, zto_free
            p_hp, p_qb, p_half, p_nkt, p_gates, pos, tav, _ = st
            ui = hp_state.get("ui", 0)
            hp_state["ui"] = ui + 1
            w = ui % 4
            b = p_half
            t_ev = sc.add("vector", lambda v, w=w, b=b: v.tensor_copy(
                out=zsc[w][:], in_=pzb[b]), waits=(tav, zn_prev[w]), inc="vector")
            pz_free[b] = t_ev
            t_dw = sc.dma("gpsimd", "gdwr%d" % w, den_d[w][:, :], zsc[w][64:65, :],
                          waits=(t_ev,))
            t_db = sc.dma("gpsimd", "gdbc%d" % w, denbc[w][:],
                          den_d[w].ap().partition_broadcast(64), waits=(t_dw,))
            t_rc = sc.add("vector", lambda v, w=w: v.reciprocal(
                out=denbr[w][:], in_=denbc[w][:]), waits=(t_db, zn_prev[w]),
                inc="vector")
            if p_half == 0:
                out_ap = zT2[0:64, p_hp * S + p_qb * 512: p_hp * S + (p_qb + 1) * 512]
                t_zn = sc.add("vector", lambda v, out_ap=out_ap, w=w: v.tensor_tensor(
                    out=out_ap, in0=zsc[w][0:64, :], in1=denbr[w][:], op=ALU.mult),
                    waits=(t_rc,), inc="vector")
            else:
                out_ap = zTo[:, p_qb * 512:(p_qb + 1) * 512]
                t_zn = sc.add("vector", lambda v, out_ap=out_ap, w=w: v.tensor_tensor(
                    out=out_ap, in0=zsc[w][0:64, :], in1=denbr[w][:], op=ALU.mult),
                    waits=(t_rc, zto_free), inc="vector")
                if p_qb == 3:
                    zto_free = sc.dma("gpsimd", "gzto",
                                      zT2[64:128, p_hp * S:(p_hp + 1) * S], zTo[:],
                                      waits=(t_zn,))
                    t_zrepack[p_hp] = zto_free
            zn_prev[w] = t_zn
            prb_free = t_zn
            return t_zn

        exp_prev2 = [hp_state.get("exp_prev_0"), hp_state.get("exp_prev_1")]
        for qb in range(4):
            nkt = 4 * qb + 4
            burst_toks = [None, None]
            if pending is not None:
                burst_toks[0] = av_burst(pending[0])
                burst_toks[1] = av_burst(pending[1])
            exp_toks = {0: {}, 1: {}}
            gates2 = {0: {}, 1: {}}
            tmm_last = {}
            for g in range(nkt // 2):
                for half in range(2):
                    p0, p1_ = 64 * half, 64 * half + 64
                    tmm2 = None
                    for kt in (2 * g, 2 * g + 1):
                        slot = kt % 2
                        tmm2 = sc.add("tensor",
                            lambda te, kt=kt, qb=qb, p0=p0, p1_=p1_, half=half, slot=slot: te.matmul(
                                ps_[half][:, slot * 512:(slot + 1) * 512],
                                kT2[p0:p1_, kt * 128:(kt + 1) * 128],
                                qT2[p0:p1_, qb * 512:(qb + 1) * 512],
                                start=True, stop=True),
                            waits=(exp_toks[half].get(g - 1), qkv_hist[-1],
                                   exp_prev2[half] if g == 0 else None),
                            inc="tensor" if kt % 2 == 1 else None)
                    tmm_last[half] = tmm2
                for half in range(2):
                    bias_ap = scal[:, 20 + half:21 + half]
                    texp = sc.add("scalar",
                        lambda s2, g=g, half=half, bias_ap=bias_ap: s2.activation(
                            out=expw[half][:, g * 1024:(g + 1) * 1024],
                            in_=ps_[half][:, 0:1024], func=ACT.Exp,
                            scale=0.125, bias=bias_ap),
                        waits=(tmm_last[half], bias_tok[half], burst_toks[half]),
                        inc="scalar")
                    exp_toks[half][g] = texp
                    for kt in (2 * g, 2 * g + 1):
                        d = kt - 4 * qb
                        if d >= 0:
                            gates2[half][kt] = sc.add("gpsimd",
                                lambda g2, half=half, kt=kt, d=d: g2.tensor_tensor(
                                    out=expw[half][:, kt * 512:(kt + 1) * 512],
                                    in0=expw[half][:, kt * 512:(kt + 1) * 512],
                                    in1=dmask[:, d * 512:(d + 1) * 512], op=ALU.mult),
                                waits=(texp, t_dm), inc="gpsimd")
                        else:
                            gates2[half][kt] = texp
            prev_exp = exp_toks[1][nkt // 2 - 1]
            exp_prev2 = [exp_toks[0][nkt // 2 - 1], exp_toks[1][nkt // 2 - 1]]
            hp_state["score_last"] = tmm_last[1]
            if pending is not None:
                finish_av(pending[0])
                finish_av(pending[1])
            pending = [[hp, qb, 0, nkt, gates2[0], 0, None, None],
                       [hp, qb, 1, nkt, gates2[1], 0, None, None]]
        av_burst(pending[0])
        av_burst(pending[1])
        finish_av(pending[0])
        t_zn_last = finish_av(pending[1])
        hp_state["exp_prev_0"] = exp_prev2[0]
        hp_state["exp_prev_1"] = exp_prev2[1]
        vx_free = [t_attn_last_te, t_attn_last_te]
    t_attn_z = t_zn_last

    mem.free("wq", "wk", "wv")
    wo2 = mem.sb("wo2", [128, ND * D], BF16, side="right")
    tk = None
    for kd in range(ND):
        tk = sc.dma("gpsimd", "wcast", wo2[:, kd * D:(kd + 1) * D],
                    wT_d["o"][kd * 128:(kd + 1) * 128, :], waits=(qkv_hist[-1],))
    t_wo_ld = tk
    t_zrp = tuple(t_zrepack[p] for p in range(H // 2))
    wos = [mem.sb(f"wos{i}", [128, D], F32) for i in range(2)]
    wos_free = [None, None]
    wo_sp_fin = [None, None]
    t_wo_sp = None
    for tt in range(NT):
        i = tt % 2
        t_ev = None
        for c in range(2):
            pp = qkv_slot(len(qkv_hist))
            gate = qkv_hist[-2] if len(qkv_hist) >= 2 else None
            tmm = None
            for kd in range(ND):
                tmm = sc.add("tensor", lambda te, pp=pp, kd=kd, tt=tt, c=c: te.matmul(
                    pp, zT2[:, kd * S + tt * 128: kd * S + (tt + 1) * 128],
                    wo2[:, kd * D + c * 512: kd * D + (c + 1) * 512],
                    start=(kd == 0), stop=(kd == ND - 1)),
                    waits=(t_attn_z, t_wo_ld, gate) + t_zrp,
                    inc="tensor" if kd == ND - 1 else None)
            t_ev = sc.add("scalar", lambda s2, pp=pp, i=i, c=c: s2.activation(
                out=wos[i][:, c * 512:(c + 1) * 512], in_=pp, func=ACT.Copy),
                waits=(tmm, wos_free[i]), inc="scalar")
            qkv_hist.append(t_ev)
        t_wo_sp = sc.dma("sync", f"wos{i}", attn_d[tt * 128:(tt + 1) * 128, :],
                         wos[i][:], waits=(t_ev,))
        wos_free[i] = t_wo_sp
        wo_sp_fin[i] = t_wo_sp

    mem.free("kT2", "qT2", "vx0", "vx1", "nsc", "expw0", "expw1", "zT2", "zTo",
             "zsc0", "zsc1", "zsc2", "zsc3", "denbc0", "denbc1", "denbc2",
             "denbc3", "denbr0", "denbr1", "denbr2", "denbr3", "wo2", "wos0", "wos1",
             "ps0", "ps1", "pzx", "pmx")

    jw2 = (sc.now("vector"), sc.now("scalar"), sc.now("tensor"),
           sc.now("wos0") if "wos0" in sc.cnt else None,
           sc.now("wos1") if "wos1" in sc.cnt else None)
    BUFN = mem.sb("BUF", [128, NI * S], BF16)
    iscr = mem.sb("iscr", [128, I3], I32)
    smx = mem.sb("smx", [128, 96], F32)
    wnat = [mem.sb(f"wnat{i}", [128, D], F32, side="right") for i in range(2)]
    wdqo = [mem.sb(f"wdqo{i}", [128, D], BF16, side="right") for i in range(2)]
    wnatD = [mem.sb("wnatD", [128, I3], F32, side="right")]
    wdqoD = [mem.sb("wdqoD", [128, I3], BF16, side="right")]
    bmx = mem.sb("bmx", [128, 96], F32, side="right")
    rix = mem.sb("rix", [128, 96], F32, side="right")
    wtst = [mem.sb("wtst0", [128, I3], BF16, side="right")]
    ptr2 = [mem.ps(f"ptr2_{i}", [128, 128], BF16) for i in range(2)]

    wnat_free = [None, None]
    wdqo_free = [None, None]
    wnatD_free = [None]
    wdqoD_free = [None]
    wtst_free = [None]
    UG_BUFS = (wnat, wdqo, wnat_free, wdqo_free, "wnat")
    DN_BUFS = (wnatD, wdqoD, wnatD_free, wdqoD_free, "wnatD")
    ptr2_free = {}
    prev_fscr = None

    def dequant_tile(idx, src_ap, w, waits=(), bufs=None):
        wnat_, wdqo_, wnat_free_, wdqo_free_, strm = bufs
        i = idx % len(wnat_)
        nb = w // 32
        t_l = sc.dma("sync", strm + str(i), wnat_[i][:, 0:w], src_ap,
                     waits=(wnat_free_[i],) + jw2)
        w3 = wnat_[i][:, 0:w].rearrange("p (n b) -> p n b", b=32)
        t_b = sc.add("vector", lambda v, w3=w3, nb=nb: v.reduce_max(
            out=bmx[:, 0:nb], in_=w3, axis=AX, apply_absolute_value=True),
            waits=(t_l, dq_state["prev_q"]), inc="vector")
        t_s = sc.add("vector", lambda v, nb=nb: v.tensor_scalar(
            out=smx[:, 0:nb], in0=bmx[:, 0:nb], scalar1=1.0 / 7.0, scalar2=EPS12,
            op0=ALU.mult, op1=ALU.max), waits=(t_b,), inc="vector")
        t_r = sc.add("vector", lambda v, nb=nb: v.reciprocal(
            out=rix[:, 0:nb], in_=smx[:, 0:nb]), waits=(t_s,), inc="vector")
        rb = rix[:, 0:nb].unsqueeze(2).broadcast_to([128, nb, 32])
        sb_ = smx[:, 0:nb].unsqueeze(2).broadcast_to([128, nb, 32])
        t_q = sc.add("vector", lambda v, w3=w3, rb=rb, w=w: v.tensor_tensor(
            out=iscr[:, 0:w].rearrange("p (n b) -> p n b", b=32), in0=w3, in1=rb,
            op=ALU.mult), waits=(t_r, dq_state["prev_q"]), inc="vector")
        dq_state["prev_q"] = t_q
        t_w2 = sc.add("vector", lambda v, i=i, sb_=sb_, w=w, wdqo_=wdqo_: v.tensor_tensor(
            out=wdqo_[i][:, 0:w].rearrange("p (n b) -> p n b", b=32),
            in0=iscr[:, 0:w].rearrange("p (n b) -> p n b", b=32), in1=sb_,
            op=ALU.mult), waits=(t_q, wdqo_free_[i]) + tuple(waits), inc="vector")
        wnat_free_[i] = t_w2
        return i, t_w2

    dq_state = {"prev_q": None}
    t_dq_down = None

    def down_dq_job(j):
        nonlocal t_dq_down
        i, t_w2 = dequant_tile(j, wdown_d[j * 128:(j + 1) * 128, :], I3,
                               waits=(wtst_free[0],), bufs=DN_BUFS)
        tevT = None
        for k in range(NI):
            pp = ptr2[k % 2]
            tmmT = sc.add("tensor", lambda te, pp=pp, i=i, k=k: te.transpose(
                pp[:], wdqoD[i][:, k * 128:(k + 1) * 128], ident[:]),
                waits=(t_w2, t_id, ptr2_free.get(k % 2)), inc="tensor")
            tevT = sc.add("scalar", lambda s2, pp=pp, i=i, k=k: s2.activation(
                out=wtst[0][:, k * 128:(k + 1) * 128], in_=pp[:], func=ACT.Copy),
                waits=(tmmT, wtst_free[0]), inc="scalar")
            ptr2_free[k % 2] = tevT
        wdqoD_free[i] = tevT
        t_dq_down = sc.dma(
            "sync", "wtsp0",
            wdnT_d[:, j * 128:(j + 1) * 128].rearrange("(t p) c -> p t c", p=128),
            wtst[0][:].rearrange("p (t c) -> p t c", c=128), waits=(tevT,))
        wtst_free[0] = t_dq_down

    t_xq = {}
    prev_xq = None
    for dt in range(ND):
        sl = xnT[:, dt * S:(dt + 1) * S]
        ti = sc.add("vector", lambda v, sl=sl: v.tensor_scalar(
            out=iscr[:, 0:S], in0=sl, scalar1=scal[:, 1:2], scalar2=None,
            op0=ALU.mult), waits=(t_rx, t_qkv_last, prev_xq), inc="vector")
        t_xq[dt] = prev_xq = sc.add("vector", lambda v, sl=sl: v.tensor_scalar(
            out=sl, in0=iscr[:, 0:S], scalar1=scal[:, 0:1], scalar2=None,
            op0=ALU.mult), waits=(ti,), inc="vector")
    t_xq_all = t_xq[ND - 1]

    upst = [mem.sb(f"upst{i}", [128, S], BF16) for i in range(2)]
    wstU = [mem.sb(f"wstU{i}", [128, D], BF16) for i in range(2)]
    wstG = [mem.sb(f"wstG{i}", [128, D], BF16) for i in range(2)]
    pug = [mem.ps(f"pug{i}", [128, 512]) for i in range(2)]

    wst_free = {"up": [None, None], "gate": [None, None]}
    upst_free = [None, None]
    t_mx_prev = None
    deferred_mx = None
    pug_hist = []
    t_upmax = None
    t_gatemax = None
    dq_state["prev_q"] = t_xq_all
    for it in range(NI):
        i = it % 2
        wtoks = {}
        dq2 = {}
        for nmi, (nm, wbuf, srcw_d) in enumerate(
                (("up", wstU, wup_d), ("gate", wstG, wgate_d))):
            dqi, t_w2 = dequant_tile(2 * it + nmi, srcw_d[it * 128:(it + 1) * 128, :],
                                     D, bufs=UG_BUFS)
            dq2[nm] = (dqi, t_w2)
        if it % 3 == 0:
            down_dq_job(it // 3)
        for nmi, (nm, wbuf, srcw_d) in enumerate(
                (("up", wstU, wup_d), ("gate", wstG, wgate_d))):
            dqi, t_w2 = dq2[nm]
            tevT = None
            for kd in range(ND):
                pp = ptr2[kd % 2]
                tmmT = sc.add("tensor", lambda te, pp=pp, dqi=dqi, kd=kd: te.transpose(
                    pp[:], wdqo[dqi][:, kd * 128:(kd + 1) * 128], ident[:]),
                    waits=(t_w2, t_id, ptr2_free.get(kd % 2)), inc="tensor")
                tevT = sc.add("scalar", lambda s2, pp=pp, wbuf=wbuf, i=i, kd=kd: s2.activation(
                    out=wbuf[i][:, kd * 128:(kd + 1) * 128], in_=pp[:], func=ACT.Copy),
                    waits=(tmmT, wst_free[nm][i]), inc="scalar")
                ptr2_free[kd % 2] = tevT
            wdqo_free[dqi] = tevT
            wtoks[nm] = tevT
        for nm, wbuf in (("up", wstU), ("gate", wstG)):
            ev_t = None
            for c in range(4):
                pp = pug[len(pug_hist) % 2]
                gate = pug_hist[-2] if len(pug_hist) >= 2 else None
                tmm = None
                for kd in range(ND):
                    tmm = sc.add("tensor", lambda te, pp=pp, wbuf=wbuf, i=i, kd=kd, c=c: te.matmul(
                        pp[:], wbuf[i][:, kd * 128:(kd + 1) * 128],
                        xnT[:, kd * S + c * 512: kd * S + (c + 1) * 512],
                        start=(kd == 0), stop=(kd == ND - 1)),
                        waits=(wtoks[nm], t_xq_all, gate),
                        inc="tensor" if kd == ND - 1 else None)
                if nm == "up":
                    out_ap = upst[i][:, c * 512:(c + 1) * 512]
                    ev_t = sc.add("scalar", lambda s2, pp=pp, out_ap=out_ap: s2.activation(
                        out=out_ap, in_=pp[:], func=ACT.Copy),
                        waits=(tmm, upst_free[i] if c == 0 else None),
                        inc="scalar")
                else:
                    out_ap = BUFN[:, it * S + c * 512: it * S + (c + 1) * 512]
                    ev_t = sc.add("scalar", lambda s2, pp=pp, out_ap=out_ap: s2.activation(
                        out=out_ap, in_=pp[:], func=ACT.Copy), waits=(tmm,), inc="scalar")
                pug_hist.append(ev_t)
            wst_free[nm][i] = ev_t
            if deferred_mx is not None:
                (d_src, d_cslot, d_ev) = deferred_mx
                t_red = sc.add("vector", lambda v, d_src=d_src: v.reduce_max(
                    out=cmx[:, 7:8], in_=d_src, axis=AX, apply_absolute_value=True),
                    waits=(d_ev, t_mx_prev), inc="vector")
                t_mx = sc.add("vector", lambda v, d_cslot=d_cslot: v.tensor_tensor(
                    out=cmx[:, d_cslot:d_cslot + 1], in0=cmx[:, d_cslot:d_cslot + 1],
                    in1=cmx[:, 7:8], op=ALU.max), waits=(t_red, t_cmx0), inc="vector")
                t_mx_prev = t_mx
                if d_cslot == 1:
                    t_upmax = t_mx
                else:
                    t_gatemax = t_mx
            src2 = upst[i][:] if nm == "up" else BUFN[:, it * S:(it + 1) * S]
            deferred_mx = (src2, 1 if nm == "up" else 2, ev_t)
            if nm == "up":
                upst_free[i] = sc.dma("sync", f"upst{i}", up_d[it * 128:(it + 1) * 128, :],
                                      upst[i][:], waits=(ev_t,))
    (d_src, d_cslot, d_ev) = deferred_mx
    t_red = sc.add("vector", lambda v, d_src=d_src: v.reduce_max(
        out=cmx[:, 7:8], in_=d_src, axis=AX, apply_absolute_value=True),
        waits=(d_ev, t_mx_prev), inc="vector")
    t_gatemax = sc.add("vector", lambda v, d_cslot=d_cslot: v.tensor_tensor(
        out=cmx[:, d_cslot:d_cslot + 1], in0=cmx[:, d_cslot:d_cslot + 1],
        in1=cmx[:, 7:8], op=ALU.max), waits=(t_red, t_cmx0), inc="vector")
    t_dq_down_all = (wtst_free[0], wtst_free[0])
    mem.free("wstU0", "wstU1", "wstG0", "wstG1")
    mem.free("wtst0", "rix", "bmx", "wdqoD", "wnatD", "wdqo0", "wdqo1",
             "wnat0", "wnat1", "xnT")

    t_u1 = tree(cmx[:, 1:2], ccs[0:1, 16:17], ALU.max, 128, (t_upmax,))
    t_g1 = tree(cmx[:, 2:3], ccs[0:1, 17:18], ALU.max, 128, (t_gatemax, t_u1))
    t_bc1 = allreduce(1, t_g1)
    t_su = sc.add("vector", lambda v: v.tensor_scalar(out=scal[:, 2:3], in0=ccb[:, 16:17],
                  scalar1=1.0 / QMAX, scalar2=EPS12, op0=ALU.mult, op1=ALU.max),
                  waits=(t_bc1,), inc="vector")
    t_ru = sc.add("vector", lambda v: v.reciprocal(out=scal[:, 3:4], in_=scal[:, 2:3]),
                  waits=(t_su,), inc="vector")
    t_sg = sc.add("vector", lambda v: v.tensor_scalar(out=scal[:, 4:5], in0=ccb[:, 17:18],
                  scalar1=1.0 / QMAX, scalar2=EPS12, op0=ALU.mult, op1=ALU.max),
                  waits=(t_bc1,), inc="vector")
    t_rg = sc.add("vector", lambda v: v.reciprocal(out=scal[:, 5:6], in_=scal[:, 4:5]),
                  waits=(t_sg,), inc="vector")

    iA = [mem.sb(f"iA{i}", [128, S], I32) for i in range(2)]
    iB = [mem.sb(f"iB{i}", [128, S], I32) for i in range(2)]
    fsg = [mem.sb(f"fsg{i}", [128, S], F32) for i in range(2)]

    iA_free = [None, None]
    iB_free = [None, None]
    fsg_free = [None, None]
    t_pmax = None
    for it in range(NI):
        i = it % 2
        buf_sl = BUFN[:, it * S:(it + 1) * S]
        tGo = sc.add("scalar", lambda s2, buf_sl=buf_sl, i=i: s2.activation(
            out=iA[i][:], in_=buf_sl, func=ACT.Copy, scale=scal[:, 5:6]),
            waits=(t_rg, iA_free[i], t_dq_down_all[0], t_dq_down_all[1]), inc="scalar")
        tsg_ = sc.add("scalar", lambda s2, i=i: s2.activation(
            out=fsg[i][:], in_=iA[i][:], func=ACT.Sigmoid, scale=scal[:, 4:5]),
            waits=(tGo, fsg_free[i]), inc="scalar")
        tSo = sc.add("scalar", lambda s2, i=i: s2.activation(
            out=iB[i][:], in_=fsg[i][:], func=ACT.Copy, scale=65536.0),
            waits=(tsg_, iB_free[i]), inc="scalar")
        fsg_free[i] = tSo
        tP = sc.add("vector", lambda v, buf_sl=buf_sl, i=i: v.tensor_tensor(
            out=buf_sl, in0=iA[i][:], in1=iB[i][:], op=ALU.mult),
            waits=(tSo,), inc="vector")
        iA_free[i] = tP
        iB_free[i] = tP
        t_red = sc.add("vector", lambda v, buf_sl=buf_sl: v.reduce_max(
            out=cmx[:, 7:8], in_=buf_sl, axis=AX, apply_absolute_value=True),
            waits=(tP,), inc="vector")
        t_pmax = sc.add("vector", lambda v: v.tensor_tensor(
            out=cmx[:, 3:4], in0=cmx[:, 3:4], in1=cmx[:, 7:8], op=ALU.max),
            waits=(t_red,), inc="vector")

    t_p1 = tree(cmx[:, 3:4], ccs[0:1, 32:33], ALU.max, 128, (t_pmax,))
    t_bc2 = allreduce(2, t_p1)
    t_a = sc.add("vector", lambda v: v.tensor_tensor(out=scal[:, 22:23],
                 in0=ccb[:, 32:33], in1=scal[:, 4:5], op=ALU.mult),
                 waits=(t_bc2,), inc="vector")
    t_b2 = sc.add("vector", lambda v: v.tensor_scalar(out=scal[:, 6:7], in0=scal[:, 22:23],
                  scalar1=1.0 / (65536.0 * QMAX), scalar2=EPS12, op0=ALU.mult,
                  op1=ALU.max), waits=(t_a,), inc="vector")
    t_c = sc.add("vector", lambda v: v.reciprocal(out=scal[:, 23:24], in_=scal[:, 6:7]),
                 waits=(t_b2,), inc="vector")
    t_d2 = sc.add("vector", lambda v: v.tensor_scalar(out=scal[:, 24:25], in0=scal[:, 4:5],
                  scalar1=1.0 / 65536.0, scalar2=None, op0=ALU.mult), inc="vector")
    t_r2 = sc.add("vector", lambda v: v.tensor_tensor(out=scal[:, 7:8], in0=scal[:, 24:25],
                  in1=scal[:, 23:24], op=ALU.mult), waits=(t_c, t_d2), inc="vector")

    t_omax = None
    for it in range(NI):
        i = it % 2
        buf_sl = BUFN[:, it * S:(it + 1) * S]
        t_ul = sc.dma("sync", f"upst{i}", upst[i][:], up_d[it * 128:(it + 1) * 128, :],
                      waits=(upst_free[i],))
        tR2 = sc.add("scalar", lambda s2, buf_sl=buf_sl, i=i: s2.activation(
            out=iA[i][:], in_=buf_sl, func=ACT.Copy, scale=scal[:, 7:8]),
            waits=(t_r2, iA_free[i]), inc="scalar")
        tU = sc.add("scalar", lambda s2, i=i: s2.activation(
            out=iB[i][:], in_=upst[i][:], func=ACT.Copy, scale=scal[:, 3:4]),
            waits=(t_ru, t_ul, iB_free[i]), inc="scalar")
        tO = sc.add("vector", lambda v, buf_sl=buf_sl, i=i: v.tensor_tensor(
            out=buf_sl, in0=iA[i][:], in1=iB[i][:], op=ALU.mult),
            waits=(tR2, tU), inc="vector")
        iA_free[i] = tO
        iB_free[i] = tO
        upst_free[i] = tO
        t_red = sc.add("vector", lambda v, buf_sl=buf_sl: v.reduce_max(
            out=cmx[:, 7:8], in_=buf_sl, axis=AX, apply_absolute_value=True),
            waits=(tO,), inc="vector")
        t_omax = sc.add("vector", lambda v: v.tensor_tensor(
            out=cmx[:, 4:5], in0=cmx[:, 4:5], in1=cmx[:, 7:8], op=ALU.max),
            waits=(t_red,), inc="vector")

    t_o1t = tree(cmx[:, 4:5], ccs[0:1, 48:49], ALU.max, 128, (t_omax,))
    t_bc3 = allreduce(3, t_o1t)
    t_e = sc.add("vector", lambda v: v.tensor_tensor(out=scal[:, 25:26], in0=scal[:, 6:7],
                 in1=scal[:, 2:3], op=ALU.mult), waits=(t_b2, t_su), inc="vector")
    t_f = sc.add("vector", lambda v: v.tensor_tensor(out=scal[:, 22:23], in0=ccb[:, 48:49],
                 in1=scal[:, 25:26], op=ALU.mult), waits=(t_bc3, t_e), inc="vector")
    t_g = sc.add("vector", lambda v: v.tensor_scalar(out=scal[:, 26:27], in0=scal[:, 22:23],
                 scalar1=1.0 / QMAX, scalar2=EPS12, op0=ALU.mult, op1=ALU.max),
                 waits=(t_f,), inc="vector")
    t_h = sc.add("vector", lambda v: v.reciprocal(out=scal[:, 27:28], in_=scal[:, 26:27]),
                 waits=(t_g,), inc="vector")
    t_r3 = sc.add("vector", lambda v: v.tensor_tensor(out=scal[:, 28:29], in0=scal[:, 25:26],
                  in1=scal[:, 27:28], op=ALU.mult), waits=(t_h,), inc="vector")

    t_oq = {}
    for it in range(NI):
        i = it % 2
        buf_sl = BUFN[:, it * S:(it + 1) * S]
        tq1 = sc.add("scalar", lambda s2, buf_sl=buf_sl, i=i: s2.activation(
            out=iA[i][:], in_=buf_sl, func=ACT.Copy, scale=scal[:, 28:29]),
            waits=(t_r3, iA_free[i]), inc="scalar")
        t_oq[it] = sc.add("vector", lambda v, buf_sl=buf_sl, i=i: v.tensor_copy(
            out=buf_sl, in_=iA[i][:]), waits=(tq1,), inc="vector")
        iA_free[i] = t_oq[it]

    mem.free("iA0", "iA1", "iB0", "iB1", "fsg0", "fsg1", "iscr", "smx",
             "upst0", "upst1", "pug0", "pug1")

    jw6 = (sc.now("vector"), sc.now("scalar"))
    wdn = mem.sb("wdn", [128, 2 * NI * 512], BF16)
    xrl = [mem.sb(f"xrl{i}", [128, D], F32) for i in range(2)]
    arl = [mem.sb(f"arl{i}", [128, D], F32) for i in range(2)]
    ost = [mem.sb(f"ost{i}", [128, D], F32) for i in range(2)]
    pdn = [mem.ps(f"pdn{i}", [128, 512]) for i in range(2)]

    t_wdn = None
    for dc in range(2):
        for it in range(NI):
            t_wdn = sc.dma(
                "sync", "wdn",
                wdn[:, dc * NI * 512 + it * 512: dc * NI * 512 + (it + 1) * 512],
                wdnT_d[it * 128:(it + 1) * 128, dc * 512:(dc + 1) * 512],
                waits=t_dq_down_all + jw6)

    xrl_free = [None, None]
    arl_free = [None, None]
    ost_free = [None, None]
    pdn_hist = []
    out_fin = [None, None]
    for tt in range(NT):
        i = tt % 2
        t_xl = sc.dma("sync", f"xrl{i}", xrl[i][:], x_d[tt * 128:(tt + 1) * 128, :],
                      waits=(xrl_free[i],) + jw6)
        t_al = sc.dma("sync", f"arl{i}", arl[i][:], attn_d[tt * 128:(tt + 1) * 128, :],
                      waits=(arl_free[i], wo_sp_fin[0], wo_sp_fin[1]) + jw6)
        for dc in range(2):
            pp = pdn[len(pdn_hist) % 2]
            gate = pdn_hist[-2] if len(pdn_hist) >= 2 else None
            tmm = None
            for it in range(NI):
                tmm = sc.add("tensor", lambda te, pp=pp, it=it, tt=tt, dc=dc: te.matmul(
                    pp[:], BUFN[:, it * S + tt * 128: it * S + (tt + 1) * 128],
                    wdn[:, dc * NI * 512 + it * 512: dc * NI * 512 + (it + 1) * 512],
                    start=(it == 0), stop=(it == NI - 1)),
                    waits=(t_oq[it], t_wdn, gate),
                    inc="tensor" if it == NI - 1 else None)
            t_ev = sc.add("scalar", lambda s2, pp=pp, i=i, dc=dc: s2.activation(
                out=ost[i][:, dc * 512:(dc + 1) * 512], in_=pp[:], func=ACT.Copy,
                scale=scal[:, 26:27]),
                waits=(tmm, t_g, ost_free[i] if dc == 0 else None), inc="scalar")
            pdn_hist.append(t_ev)
            t_a1 = sc.add("vector", lambda v, i=i, dc=dc: v.tensor_tensor(
                out=ost[i][:, dc * 512:(dc + 1) * 512],
                in0=ost[i][:, dc * 512:(dc + 1) * 512],
                in1=xrl[i][:, dc * 512:(dc + 1) * 512], op=ALU.add),
                waits=(t_ev, t_xl), inc="vector")
            t_a2 = sc.add("vector", lambda v, i=i, dc=dc: v.tensor_tensor(
                out=ost[i][:, dc * 512:(dc + 1) * 512],
                in0=ost[i][:, dc * 512:(dc + 1) * 512],
                in1=arl[i][:, dc * 512:(dc + 1) * 512], op=ALU.add),
                waits=(t_a1, t_al), inc="vector")
        t_st = sc.dma("sync", f"ost{i}", out_d[tt * 128:(tt + 1) * 128, :], ost[i][:],
                      waits=(t_a2,))
        xrl_free[i] = t_st
        arl_free[i] = t_st
        ost_free[i] = t_st
        out_fin[i] = t_st

    sc.add("gpsimd", fn=None, waits=(out_fin[0], out_fin[1]))

    with nc.Block() as block:
        sc.emit(block)
    es.close()
    return nc


_NC_CACHE = [None]


def kernel(L0_attn_x, L0_attn_W_K, L0_attn_W_Q, L0_attn_W_V, L0_attn_W_O,
           L0_ffn_w_gate, L0_ffn_w_up, L0_ffn_w_down):
    f = np.float32
    x = np.ascontiguousarray(np.asarray(L0_attn_x, f))
    wq = np.ascontiguousarray(np.asarray(L0_attn_W_Q, f).transpose(2, 0, 1).reshape(D, D))
    wk = np.ascontiguousarray(np.asarray(L0_attn_W_K, f).transpose(2, 0, 1).reshape(D, D))
    wv = np.ascontiguousarray(np.asarray(L0_attn_W_V, f).transpose(2, 0, 1).reshape(D, D))
    wo = np.ascontiguousarray(np.asarray(L0_attn_W_O, f).T)
    wupn = np.ascontiguousarray(np.asarray(L0_ffn_w_up, f))
    wgn = np.ascontiguousarray(np.asarray(L0_ffn_w_gate, f))
    wdn_ = np.ascontiguousarray(np.asarray(L0_ffn_w_down, f))
    ident = np.eye(128, dtype=BF)
    identf = np.eye(128, dtype=f)
    kk = np.arange(128)[:, None]
    dmsk = np.concatenate(
        [((d * 128 + kk) <= np.arange(512)[None, :]) for d in range(4)],
        axis=1).astype(BF)
    ones1 = np.ones((1, 128), dtype=BF)
    ones1f = np.ones((1, 128), dtype=f)
    onesc = np.ones((128, 1), dtype=BF)

    if _NC_CACHE[0] is None:
        _NC_CACHE[0] = build_nc()
    nc = _NC_CACHE[0]
    in_maps = []
    for c in range(NCORES):
        in_maps.append({"x": np.ascontiguousarray(x[c]), "wqT": wq, "wkT": wk,
                        "wvT": wv, "woT": wo, "wup": wupn, "wgate": wgn,
                        "wdown": wdn_, "ident": ident, "dmask": dmsk,
                        "ones1": ones1, "ones1f": ones1f, "onesc": onesc, "identf": identf})
    res = run_bass_kernel_spmd(nc, in_maps, core_ids=list(range(NCORES)))
    return np.stack([res.results[c]["out"] for c in range(NCORES)], axis=0)


# version 14
# speedup vs baseline: 1.0859x; 1.0749x over previous
import numpy as np
import ml_dtypes
import concourse.bass as bass
import concourse.mybir as mybir
from concourse.bass_utils import run_bass_kernel_spmd

F32 = mybir.dt.float32
BF16 = mybir.dt.bfloat16
I32 = mybir.dt.int32
AX = mybir.AxisListType.X
ALU = mybir.AluOpType
ACT = mybir.ActivationFunctionType
BF = ml_dtypes.bfloat16

S, D, H, DH, I3 = 2048, 1024, 16, 64, 3072
NT, ND, NI = S // 128, D // 128, I3 // 128
NCORES = 8
QMAX = 32767.0
EPS12 = 1e-12

ENGS = ("tensor", "vector", "scalar", "gpsimd", "sync")


class Sched:

    def __init__(self, nc, es):
        self.nc = nc
        self.es = es
        self.steps = {e: [] for e in ENGS}
        names = list(ENGS) + ["cc"]
        self.sems = {n: es.enter_context(nc.semaphore("sem_" + n)) for n in names}
        self.cnt = {n: 0 for n in names}
        self.wm = {}

    def now(self, name):
        return (name, self.cnt[name])

    def stream(self, name):
        if name not in self.sems:
            self.sems[name] = self.es.enter_context(self.nc.semaphore("sem_" + name))
            self.cnt[name] = 0
        return name

    def add(self, eng, fn=None, waits=(), inc=None, n=1):
        todo = []
        for t in waits:
            if t is None:
                continue
            s, v = t
            key = (eng, s)
            if self.wm.get(key, 0) < v:
                self.wm[key] = v
                todo.append((self.sems[s], v))
        tok, sem = None, None
        if inc is not None:
            self.cnt[inc] += n
            tok = (inc, self.cnt[inc])
            sem = self.sems[inc]
        amt = n

        def thunk(obj, todo=todo, fn=fn, sem=sem, amt=amt):
            for (sm, v) in todo:
                obj.wait_ge(sm, v)
            ins = fn(obj) if fn is not None else None
            if sem is not None:
                ins.then_inc(sem, amt)

        self.steps[eng].append(thunk)
        return tok

    def dma(self, qeng, stream, out, in_, waits=()):
        return self.add(qeng, lambda o, out=out, in_=in_: o.dma_start(out=out, in_=in_),
                        waits=waits, inc=self.stream(stream), n=16)

    def dmat(self, stream, out, in_, waits=()):
        return self.add("sync",
                        lambda o, out=out, in_=in_: o.dma_start(out=out, in_=in_, transpose=True),
                        waits=waits, inc=self.stream(stream), n=16)

    def emit(self, block):
        for e in ENGS:
            thunks = self.steps[e]

            def body(obj, thunks=thunks):
                for t in thunks:
                    t(obj)
            getattr(block, e)(body)


class Mem:
    def __init__(self, nc):
        self.nc = nc
        self.guards = {}
        self.order = {}
        self.n = 0

    def _reg(self, name, g):
        t = g.__enter__()
        self.guards[name] = g
        self.order[name] = self.n
        self.n += 1
        return t

    def sb(self, name, shape, dt, side=None):
        return self._reg(name, self.nc.sbuf_tensor(name, shape, dt, side=side))

    def ps(self, name, shape, dt=F32):
        return self._reg(name, self.nc.psum_tensor(name, shape, dt))

    def free(self, *names):
        for n in sorted(names, key=lambda n: self.order[n], reverse=True):
            self.guards.pop(n).__exit__(None, None, None)


def build_nc():
    nc = bass.Bass("TRN2", target_bir_lowering=False)

    x_d = nc.dram_tensor("x", [S, D], F32, kind="ExternalInput")
    wT_d = {n: nc.dram_tensor("w" + n + "T", [D, D], F32, kind="ExternalInput")
            for n in ("q", "k", "v", "o")}
    wup_d = nc.dram_tensor("wup", [I3, D], F32, kind="ExternalInput")
    wgate_d = nc.dram_tensor("wgate", [I3, D], F32, kind="ExternalInput")
    wdown_d = nc.dram_tensor("wdown", [D, I3], F32, kind="ExternalInput")
    ident_d = nc.dram_tensor("ident", [128, 128], BF16, kind="ExternalInput")
    identf_d = nc.dram_tensor("identf", [128, 128], F32, kind="ExternalInput")
    dmask_d = nc.dram_tensor("dmask", [128, 2048], BF16, kind="ExternalInput")
    ones1_d = nc.dram_tensor("ones1", [1, 128], BF16, kind="ExternalInput")
    ones1f_d = nc.dram_tensor("ones1f", [1, 128], F32, kind="ExternalInput")
    onesc_d = nc.dram_tensor("onesc", [128, 1], BF16, kind="ExternalInput")
    out_d = nc.dram_tensor("out", [S, D], F32, kind="ExternalOutput")
    cc_in = [nc.dram_tensor(f"cc_in{r}", [1, 16], F32) for r in range(4)]
    cc_out = [nc.dram_tensor(f"cc_out{r}", [1, 16], F32) for r in range(4)]
    wdnT_d = nc.dram_tensor("wdnT", [I3, D], BF16)
    up_d = nc.dram_tensor("upsp", [I3, S], BF16)
    attn_d = nc.dram_tensor("attnsp", [S, D], F32)
    den_d = [nc.dram_tensor(f"densp{i}", [1, 512], F32) for i in range(4)]

    from contextlib import ExitStack
    es = ExitStack()
    sc = Sched(nc, es)
    mem = Mem(nc)

    ident = mem.sb("s_ident", [128, 128], BF16)
    identf = mem.sb("s_identf", [128, 128], F32)
    colws = mem.sb("s_colws", [128, 128], F32)
    ptre = mem.ps("s_ptre", [128, 512], F32)
    dmask = mem.sb("s_dmask", [128, 2048], BF16)
    ones1 = mem.sb("s_ones1", [1, 128], BF16)
    ones1f = mem.sb("s_ones1f", [1, 128], F32)
    onesc = mem.sb("s_onesc", [128, 1], BF16)
    trs = mem.sb("s_trs", [32, 32], F32)
    trs2 = mem.sb("s_trs2", [32, 32], F32)
    scal = mem.sb("scal", [128, 64], F32)
    ccs = mem.sb("ccs", [1, 64], F32)
    ccb = mem.sb("ccb", [128, 64], F32)
    cmx = mem.sb("cmx", [128, 8], F32)

    sc.dma("sync", "const", ident[:], ident_d[:, :])
    sc.dma("sync", "const", identf[:], identf_d[:, :])
    sc.dma("sync", "const", dmask[:], dmask_d[:, :])
    sc.dma("sync", "const", ones1[:], ones1_d[:, :])
    sc.dma("sync", "const", ones1f[:], ones1f_d[:, :])
    t_const = sc.dma("sync", "const", onesc[:], onesc_d[:, :])
    t_id = t_dm = t_o1 = t_o1f = t_const
    t_cmx0 = sc.add("vector", lambda v: v.memset(cmx[:], 0.0), inc="vector")
    t_trs0 = sc.add("vector", lambda v: v.memset(trs[:], 0.0), inc="vector")
    t_ccs0 = sc.add("vector", lambda v: v.memset(ccs[:], 0.0), inc="vector")
    t_cw0 = sc.add("vector", lambda v: v.memset(colws[:], 0.0), inc="vector")

    tree_busy = [None]

    def tree(col_ap, out_ap, op, p0, waits):
        t1 = sc.add("vector", lambda v, col_ap=col_ap, p0=p0: v.tensor_copy(
            out=colws[0:p0, 0:1], in_=col_ap),
            waits=tuple(waits) + (tree_busy[0], t_cw0), inc="vector")
        t2 = sc.add("tensor", lambda te: te.transpose(ptre[:, 0:128], colws[:], identf[:]),
                    waits=(t1, t_id, tree_busy[0]), inc="tensor")
        tok = sc.add("vector", lambda v, out_ap=out_ap, op=op, p0=p0: v.tensor_reduce(
            out=out_ap, in_=ptre[0:1, 0:p0], axis=AX, op=op), waits=(t2,), inc="vector")
        tree_busy[0] = tok
        return tok

    def allreduce(r, t_staged):
        t_w = sc.dma("gpsimd", "gcc", cc_in[r][:, :], ccs[0:1, 16 * r:16 * r + 16],
                     waits=(t_staged, t_ccs0))
        t_ar = sc.add("gpsimd", lambda g, r=r: g.collective_compute(
            "AllReduce", ALU.max, replica_groups=[list(range(NCORES))],
            ins=[cc_in[r].ap().opt()], outs=[cc_out[r].ap().opt()]),
            waits=(t_w,), inc="cc")
        return sc.dma("sync", "ccb", ccb[:, 16 * r:16 * r + 16],
                      cc_out[r].ap().partition_broadcast(128), waits=(t_ar,))

    xnT = mem.sb("xnT", [128, ND * S], BF16, side="right")
    xst = [mem.sb(f"xst{i}", [128, D], F32) for i in range(2)]
    xnst = [mem.sb(f"xnst{i}", [128, D], BF16) for i in range(2)]
    lns = mem.sb("lns", [128, 64], F32)
    sqd = mem.sb("sqd", [128, D], F32)
    ptr = [mem.ps(f"ptr{i}", [128, 128], BF16) for i in range(2)]

    xst_free = [None, None]
    xnst_free = [None, None]
    ptr_free = {}
    t_xn_done = None
    t_xnmax = None
    t_sums = {}
    for tt in range(NT):
        i = tt % 2
        t_ld = sc.dma("sync", f"xst{i}", xst[i][:], x_d[tt * 128:(tt + 1) * 128, :],
                      waits=(xst_free[i],))
        x_ap = xst[i][:]
        t1 = sc.add("vector", lambda v, x_ap=x_ap, tt=tt: v.reduce_sum(
            out=lns[:, tt:tt + 1], in_=x_ap, axis=AX), waits=(t_ld,), inc="vector")
        t2 = sc.add("vector", lambda v, x_ap=x_ap, tt=tt: v.scalar_tensor_tensor(
            out=sqd[:], in0=x_ap, scalar=1.0, in1=x_ap, op0=ALU.mult, op1=ALU.mult,
            accum_out=lns[:, 16 + tt:17 + tt]), waits=(t1,), inc="vector")
        t_sums[tt] = t2
        xst_free[tt % 2] = t2
    tb = sc.add("vector", lambda v: v.tensor_scalar(
        out=lns[:, 0:16], in0=lns[:, 0:16], scalar1=1.0 / D, scalar2=None,
        op0=ALU.mult), waits=(t_sums[NT - 1],), inc="vector")
    tb = sc.add("vector", lambda v: v.tensor_scalar(
        out=lns[:, 16:32], in0=lns[:, 16:32], scalar1=1.0 / D, scalar2=None,
        op0=ALU.mult), waits=(tb,), inc="vector")
    tb = sc.add("vector", lambda v: v.scalar_tensor_tensor(
        out=lns[:, 32:48], in0=lns[:, 0:16], scalar=1.0, in1=lns[:, 0:16],
        op0=ALU.mult, op1=ALU.mult), waits=(tb,), inc="vector")
    tb = sc.add("vector", lambda v: v.tensor_tensor(
        out=lns[:, 32:48], in0=lns[:, 16:32], in1=lns[:, 32:48],
        op=ALU.subtract), waits=(tb,), inc="vector")
    tb = sc.add("vector", lambda v: v.tensor_scalar(
        out=lns[:, 32:48], in0=lns[:, 32:48], scalar1=1e-5, scalar2=None,
        op0=ALU.add), waits=(tb,), inc="vector")
    tb = sc.add("scalar", lambda s: s.activation(
        out=lns[:, 48:64], in_=lns[:, 32:48], func=ACT.Sqrt), waits=(tb,), inc="scalar")
    tb = sc.add("vector", lambda v: v.reciprocal(
        out=lns[:, 32:48], in_=lns[:, 48:64]), waits=(tb,), inc="vector")
    t_stats = sc.add("vector", lambda v: v.scalar_tensor_tensor(
        out=lns[:, 48:64], in0=lns[:, 0:16], scalar=-1.0, in1=lns[:, 32:48],
        op0=ALU.mult, op1=ALU.mult), waits=(tb,), inc="vector")
    for tt in range(NT):
        i = tt % 2
        t_ld = sc.dma("sync", f"xst{i}", xst[i][:], x_d[tt * 128:(tt + 1) * 128, :],
                      waits=(xst_free[i],))
        x_ap = xst[i][:]
        n_ap = xnst[i][:]
        t7 = sc.add("vector", lambda v, x_ap=x_ap, n_ap=n_ap, tt=tt: v.tensor_scalar(
            out=n_ap, in0=x_ap, scalar1=lns[:, 32 + tt:33 + tt],
            scalar2=lns[:, 48 + tt:49 + tt], op0=ALU.mult, op1=ALU.add),
            waits=(t_ld, t_stats, xnst_free[i]), inc="vector")
        xst_free[i] = t7
        t8 = sc.add("vector", lambda v, n_ap=n_ap: v.reduce_max(
            out=lns[:, 0:1], in_=n_ap, axis=AX, apply_absolute_value=True),
            waits=(t7, t_cmx0), inc="vector")
        t_xnmax = sc.add("vector", lambda v: v.tensor_tensor(
            out=cmx[:, 0:1], in0=cmx[:, 0:1], in1=lns[:, 0:1], op=ALU.max),
            waits=(t8,), inc="vector")
        last_tr = None
        for dt in range(ND):
            pp = ptr[dt % 2]
            tmm = sc.add("tensor", lambda te, n_ap=n_ap, dt=dt, pp=pp: te.transpose(
                pp[:], n_ap[:, dt * 128:(dt + 1) * 128], ident[:]),
                waits=(t7, t_id, ptr_free.get(dt % 2)), inc="tensor")
            tev = sc.add("vector", lambda v, pp=pp, dt=dt, tt=tt: v.tensor_copy(
                out=xnT[:, dt * S + tt * 128: dt * S + (tt + 1) * 128], in_=pp[:]),
                waits=(tmm,), inc="vector")
            ptr_free[dt % 2] = tev
            last_tr = tmm
        xnst_free[i] = last_tr
        t_xn_done = tev
    mem.free("xst0", "xst1", "xnst0", "xnst1", "lns", "sqd", "ptr0", "ptr1")

    t_st0 = tree(cmx[:, 0:1], ccs[0:1, 0:1], ALU.max, 128, (t_xnmax,))
    t_bc0 = allreduce(0, t_st0)
    t_sx = sc.add("vector", lambda v: v.tensor_scalar(out=scal[:, 0:1], in0=ccb[:, 0:1],
                  scalar1=1.0 / QMAX, scalar2=EPS12, op0=ALU.mult, op1=ALU.max),
                  waits=(t_bc0,), inc="vector")
    t_rx = sc.add("vector", lambda v: v.reciprocal(out=scal[:, 1:2], in_=scal[:, 0:1]),
                  waits=(t_sx,), inc="vector")

    jw1 = (sc.now("vector"), sc.now("scalar"), sc.now("tensor"))
    wsb = {n: mem.sb("w" + n, [128, ND * D], BF16, side="right") for n in ("q", "k", "v")}
    t_wld = {}
    for n in ("q", "k", "v"):
        tk = None
        for kd in range(ND):
            tk = sc.dma("gpsimd", "wcast", wsb[n][:, kd * D:(kd + 1) * D],
                        wT_d[n][kd * 128:(kd + 1) * 128, :], waits=jw1)
        t_wld[n] = tk
    for n in ("q", "k", "v"):
        t_wld[n] = tk

    kT2 = mem.sb("kT2", [128, S], BF16)
    qT2 = mem.sb("qT2", [128, S], BF16)
    vx = [mem.sb(f"vx{i}", [128, NT * 65], BF16) for i in range(2)]
    nsc = mem.sb("nsc", [128, S], BF16)
    expw = [mem.sb(f"expw{i}", [128, NT * 512], BF16) for i in range(2)]
    zT2 = mem.sb("zT2", [128, ND * S], BF16)
    zTo = mem.sb("zTo", [64, S], BF16)
    zsc = [mem.sb(f"zsc{i}", [65, 512], F32) for i in range(4)]
    denbc = [mem.sb(f"denbc{i}", [64, 512], F32) for i in range(4)]
    denbr = [mem.sb(f"denbr{i}", [64, 512], F32) for i in range(4)]
    ps_ = [mem.ps(f"ps{i}", [128, 1024]) for i in range(2)]
    pzx = mem.ps("pzx", [128, 1024])
    pzb = [pzx[0:65, 0:512], ptre[0:65, 0:512]]
    pnr = pzx[0:1, 512:1024]
    pmx = mem.ps("pmx", [128, 512])
    prb = pmx[0:64, 0:512]

    t_vones = [sc.add("vector", lambda v, i=i: v.memset(
        vx[i][:].rearrange("p (t c) -> p t c", c=65)[:, :, 64:65], 1.0), inc="vector")
        for i in range(2)]

    qkv_hist = []
    vx_free = [None, None]
    prev_exp = None
    pz_free = {0: None, 1: None}
    prb_free = None
    zn_prev = [None, None, None, None]
    t_qkv_last = None
    t_attn_last_te = None
    pmx_free = None
    hp_state = {}
    zto_free = None
    t_zrepack = {}

    def qkv_slot(idx):
        return pzx[0:128, 0:512] if idx % 2 == 0 else pmx[0:128, 0:512]

    for hp in range(H // 2):
        for wnm, dst in (("k", kT2), ("q", qT2)):
            for c in range(4):
                pp = qkv_slot(len(qkv_hist))
                gate = qkv_hist[-2] if len(qkv_hist) >= 2 else None
                tmm = None
                for kd in range(ND):
                    tmm = sc.add("tensor", lambda te, pp=pp, wnm=wnm, kd=kd, hp=hp, c=c: te.matmul(
                        pp, wsb[wnm][:, kd * D + hp * 128: kd * D + hp * 128 + 128],
                        xnT[:, kd * S + c * 512: kd * S + (c + 1) * 512],
                        start=(kd == 0), stop=(kd == ND - 1)),
                        waits=(t_xn_done, t_wld[wnm], gate,
                               pz_free[0] if c == 0 and wnm == "k" else None,
                               pz_free[1] if c == 0 and wnm == "k" else None,
                               pmx_free if c == 0 and wnm == "k" else None,
                               hp_state.get("score_last") if c == 0 and wnm == "k" else None),
                        inc="tensor" if kd == ND - 1 else None)
                tev = sc.add("vector", lambda v, pp=pp, c=c, dst=dst: v.tensor_copy(
                    out=dst[:, c * 512:(c + 1) * 512], in_=pp),
                    waits=(tmm,), inc="vector")
                qkv_hist.append(tev)
        for tt in range(NT):
            pp = qkv_slot(len(qkv_hist))
            gate = qkv_hist[-2] if len(qkv_hist) >= 2 else None
            tmm = None
            for kd in range(ND):
                tmm = sc.add("tensor", lambda te, pp=pp, kd=kd, tt=tt, hp=hp: te.matmul(
                    pp[:, 0:128], xnT[:, kd * S + tt * 128: kd * S + (tt + 1) * 128],
                    wsb["v"][:, kd * D + hp * 128: kd * D + hp * 128 + 128],
                    start=(kd == 0), stop=(kd == ND - 1)),
                    waits=(t_xn_done, t_wld["v"], gate,
                           vx_free[0] if tt == 0 else None,
                           vx_free[1] if tt == 0 else None),
                    inc="tensor" if kd == ND - 1 else None)
            tev = None
            for half in range(2):
                tev = sc.add("vector", lambda v, pp=pp, half=half, tt=tt: v.tensor_copy(
                    out=vx[half][:, tt * 65: tt * 65 + 64],
                    in_=pp[:, 64 * half:64 * half + 64]),
                    waits=(tmm, t_vones[half]), inc="vector")
            qkv_hist.append(tev)
        t_qkv_last = tmm

        bias_tok = {}
        pmx_busy = hp_state.get("pmx_busy")
        nsc_reader = hp_state.get("nsc_busy")
        nrm_prev = hp_state.get("nrm_prev")
        nrm_tok = {}
        for j, wch in enumerate(("q", "k")):
            srcT = qT2 if wch == "q" else kT2
            tsq = sc.add("vector", lambda v, srcT=srcT: v.tensor_tensor(
                out=nsc[:], in0=srcT[:], in1=srcT[:], op=ALU.mult),
                waits=(qkv_hist[-1], prev_exp, nsc_reader), inc="vector")
            for half in range(2):
                sl = 8 + 2 * half + j
                tred = None
                for c in range(4):
                    tm = sc.add("tensor", lambda te, c=c, half=half: te.matmul(
                        pnr, onesc[64 * half:64 * half + 64, :],
                        nsc[64 * half:64 * half + 64, c * 512:(c + 1) * 512],
                        start=True, stop=True),
                        waits=(tsq, t_o1f, tred, pmx_busy, hp_state.get("pnr_busy")),
                        inc="tensor")
                    tred = sc.add("vector", lambda v, c=c: v.reduce_max(
                        out=scal[0:1, 30 + c:31 + c], in_=pnr, axis=AX),
                        waits=(tm, nrm_prev), inc="vector")
                nsc_reader = tm
                hp_state["pnr_busy"] = tred
                tok = sc.add("vector", lambda v: v.tensor_tensor(
                    out=scal[0:1, 30:31], in0=scal[0:1, 30:31], in1=scal[0:1, 31:32],
                    op=ALU.max), waits=(tred,), inc="vector")
                tok = sc.add("vector", lambda v: v.tensor_tensor(
                    out=scal[0:1, 32:33], in0=scal[0:1, 32:33], in1=scal[0:1, 33:34],
                    op=ALU.max), waits=(tok,), inc="vector")
                nrm_tok[(wch, half)] = nrm_prev = sc.add(
                    "vector", lambda v, sl=sl: v.tensor_tensor(
                        out=scal[0:1, sl:sl + 1], in0=scal[0:1, 30:31],
                        in1=scal[0:1, 32:33], op=ALU.max), waits=(tok,), inc="vector")
        hp_state["nsc_busy"] = nsc_reader
        hp_state["nrm_prev"] = nrm_prev
        for half in range(2):
            toks = []
            for j, wch in enumerate(("q", "k")):
                sl = 8 + 2 * half + j
                tok = sc.add("tensor", lambda te, sl=sl, half=half, j=j: te.matmul(
                    pmx[:, 2 + 2 * half + j:3 + 2 * half + j], ones1f[:],
                    scal[0:1, sl:sl + 1], start=True, stop=True),
                    waits=(nrm_tok[(wch, half)], t_o1f, prb_free, pmx_busy,
                           qkv_hist[-1]), inc="tensor")
                toks.append(tok)
            tok = sc.add("vector", lambda v, half=half: v.tensor_copy(
                out=scal[:, 34:35], in_=pmx[:, 2 + 2 * half:3 + 2 * half]),
                waits=tuple(toks), inc="vector")
            tok = sc.add("vector", lambda v, half=half: v.tensor_tensor(
                out=scal[:, 16 + half:17 + half], in0=scal[:, 34:35],
                in1=pmx[:, 3 + 2 * half:4 + 2 * half], op=ALU.mult),
                waits=(tok,), inc="vector")
            pmx_busy = tok
            pmx_free = tok
            tok = sc.add("scalar", lambda s2, half=half: s2.activation(
                out=scal[:, 18 + half:19 + half], in_=scal[:, 16 + half:17 + half],
                func=ACT.Sqrt), waits=(tok,), inc="scalar")
            bias_tok[half] = sc.add("vector", lambda v, half=half: v.tensor_scalar(
                out=scal[:, 20 + half:21 + half], in0=scal[:, 18 + half:19 + half],
                scalar1=-0.125, scalar2=None, op0=ALU.mult), waits=(tok,), inc="vector")
        hp_state["pmx_busy"] = pmx_busy

        units = [(qb, half) for qb in range(4) for half in range(2)]
        pending = None

        def emit_av_chunk(st):
            nonlocal t_attn_last_te
            p_hp, p_qb, p_half, p_nkt, p_gates, pos = st[:6]
            b = st[7]
            end = min(pos + 2, p_nkt)
            for kt in range(pos, end):
                tav = sc.add("tensor", lambda te, kt=kt, p_half=p_half, p_nkt=p_nkt, b=b: te.matmul(
                    pzb[b], vx[p_half][:, kt * 65:(kt + 1) * 65],
                    expw[p_half][:, kt * 512:(kt + 1) * 512],
                    start=(kt == 0), stop=(kt == p_nkt - 1)),
                    waits=(p_gates[kt], pz_free[b] if kt == 0 else None),
                    inc="tensor" if kt == p_nkt - 1 else None)
                if kt == p_nkt - 1:
                    t_attn_last_te = tav
                    hp_state["av_last_%d" % p_half] = tav
                    st[6] = tav
            st[5] = end
            return end < p_nkt

        def finish_av(st):
            nonlocal prb_free, zto_free
            while emit_av_chunk(st):
                pass
            p_hp, p_qb, p_half, p_nkt, p_gates, pos, tav, b = st
            ui = hp_state.get("ui", 0)
            hp_state["ui"] = ui + 1
            w = ui % 4
            t_ev = sc.add("vector", lambda v, w=w, b=b: v.tensor_copy(
                out=zsc[w][:], in_=pzb[b]), waits=(tav, zn_prev[w]), inc="vector")
            pz_free[b] = t_ev
            t_dw = sc.dma("gpsimd", "gdwr%d" % w, den_d[w][:, :], zsc[w][64:65, :],
                          waits=(t_ev,))
            t_db = sc.dma("gpsimd", "gdbc%d" % w, denbc[w][:],
                          den_d[w].ap().partition_broadcast(64), waits=(t_dw,))
            t_rc = sc.add("vector", lambda v, w=w: v.reciprocal(
                out=denbr[w][:], in_=denbc[w][:]), waits=(t_db, zn_prev[w]),
                inc="vector")
            if p_half == 0:
                out_ap = zT2[0:64, p_hp * S + p_qb * 512: p_hp * S + (p_qb + 1) * 512]
                t_zn = sc.add("vector", lambda v, out_ap=out_ap, w=w: v.tensor_tensor(
                    out=out_ap, in0=zsc[w][0:64, :], in1=denbr[w][:], op=ALU.mult),
                    waits=(t_rc,), inc="vector")
            else:
                out_ap = zTo[:, p_qb * 512:(p_qb + 1) * 512]
                t_zn = sc.add("vector", lambda v, out_ap=out_ap, w=w: v.tensor_tensor(
                    out=out_ap, in0=zsc[w][0:64, :], in1=denbr[w][:], op=ALU.mult),
                    waits=(t_rc, zto_free), inc="vector")
                if p_qb == 3:
                    zto_free = sc.dma("gpsimd", "gzto",
                                      zT2[64:128, p_hp * S:(p_hp + 1) * S], zTo[:],
                                      waits=(t_zn,))
                    t_zrepack[p_hp] = zto_free
            zn_prev[w] = t_zn
            prb_free = t_zn
            return t_zn

        for (qb, half) in units:
            nkt = 4 * qb + 4
            bias_ap = scal[:, 20 + half:21 + half]
            p0, p1_ = 64 * half, 64 * half + 64
            exp_toks = {}
            gates = {}
            for g in range(nkt // 2):
                tmm2 = None
                for kt in (2 * g, 2 * g + 1):
                    slot = kt % 2
                    tmm2 = sc.add("tensor", lambda te, kt=kt, qb=qb, p0=p0, p1_=p1_, half=half, slot=slot: te.matmul(
                        ps_[half][:, slot * 512:(slot + 1) * 512],
                        kT2[p0:p1_, kt * 128:(kt + 1) * 128],
                        qT2[p0:p1_, qb * 512:(qb + 1) * 512], start=True, stop=True),
                        waits=(exp_toks.get(g - 1), qkv_hist[-1],
                               hp_state.get("av_last_%d" % half) if g == 0 else None),
                        inc="tensor" if kt % 2 == 1 else None)
                if pending is not None:
                    emit_av_chunk(pending)
                texp = sc.add("scalar", lambda s2, g=g, half=half, bias_ap=bias_ap: s2.activation(
                    out=expw[half][:, g * 1024:(g + 1) * 1024], in_=ps_[half][:, 0:1024],
                    func=ACT.Exp, scale=0.125, bias=bias_ap),
                    waits=(tmm2, bias_tok[half]), inc="scalar")
                exp_toks[g] = texp
                for kt in (2 * g, 2 * g + 1):
                    d = kt - 4 * qb
                    if d >= 0:
                        gates[kt] = sc.add("gpsimd", lambda g2, half=half, kt=kt, d=d: g2.tensor_tensor(
                            out=expw[half][:, kt * 512:(kt + 1) * 512],
                            in0=expw[half][:, kt * 512:(kt + 1) * 512],
                            in1=dmask[:, d * 512:(d + 1) * 512], op=ALU.mult),
                            waits=(texp, t_dm), inc="gpsimd")
                    else:
                        gates[kt] = texp
            prev_exp = exp_toks[nkt // 2 - 1]
            hp_state["score_last"] = tmm2
            if pending is not None:
                finish_av(pending)
            pending = [hp, qb, half, nkt, gates, 0, None,
                       (4 * hp + qb * 2 + half) % 2]
        t_zn_last = finish_av(pending)
        vx_free = [t_attn_last_te, t_attn_last_te]
    t_attn_z = t_zn_last

    mem.free("wq", "wk", "wv")
    wo2 = mem.sb("wo2", [128, ND * D], BF16, side="right")
    tk = None
    for kd in range(ND):
        tk = sc.dma("gpsimd", "wcast", wo2[:, kd * D:(kd + 1) * D],
                    wT_d["o"][kd * 128:(kd + 1) * 128, :], waits=(qkv_hist[-1],))
    t_wo_ld = tk
    t_zrp = tuple(t_zrepack[p] for p in range(H // 2))
    wos = [mem.sb(f"wos{i}", [128, D], F32) for i in range(2)]
    wos_free = [None, None]
    wo_sp_fin = [None, None]
    t_wo_sp = None
    for tt in range(NT):
        i = tt % 2
        t_ev = None
        for c in range(2):
            pp = qkv_slot(len(qkv_hist))
            gate = qkv_hist[-2] if len(qkv_hist) >= 2 else None
            tmm = None
            for kd in range(ND):
                tmm = sc.add("tensor", lambda te, pp=pp, kd=kd, tt=tt, c=c: te.matmul(
                    pp, zT2[:, kd * S + tt * 128: kd * S + (tt + 1) * 128],
                    wo2[:, kd * D + c * 512: kd * D + (c + 1) * 512],
                    start=(kd == 0), stop=(kd == ND - 1)),
                    waits=(t_attn_z, t_wo_ld, gate) + t_zrp,
                    inc="tensor" if kd == ND - 1 else None)
            t_ev = sc.add("scalar", lambda s2, pp=pp, i=i, c=c: s2.activation(
                out=wos[i][:, c * 512:(c + 1) * 512], in_=pp, func=ACT.Copy),
                waits=(tmm, wos_free[i]), inc="scalar")
            qkv_hist.append(t_ev)
        t_wo_sp = sc.dma("sync", f"wos{i}", attn_d[tt * 128:(tt + 1) * 128, :],
                         wos[i][:], waits=(t_ev,))
        wos_free[i] = t_wo_sp
        wo_sp_fin[i] = t_wo_sp

    mem.free("kT2", "qT2", "vx0", "vx1", "nsc", "expw0", "expw1", "zT2", "zTo",
             "zsc0", "zsc1", "zsc2", "zsc3", "denbc0", "denbc1", "denbc2",
             "denbc3", "denbr0", "denbr1", "denbr2", "denbr3", "wo2", "wos0", "wos1",
             "ps0", "ps1", "pzx", "pmx")

    jw2 = (sc.now("vector"), sc.now("scalar"), sc.now("tensor"),
           sc.now("wos0") if "wos0" in sc.cnt else None,
           sc.now("wos1") if "wos1" in sc.cnt else None)
    BUFN = mem.sb("BUF", [128, NI * S], BF16)
    iscr = mem.sb("iscr", [128, I3], I32)
    smx = mem.sb("smx", [128, 96], F32)
    wnat = [mem.sb(f"wnat{i}", [128, D], F32, side="right") for i in range(2)]
    wdqo = [mem.sb(f"wdqo{i}", [128, D], BF16, side="right") for i in range(2)]
    wnatD = [mem.sb("wnatD", [128, I3], F32, side="right")]
    wdqoD = [mem.sb("wdqoD", [128, I3], BF16, side="right")]
    bmx = mem.sb("bmx", [128, 96], F32, side="right")
    rix = mem.sb("rix", [128, 96], F32, side="right")
    wtst = [mem.sb("wtst0", [128, I3], BF16, side="right")]
    ptr2 = [mem.ps(f"ptr2_{i}", [128, 128], BF16) for i in range(2)]

    wnat_free = [None, None]
    wdqo_free = [None, None]
    wnatD_free = [None]
    wdqoD_free = [None]
    wtst_free = [None]
    UG_BUFS = (wnat, wdqo, wnat_free, wdqo_free, "wnat")
    DN_BUFS = (wnatD, wdqoD, wnatD_free, wdqoD_free, "wnatD")
    ptr2_free = {}
    prev_fscr = None

    def dequant_tile(idx, src_ap, w, waits=(), bufs=None):
        wnat_, wdqo_, wnat_free_, wdqo_free_, strm = bufs
        i = idx % len(wnat_)
        nb = w // 32
        t_l = sc.dma("sync", strm + str(i), wnat_[i][:, 0:w], src_ap,
                     waits=(wnat_free_[i],) + jw2)
        w3 = wnat_[i][:, 0:w].rearrange("p (n b) -> p n b", b=32)
        t_b = sc.add("vector", lambda v, w3=w3, nb=nb: v.reduce_max(
            out=bmx[:, 0:nb], in_=w3, axis=AX, apply_absolute_value=True),
            waits=(t_l, dq_state["prev_q"]), inc="vector")
        t_s = sc.add("vector", lambda v, nb=nb: v.tensor_scalar(
            out=smx[:, 0:nb], in0=bmx[:, 0:nb], scalar1=1.0 / 7.0, scalar2=EPS12,
            op0=ALU.mult, op1=ALU.max), waits=(t_b,), inc="vector")
        t_r = sc.add("vector", lambda v, nb=nb: v.reciprocal(
            out=rix[:, 0:nb], in_=smx[:, 0:nb]), waits=(t_s,), inc="vector")
        rb = rix[:, 0:nb].unsqueeze(2).broadcast_to([128, nb, 32])
        sb_ = smx[:, 0:nb].unsqueeze(2).broadcast_to([128, nb, 32])
        t_q = sc.add("vector", lambda v, w3=w3, rb=rb, w=w: v.tensor_tensor(
            out=iscr[:, 0:w].rearrange("p (n b) -> p n b", b=32), in0=w3, in1=rb,
            op=ALU.mult), waits=(t_r, dq_state["prev_q"]), inc="vector")
        dq_state["prev_q"] = t_q
        t_w2 = sc.add("vector", lambda v, i=i, sb_=sb_, w=w, wdqo_=wdqo_: v.tensor_tensor(
            out=wdqo_[i][:, 0:w].rearrange("p (n b) -> p n b", b=32),
            in0=iscr[:, 0:w].rearrange("p (n b) -> p n b", b=32), in1=sb_,
            op=ALU.mult), waits=(t_q, wdqo_free_[i]) + tuple(waits), inc="vector")
        wnat_free_[i] = t_w2
        return i, t_w2

    dq_state = {"prev_q": None}
    t_dq_down = None

    def down_dq_chunk(j, ch):
        nonlocal t_dq_down
        i = 0
        nb = 32
        w0 = ch * 1024
        t_l = sc.dma("sync", "wnatD0", wnatD[0][:, w0:w0 + 1024],
                     wdown_d[j * 128:(j + 1) * 128, w0:w0 + 1024],
                     waits=(wnatD_free[0] if ch == 0 else None,) + jw2)
        w3 = wnatD[0][:, w0:w0 + 1024].rearrange("p (n b) -> p n b", b=32)
        t_b = sc.add("vector", lambda v, w3=w3: v.reduce_max(
            out=bmx[:, 32:64], in_=w3, axis=AX, apply_absolute_value=True),
            waits=(t_l, dq_state["prev_q"]), inc="vector")
        t_s = sc.add("vector", lambda v: v.tensor_scalar(
            out=smx[:, 32:64], in0=bmx[:, 32:64], scalar1=1.0 / 7.0, scalar2=EPS12,
            op0=ALU.mult, op1=ALU.max), waits=(t_b,), inc="vector")
        t_r = sc.add("vector", lambda v: v.reciprocal(
            out=rix[:, 32:64], in_=smx[:, 32:64]), waits=(t_s,), inc="vector")
        rb = rix[:, 32:64].unsqueeze(2).broadcast_to([128, 32, 32])
        sb_ = smx[:, 32:64].unsqueeze(2).broadcast_to([128, 32, 32])
        t_q = sc.add("vector", lambda v, w3=w3, rb=rb, w0=w0: v.tensor_tensor(
            out=iscr[:, w0:w0 + 1024].rearrange("p (n b) -> p n b", b=32), in0=w3,
            in1=rb, op=ALU.mult), waits=(t_r, dq_state["prev_q"]), inc="vector")
        dq_state["prev_q"] = t_q
        t_w2 = sc.add("vector", lambda v, sb_=sb_, w0=w0: v.tensor_tensor(
            out=wdqoD[0][:, w0:w0 + 1024].rearrange("p (n b) -> p n b", b=32),
            in0=iscr[:, w0:w0 + 1024].rearrange("p (n b) -> p n b", b=32), in1=sb_,
            op=ALU.mult), waits=(t_q, wdqoD_free[0] if ch == 0 else None),
            inc="vector")
        tevT = None
        for k in range(8):
            kk2 = ch * 8 + k
            pp = ptr2[k % 2]
            tmmT = sc.add("tensor", lambda te, pp=pp, kk2=kk2: te.transpose(
                pp[:], wdqoD[0][:, kk2 * 128:(kk2 + 1) * 128], ident[:]),
                waits=(t_w2, t_id, ptr2_free.get(k % 2)), inc="tensor")
            tevT = sc.add("scalar", lambda s2, pp=pp, kk2=kk2: s2.activation(
                out=wtst[0][:, kk2 * 128:(kk2 + 1) * 128], in_=pp[:], func=ACT.Copy),
                waits=(tmmT, wtst_free[0] if (kk2 == 0) else None), inc="scalar")
            ptr2_free[k % 2] = tevT
        if ch == 2:
            wdqoD_free[0] = tevT
            wnatD_free[0] = tevT
            t_dq_down = sc.dma(
                "sync", "wtsp0",
                wdnT_d[:, j * 128:(j + 1) * 128].rearrange("(t p) c -> p t c", p=128),
                wtst[0][:].rearrange("p (t c) -> p t c", c=128), waits=(tevT,))
            wtst_free[0] = t_dq_down

    t_xq = {}
    prev_xq = None
    for dt in range(ND):
        sl = xnT[:, dt * S:(dt + 1) * S]
        ti = sc.add("vector", lambda v, sl=sl: v.tensor_scalar(
            out=iscr[:, 0:S], in0=sl, scalar1=scal[:, 1:2], scalar2=None,
            op0=ALU.mult), waits=(t_rx, t_qkv_last, prev_xq), inc="vector")
        t_xq[dt] = prev_xq = sc.add("vector", lambda v, sl=sl: v.tensor_scalar(
            out=sl, in0=iscr[:, 0:S], scalar1=scal[:, 0:1], scalar2=None,
            op0=ALU.mult), waits=(ti,), inc="vector")
    t_xq_all = t_xq[ND - 1]

    upst = [mem.sb(f"upst{i}", [128, S], BF16) for i in range(2)]
    wstU = [mem.sb(f"wstU{i}", [128, D], BF16) for i in range(2)]
    wstG = [mem.sb(f"wstG{i}", [128, D], BF16) for i in range(2)]
    pug = [mem.ps(f"pug{i}", [128, 512]) for i in range(2)]

    wst_free = {"up": [None, None], "gate": [None, None]}
    upst_free = [None, None]
    t_mx_prev = None
    deferred_mx = None
    pug_hist = []
    t_upmax = None
    t_gatemax = None
    dq_state["prev_q"] = t_xq_all
    for it in range(NI):
        i = it % 2
        wtoks = {}
        dq2 = {}
        for nmi, (nm, wbuf, srcw_d) in enumerate(
                (("up", wstU, wup_d), ("gate", wstG, wgate_d))):
            dqi, t_w2 = dequant_tile(2 * it + nmi, srcw_d[it * 128:(it + 1) * 128, :],
                                     D, bufs=UG_BUFS)
            dq2[nm] = (dqi, t_w2)
        down_dq_chunk(it // 3, it % 3)
        for nmi, (nm, wbuf, srcw_d) in enumerate(
                (("up", wstU, wup_d), ("gate", wstG, wgate_d))):
            dqi, t_w2 = dq2[nm]
            tevT = None
            for kd in range(ND):
                pp = ptr2[kd % 2]
                tmmT = sc.add("tensor", lambda te, pp=pp, dqi=dqi, kd=kd: te.transpose(
                    pp[:], wdqo[dqi][:, kd * 128:(kd + 1) * 128], ident[:]),
                    waits=(t_w2, t_id, ptr2_free.get(kd % 2)), inc="tensor")
                tevT = sc.add("scalar", lambda s2, pp=pp, wbuf=wbuf, i=i, kd=kd: s2.activation(
                    out=wbuf[i][:, kd * 128:(kd + 1) * 128], in_=pp[:], func=ACT.Copy),
                    waits=(tmmT, wst_free[nm][i]), inc="scalar")
                ptr2_free[kd % 2] = tevT
            wdqo_free[dqi] = tevT
            wtoks[nm] = tevT
        for nm, wbuf in (("up", wstU), ("gate", wstG)):
            ev_t = None
            for c in range(4):
                pp = pug[len(pug_hist) % 2]
                gate = pug_hist[-2] if len(pug_hist) >= 2 else None
                tmm = None
                for kd in range(ND):
                    tmm = sc.add("tensor", lambda te, pp=pp, wbuf=wbuf, i=i, kd=kd, c=c: te.matmul(
                        pp[:], wbuf[i][:, kd * 128:(kd + 1) * 128],
                        xnT[:, kd * S + c * 512: kd * S + (c + 1) * 512],
                        start=(kd == 0), stop=(kd == ND - 1)),
                        waits=(wtoks[nm], t_xq[kd], gate),
                        inc="tensor" if kd == ND - 1 else None)
                if nm == "up":
                    out_ap = upst[i][:, c * 512:(c + 1) * 512]
                    ev_t = sc.add("scalar", lambda s2, pp=pp, out_ap=out_ap: s2.activation(
                        out=out_ap, in_=pp[:], func=ACT.Copy),
                        waits=(tmm, upst_free[i] if c == 0 else None),
                        inc="scalar")
                else:
                    out_ap = BUFN[:, it * S + c * 512: it * S + (c + 1) * 512]
                    ev_t = sc.add("scalar", lambda s2, pp=pp, out_ap=out_ap: s2.activation(
                        out=out_ap, in_=pp[:], func=ACT.Copy), waits=(tmm,), inc="scalar")
                pug_hist.append(ev_t)
            wst_free[nm][i] = ev_t
            if deferred_mx is not None:
                (d_src, d_cslot, d_ev) = deferred_mx
                t_red = sc.add("vector", lambda v, d_src=d_src: v.reduce_max(
                    out=cmx[:, 7:8], in_=d_src, axis=AX, apply_absolute_value=True),
                    waits=(d_ev, t_mx_prev), inc="vector")
                t_mx = sc.add("vector", lambda v, d_cslot=d_cslot: v.tensor_tensor(
                    out=cmx[:, d_cslot:d_cslot + 1], in0=cmx[:, d_cslot:d_cslot + 1],
                    in1=cmx[:, 7:8], op=ALU.max), waits=(t_red, t_cmx0), inc="vector")
                t_mx_prev = t_mx
                if d_cslot == 1:
                    t_upmax = t_mx
                else:
                    t_gatemax = t_mx
            src2 = upst[i][:] if nm == "up" else BUFN[:, it * S:(it + 1) * S]
            deferred_mx = (src2, 1 if nm == "up" else 2, ev_t)
            if nm == "up":
                upst_free[i] = sc.dma("sync", f"upst{i}", up_d[it * 128:(it + 1) * 128, :],
                                      upst[i][:], waits=(ev_t,))
    (d_src, d_cslot, d_ev) = deferred_mx
    t_red = sc.add("vector", lambda v, d_src=d_src: v.reduce_max(
        out=cmx[:, 7:8], in_=d_src, axis=AX, apply_absolute_value=True),
        waits=(d_ev, t_mx_prev), inc="vector")
    t_gatemax = sc.add("vector", lambda v, d_cslot=d_cslot: v.tensor_tensor(
        out=cmx[:, d_cslot:d_cslot + 1], in0=cmx[:, d_cslot:d_cslot + 1],
        in1=cmx[:, 7:8], op=ALU.max), waits=(t_red, t_cmx0), inc="vector")
    t_dq_down_all = (wtst_free[0], wtst_free[0])
    mem.free("wstU0", "wstU1", "wstG0", "wstG1")
    mem.free("wtst0", "rix", "bmx", "wdqoD", "wnatD", "wdqo0", "wdqo1",
             "wnat0", "wnat1", "xnT")

    t_u1 = tree(cmx[:, 1:2], ccs[0:1, 16:17], ALU.max, 128, (t_upmax,))
    t_g1 = tree(cmx[:, 2:3], ccs[0:1, 17:18], ALU.max, 128, (t_gatemax, t_u1))
    t_bc1 = allreduce(1, t_g1)
    t_su = sc.add("vector", lambda v: v.tensor_scalar(out=scal[:, 2:3], in0=ccb[:, 16:17],
                  scalar1=1.0 / QMAX, scalar2=EPS12, op0=ALU.mult, op1=ALU.max),
                  waits=(t_bc1,), inc="vector")
    t_ru = sc.add("vector", lambda v: v.reciprocal(out=scal[:, 3:4], in_=scal[:, 2:3]),
                  waits=(t_su,), inc="vector")
    t_sg = sc.add("vector", lambda v: v.tensor_scalar(out=scal[:, 4:5], in0=ccb[:, 17:18],
                  scalar1=1.0 / QMAX, scalar2=EPS12, op0=ALU.mult, op1=ALU.max),
                  waits=(t_bc1,), inc="vector")
    t_rg = sc.add("vector", lambda v: v.reciprocal(out=scal[:, 5:6], in_=scal[:, 4:5]),
                  waits=(t_sg,), inc="vector")

    wdn = mem.sb("wdn", [128, 2 * NI * 512], BF16)
    t_wdn = None
    for dc in range(2):
        for it2 in range(NI):
            t_wdn = sc.dma(
                "sync", "wdn",
                wdn[:, dc * NI * 512 + it2 * 512: dc * NI * 512 + (it2 + 1) * 512],
                wdnT_d[it2 * 128:(it2 + 1) * 128, dc * 512:(dc + 1) * 512],
                waits=t_dq_down_all)
    iA = [mem.sb(f"iA{i}", [128, S], I32) for i in range(2)]
    iB1 = mem.sb("iB0", [128, S], I32)
    iB = [iB1, iB1]
    fsg1 = mem.sb("fsg0", [128, S], F32)
    fsg = [fsg1, fsg1]

    iA_free = [None, None]
    iB_free = [None, None]
    fsg_free = [None, None]
    t_pmax = None
    for it in range(NI):
        i = it % 2
        buf_sl = BUFN[:, it * S:(it + 1) * S]
        tGo = sc.add("scalar", lambda s2, buf_sl=buf_sl, i=i: s2.activation(
            out=iA[i][:], in_=buf_sl, func=ACT.Copy, scale=scal[:, 5:6]),
            waits=(t_rg, iA_free[i], t_dq_down_all[0], t_dq_down_all[1]), inc="scalar")
        tsg_ = sc.add("scalar", lambda s2, i=i: s2.activation(
            out=fsg[i][:], in_=iA[i][:], func=ACT.Sigmoid, scale=scal[:, 4:5]),
            waits=(tGo, fsg_free[0]), inc="scalar")
        tSo = sc.add("scalar", lambda s2, i=i: s2.activation(
            out=iB[i][:], in_=fsg[i][:], func=ACT.Copy, scale=65536.0),
            waits=(tsg_, iB_free[0]), inc="scalar")
        fsg_free[0] = tSo
        tP = sc.add("vector", lambda v, buf_sl=buf_sl, i=i: v.tensor_tensor(
            out=buf_sl, in0=iA[i][:], in1=iB[i][:], op=ALU.mult),
            waits=(tSo,), inc="vector")
        iA_free[i] = tP
        iB_free[0] = tP
        t_red = sc.add("vector", lambda v, buf_sl=buf_sl: v.reduce_max(
            out=cmx[:, 7:8], in_=buf_sl, axis=AX, apply_absolute_value=True),
            waits=(tP,), inc="vector")
        t_pmax = sc.add("vector", lambda v: v.tensor_tensor(
            out=cmx[:, 3:4], in0=cmx[:, 3:4], in1=cmx[:, 7:8], op=ALU.max),
            waits=(t_red,), inc="vector")

    t_p1 = tree(cmx[:, 3:4], ccs[0:1, 32:33], ALU.max, 128, (t_pmax,))
    t_bc2 = allreduce(2, t_p1)
    t_a = sc.add("vector", lambda v: v.tensor_tensor(out=scal[:, 22:23],
                 in0=ccb[:, 32:33], in1=scal[:, 4:5], op=ALU.mult),
                 waits=(t_bc2,), inc="vector")
    t_b2 = sc.add("vector", lambda v: v.tensor_scalar(out=scal[:, 6:7], in0=scal[:, 22:23],
                  scalar1=1.0 / (65536.0 * QMAX), scalar2=EPS12, op0=ALU.mult,
                  op1=ALU.max), waits=(t_a,), inc="vector")
    t_c = sc.add("vector", lambda v: v.reciprocal(out=scal[:, 23:24], in_=scal[:, 6:7]),
                 waits=(t_b2,), inc="vector")
    t_d2 = sc.add("vector", lambda v: v.tensor_scalar(out=scal[:, 24:25], in0=scal[:, 4:5],
                  scalar1=1.0 / 65536.0, scalar2=None, op0=ALU.mult), inc="vector")
    t_r2 = sc.add("vector", lambda v: v.tensor_tensor(out=scal[:, 7:8], in0=scal[:, 24:25],
                  in1=scal[:, 23:24], op=ALU.mult), waits=(t_c, t_d2), inc="vector")

    t_omax = None
    for it in range(NI):
        i = it % 2
        buf_sl = BUFN[:, it * S:(it + 1) * S]
        t_ul = sc.dma("sync", f"upst{i}", upst[i][:], up_d[it * 128:(it + 1) * 128, :],
                      waits=(upst_free[i],))
        tR2 = sc.add("scalar", lambda s2, buf_sl=buf_sl, i=i: s2.activation(
            out=iA[i][:], in_=buf_sl, func=ACT.Copy, scale=scal[:, 7:8]),
            waits=(t_r2, iA_free[i]), inc="scalar")
        tU = sc.add("scalar", lambda s2, i=i: s2.activation(
            out=iB[i][:], in_=upst[i][:], func=ACT.Copy, scale=scal[:, 3:4]),
            waits=(t_ru, t_ul, iB_free[0]), inc="scalar")
        tO = sc.add("vector", lambda v, buf_sl=buf_sl, i=i: v.tensor_tensor(
            out=buf_sl, in0=iA[i][:], in1=iB[i][:], op=ALU.mult),
            waits=(tR2, tU), inc="vector")
        iA_free[i] = tO
        iB_free[0] = tO
        upst_free[i] = tO
        t_red = sc.add("vector", lambda v, buf_sl=buf_sl: v.reduce_max(
            out=cmx[:, 7:8], in_=buf_sl, axis=AX, apply_absolute_value=True),
            waits=(tO,), inc="vector")
        t_omax = sc.add("vector", lambda v: v.tensor_tensor(
            out=cmx[:, 4:5], in0=cmx[:, 4:5], in1=cmx[:, 7:8], op=ALU.max),
            waits=(t_red,), inc="vector")

    t_o1t = tree(cmx[:, 4:5], ccs[0:1, 48:49], ALU.max, 128, (t_omax,))
    t_bc3 = allreduce(3, t_o1t)
    t_e = sc.add("vector", lambda v: v.tensor_tensor(out=scal[:, 25:26], in0=scal[:, 6:7],
                 in1=scal[:, 2:3], op=ALU.mult), waits=(t_b2, t_su), inc="vector")
    t_f = sc.add("vector", lambda v: v.tensor_tensor(out=scal[:, 22:23], in0=ccb[:, 48:49],
                 in1=scal[:, 25:26], op=ALU.mult), waits=(t_bc3, t_e), inc="vector")
    t_g = sc.add("vector", lambda v: v.tensor_scalar(out=scal[:, 26:27], in0=scal[:, 22:23],
                 scalar1=1.0 / QMAX, scalar2=EPS12, op0=ALU.mult, op1=ALU.max),
                 waits=(t_f,), inc="vector")
    t_h = sc.add("vector", lambda v: v.reciprocal(out=scal[:, 27:28], in_=scal[:, 26:27]),
                 waits=(t_g,), inc="vector")
    t_r3 = sc.add("vector", lambda v: v.tensor_tensor(out=scal[:, 28:29], in0=scal[:, 25:26],
                  in1=scal[:, 27:28], op=ALU.mult), waits=(t_h,), inc="vector")

    t_oq = {}
    for it in range(NI):
        i = it % 2
        buf_sl = BUFN[:, it * S:(it + 1) * S]
        tq1 = sc.add("scalar", lambda s2, buf_sl=buf_sl, i=i: s2.activation(
            out=iA[i][:], in_=buf_sl, func=ACT.Copy, scale=scal[:, 28:29]),
            waits=(t_r3, iA_free[i]), inc="scalar")
        t_oq[it] = sc.add("vector", lambda v, buf_sl=buf_sl, i=i: v.tensor_copy(
            out=buf_sl, in_=iA[i][:]), waits=(tq1,), inc="vector")
        iA_free[i] = t_oq[it]

    mem.free("iA0", "iA1", "iB0", "fsg0", "pug0", "pug1")

    jw6 = (sc.now("vector"), sc.now("scalar"))
    xrl = [mem.sb(f"xrl{i}", [128, D], F32) for i in range(2)]
    arl = [mem.sb(f"arl{i}", [128, D], F32) for i in range(2)]
    ost = [mem.sb(f"ost{i}", [128, D], F32) for i in range(2)]
    pdn = [mem.ps(f"pdn{i}", [128, 512]) for i in range(2)]

    xrl_free = [None, None]
    arl_free = [None, None]
    ost_free = [None, None]
    pdn_hist = []
    out_fin = [None, None]
    for tt in range(NT):
        i = tt % 2
        t_xl = sc.dma("sync", f"xrl{i}", xrl[i][:], x_d[tt * 128:(tt + 1) * 128, :],
                      waits=(xrl_free[i],) + jw6)
        t_al = sc.dma("sync", f"arl{i}", arl[i][:], attn_d[tt * 128:(tt + 1) * 128, :],
                      waits=(arl_free[i], wo_sp_fin[0], wo_sp_fin[1]) + jw6)
        for dc in range(2):
            pp = pdn[len(pdn_hist) % 2]
            gate = pdn_hist[-2] if len(pdn_hist) >= 2 else None
            tmm = None
            for it in range(NI):
                tmm = sc.add("tensor", lambda te, pp=pp, it=it, tt=tt, dc=dc: te.matmul(
                    pp[:], BUFN[:, it * S + tt * 128: it * S + (tt + 1) * 128],
                    wdn[:, dc * NI * 512 + it * 512: dc * NI * 512 + (it + 1) * 512],
                    start=(it == 0), stop=(it == NI - 1)),
                    waits=(t_oq[it], t_wdn, gate),
                    inc="tensor" if it == NI - 1 else None)
            t_ev = sc.add("scalar", lambda s2, pp=pp, i=i, dc=dc: s2.activation(
                out=ost[i][:, dc * 512:(dc + 1) * 512], in_=pp[:], func=ACT.Copy,
                scale=scal[:, 26:27]),
                waits=(tmm, t_g, ost_free[i] if dc == 0 else None), inc="scalar")
            pdn_hist.append(t_ev)
            t_a1 = sc.add("vector", lambda v, i=i, dc=dc: v.tensor_tensor(
                out=ost[i][:, dc * 512:(dc + 1) * 512],
                in0=ost[i][:, dc * 512:(dc + 1) * 512],
                in1=xrl[i][:, dc * 512:(dc + 1) * 512], op=ALU.add),
                waits=(t_ev, t_xl), inc="vector")
            t_a2 = sc.add("vector", lambda v, i=i, dc=dc: v.tensor_tensor(
                out=ost[i][:, dc * 512:(dc + 1) * 512],
                in0=ost[i][:, dc * 512:(dc + 1) * 512],
                in1=arl[i][:, dc * 512:(dc + 1) * 512], op=ALU.add),
                waits=(t_a1, t_al), inc="vector")
        t_st = sc.dma("sync", f"ost{i}", out_d[tt * 128:(tt + 1) * 128, :], ost[i][:],
                      waits=(t_a2,))
        xrl_free[i] = t_st
        arl_free[i] = t_st
        ost_free[i] = t_st
        out_fin[i] = t_st

    sc.add("gpsimd", fn=None, waits=(out_fin[0], out_fin[1]))

    with nc.Block() as block:
        sc.emit(block)
    es.close()
    return nc


_NC_CACHE = [None]


def kernel(L0_attn_x, L0_attn_W_K, L0_attn_W_Q, L0_attn_W_V, L0_attn_W_O,
           L0_ffn_w_gate, L0_ffn_w_up, L0_ffn_w_down):
    f = np.float32
    x = np.ascontiguousarray(np.asarray(L0_attn_x, f))
    wq = np.ascontiguousarray(np.asarray(L0_attn_W_Q, f).transpose(2, 0, 1).reshape(D, D))
    wk = np.ascontiguousarray(np.asarray(L0_attn_W_K, f).transpose(2, 0, 1).reshape(D, D))
    wv = np.ascontiguousarray(np.asarray(L0_attn_W_V, f).transpose(2, 0, 1).reshape(D, D))
    wo = np.ascontiguousarray(np.asarray(L0_attn_W_O, f).T)
    wupn = np.ascontiguousarray(np.asarray(L0_ffn_w_up, f))
    wgn = np.ascontiguousarray(np.asarray(L0_ffn_w_gate, f))
    wdn_ = np.ascontiguousarray(np.asarray(L0_ffn_w_down, f))
    ident = np.eye(128, dtype=BF)
    identf = np.eye(128, dtype=f)
    kk = np.arange(128)[:, None]
    dmsk = np.concatenate(
        [((d * 128 + kk) <= np.arange(512)[None, :]) for d in range(4)],
        axis=1).astype(BF)
    ones1 = np.ones((1, 128), dtype=BF)
    ones1f = np.ones((1, 128), dtype=f)
    onesc = np.ones((128, 1), dtype=BF)

    if _NC_CACHE[0] is None:
        _NC_CACHE[0] = build_nc()
    nc = _NC_CACHE[0]
    in_maps = []
    for c in range(NCORES):
        in_maps.append({"x": np.ascontiguousarray(x[c]), "wqT": wq, "wkT": wk,
                        "wvT": wv, "woT": wo, "wup": wupn, "wgate": wgn,
                        "wdown": wdn_, "ident": ident, "dmask": dmsk,
                        "ones1": ones1, "ones1f": ones1f, "onesc": onesc, "identf": identf})
    res = run_bass_kernel_spmd(nc, in_maps, core_ids=list(range(NCORES)))
    return np.stack([res.results[c]["out"] for c in range(NCORES)], axis=0)


# version 15
# speedup vs baseline: 1.1049x; 1.0175x over previous
import numpy as np
import ml_dtypes
import concourse.bass as bass
import concourse.mybir as mybir
from concourse.bass_utils import run_bass_kernel_spmd

F32 = mybir.dt.float32
BF16 = mybir.dt.bfloat16
I32 = mybir.dt.int32
AX = mybir.AxisListType.X
ALU = mybir.AluOpType
ACT = mybir.ActivationFunctionType
BF = ml_dtypes.bfloat16

S, D, H, DH, I3 = 2048, 1024, 16, 64, 3072
NT, ND, NI = S // 128, D // 128, I3 // 128
NCORES = 8
QMAX = 32767.0
EPS12 = 1e-12

ENGS = ("tensor", "vector", "scalar", "gpsimd", "sync")


class Sched:

    def __init__(self, nc, es):
        self.nc = nc
        self.es = es
        self.steps = {e: [] for e in ENGS}
        names = list(ENGS) + ["cc"]
        self.sems = {n: es.enter_context(nc.semaphore("sem_" + n)) for n in names}
        self.cnt = {n: 0 for n in names}
        self.wm = {}

    def now(self, name):
        return (name, self.cnt[name])

    def stream(self, name):
        if name not in self.sems:
            self.sems[name] = self.es.enter_context(self.nc.semaphore("sem_" + name))
            self.cnt[name] = 0
        return name

    def add(self, eng, fn=None, waits=(), inc=None, n=1):
        todo = []
        for t in waits:
            if t is None:
                continue
            s, v = t
            key = (eng, s)
            if self.wm.get(key, 0) < v:
                self.wm[key] = v
                todo.append((self.sems[s], v))
        tok, sem = None, None
        if inc is not None:
            self.cnt[inc] += n
            tok = (inc, self.cnt[inc])
            sem = self.sems[inc]
        amt = n

        def thunk(obj, todo=todo, fn=fn, sem=sem, amt=amt):
            for (sm, v) in todo:
                obj.wait_ge(sm, v)
            ins = fn(obj) if fn is not None else None
            if sem is not None:
                ins.then_inc(sem, amt)

        self.steps[eng].append(thunk)
        return tok

    def dma(self, qeng, stream, out, in_, waits=()):
        return self.add(qeng, lambda o, out=out, in_=in_: o.dma_start(out=out, in_=in_),
                        waits=waits, inc=self.stream(stream), n=16)

    def dmat(self, stream, out, in_, waits=()):
        return self.add("sync",
                        lambda o, out=out, in_=in_: o.dma_start(out=out, in_=in_, transpose=True),
                        waits=waits, inc=self.stream(stream), n=16)

    def emit(self, block):
        for e in ENGS:
            thunks = self.steps[e]

            def body(obj, thunks=thunks):
                for t in thunks:
                    t(obj)
            getattr(block, e)(body)


class Mem:
    def __init__(self, nc):
        self.nc = nc
        self.guards = {}
        self.order = {}
        self.n = 0

    def _reg(self, name, g):
        t = g.__enter__()
        self.guards[name] = g
        self.order[name] = self.n
        self.n += 1
        return t

    def sb(self, name, shape, dt, side=None):
        return self._reg(name, self.nc.sbuf_tensor(name, shape, dt, side=side))

    def ps(self, name, shape, dt=F32):
        return self._reg(name, self.nc.psum_tensor(name, shape, dt))

    def free(self, *names):
        for n in sorted(names, key=lambda n: self.order[n], reverse=True):
            self.guards.pop(n).__exit__(None, None, None)


def build_nc():
    nc = bass.Bass("TRN2", target_bir_lowering=False)

    x_d = nc.dram_tensor("x", [S, D], F32, kind="ExternalInput")
    wT_d = {n: nc.dram_tensor("w" + n + "T", [D, D], F32, kind="ExternalInput")
            for n in ("q", "k", "v", "o")}
    wup_d = nc.dram_tensor("wup", [I3, D], F32, kind="ExternalInput")
    wgate_d = nc.dram_tensor("wgate", [I3, D], F32, kind="ExternalInput")
    wdown_d = nc.dram_tensor("wdown", [D, I3], F32, kind="ExternalInput")
    ident_d = nc.dram_tensor("ident", [128, 128], BF16, kind="ExternalInput")
    identf_d = nc.dram_tensor("identf", [128, 128], F32, kind="ExternalInput")
    dmask_d = nc.dram_tensor("dmask", [128, 2048], BF16, kind="ExternalInput")
    ones1_d = nc.dram_tensor("ones1", [1, 128], BF16, kind="ExternalInput")
    ones1f_d = nc.dram_tensor("ones1f", [1, 128], F32, kind="ExternalInput")
    onesc_d = nc.dram_tensor("onesc", [128, 1], BF16, kind="ExternalInput")
    out_d = nc.dram_tensor("out", [S, D], F32, kind="ExternalOutput")
    cc_in = [nc.dram_tensor(f"cc_in{r}", [1, 16], F32) for r in range(4)]
    cc_out = [nc.dram_tensor(f"cc_out{r}", [1, 16], F32) for r in range(4)]
    wdnT_d = nc.dram_tensor("wdnT", [I3, D], BF16)
    up_d = nc.dram_tensor("upsp", [I3, S], BF16)
    attn_d = nc.dram_tensor("attnsp", [S, D], F32)
    den_d = [nc.dram_tensor(f"densp{i}", [1, 512], F32) for i in range(4)]

    from contextlib import ExitStack
    es = ExitStack()
    sc = Sched(nc, es)
    mem = Mem(nc)

    ident = mem.sb("s_ident", [128, 128], BF16)
    identf = mem.sb("s_identf", [128, 128], F32)
    colws = mem.sb("s_colws", [128, 128], F32)
    ptre = mem.ps("s_ptre", [128, 512], F32)
    dmask = mem.sb("s_dmask", [128, 2048], BF16)
    ones1 = mem.sb("s_ones1", [1, 128], BF16)
    ones1f = mem.sb("s_ones1f", [1, 128], F32)
    onesc = mem.sb("s_onesc", [128, 1], BF16)
    trs = mem.sb("s_trs", [32, 32], F32)
    trs2 = mem.sb("s_trs2", [32, 32], F32)
    scal = mem.sb("scal", [128, 64], F32)
    ccs = mem.sb("ccs", [1, 64], F32)
    ccb = mem.sb("ccb", [128, 64], F32)
    cmx = mem.sb("cmx", [128, 8], F32)

    sc.dma("sync", "const", ident[:], ident_d[:, :])
    sc.dma("sync", "const", identf[:], identf_d[:, :])
    sc.dma("sync", "const", dmask[:], dmask_d[:, :])
    sc.dma("sync", "const", ones1[:], ones1_d[:, :])
    sc.dma("sync", "const", ones1f[:], ones1f_d[:, :])
    t_const = sc.dma("sync", "const", onesc[:], onesc_d[:, :])
    t_id = t_dm = t_o1 = t_o1f = t_const
    t_cmx0 = sc.add("vector", lambda v: v.memset(cmx[:], 0.0), inc="vector")
    t_trs0 = sc.add("vector", lambda v: v.memset(trs[:], 0.0), inc="vector")
    t_ccs0 = sc.add("vector", lambda v: v.memset(ccs[:], 0.0), inc="vector")
    t_cw0 = sc.add("vector", lambda v: v.memset(colws[:], 0.0), inc="vector")

    tree_busy = [None]

    def tree(col_ap, out_ap, op, p0, waits):
        t1 = sc.add("vector", lambda v, col_ap=col_ap, p0=p0: v.tensor_copy(
            out=colws[0:p0, 0:1], in_=col_ap),
            waits=tuple(waits) + (tree_busy[0], t_cw0), inc="vector")
        t2 = sc.add("tensor", lambda te: te.transpose(ptre[:, 0:128], colws[:], identf[:]),
                    waits=(t1, t_id, tree_busy[0]), inc="tensor")
        tok = sc.add("vector", lambda v, out_ap=out_ap, op=op, p0=p0: v.tensor_reduce(
            out=out_ap, in_=ptre[0:1, 0:p0], axis=AX, op=op), waits=(t2,), inc="vector")
        tree_busy[0] = tok
        return tok

    def allreduce(r, t_staged):
        t_w = sc.dma("gpsimd", "gcc", cc_in[r][:, :], ccs[0:1, 16 * r:16 * r + 16],
                     waits=(t_staged, t_ccs0))
        t_ar = sc.add("gpsimd", lambda g, r=r: g.collective_compute(
            "AllReduce", ALU.max, replica_groups=[list(range(NCORES))],
            ins=[cc_in[r].ap().opt()], outs=[cc_out[r].ap().opt()]),
            waits=(t_w,), inc="cc")
        return sc.dma("sync", "ccb", ccb[:, 16 * r:16 * r + 16],
                      cc_out[r].ap().partition_broadcast(128), waits=(t_ar,))

    xnT = mem.sb("xnT", [128, ND * S], BF16, side="right")
    xst = [mem.sb(f"xst{i}", [128, D], F32) for i in range(2)]
    xnst = [mem.sb(f"xnst{i}", [128, D], BF16) for i in range(2)]
    lns = mem.sb("lns", [128, 64], F32)
    sqd = mem.sb("sqd", [128, D], F32)
    ptr = [mem.ps(f"ptr{i}", [128, 128], BF16) for i in range(2)]

    xst_free = [None, None]
    xnst_free = [None, None]
    ptr_free = {}
    t_xn_done = None
    t_xnmax = None
    t_sums = {}
    for tt in range(NT):
        i = tt % 2
        t_ld = sc.dma("sync", f"xst{i}", xst[i][:], x_d[tt * 128:(tt + 1) * 128, :],
                      waits=(xst_free[i],))
        x_ap = xst[i][:]
        t1 = sc.add("vector", lambda v, x_ap=x_ap, tt=tt: v.reduce_sum(
            out=lns[:, tt:tt + 1], in_=x_ap, axis=AX), waits=(t_ld,), inc="vector")
        t2 = sc.add("vector", lambda v, x_ap=x_ap, tt=tt: v.scalar_tensor_tensor(
            out=sqd[:], in0=x_ap, scalar=1.0, in1=x_ap, op0=ALU.mult, op1=ALU.mult,
            accum_out=lns[:, 16 + tt:17 + tt]), waits=(t1,), inc="vector")
        t_sums[tt] = t2
        xst_free[tt % 2] = t2
    tb = sc.add("vector", lambda v: v.tensor_scalar(
        out=lns[:, 0:16], in0=lns[:, 0:16], scalar1=1.0 / D, scalar2=None,
        op0=ALU.mult), waits=(t_sums[NT - 1],), inc="vector")
    tb = sc.add("vector", lambda v: v.tensor_scalar(
        out=lns[:, 16:32], in0=lns[:, 16:32], scalar1=1.0 / D, scalar2=None,
        op0=ALU.mult), waits=(tb,), inc="vector")
    tb = sc.add("vector", lambda v: v.scalar_tensor_tensor(
        out=lns[:, 32:48], in0=lns[:, 0:16], scalar=1.0, in1=lns[:, 0:16],
        op0=ALU.mult, op1=ALU.mult), waits=(tb,), inc="vector")
    tb = sc.add("vector", lambda v: v.tensor_tensor(
        out=lns[:, 32:48], in0=lns[:, 16:32], in1=lns[:, 32:48],
        op=ALU.subtract), waits=(tb,), inc="vector")
    tb = sc.add("vector", lambda v: v.tensor_scalar(
        out=lns[:, 32:48], in0=lns[:, 32:48], scalar1=1e-5, scalar2=None,
        op0=ALU.add), waits=(tb,), inc="vector")
    tb = sc.add("scalar", lambda s: s.activation(
        out=lns[:, 48:64], in_=lns[:, 32:48], func=ACT.Sqrt), waits=(tb,), inc="scalar")
    tb = sc.add("vector", lambda v: v.reciprocal(
        out=lns[:, 32:48], in_=lns[:, 48:64]), waits=(tb,), inc="vector")
    t_stats = sc.add("vector", lambda v: v.scalar_tensor_tensor(
        out=lns[:, 48:64], in0=lns[:, 0:16], scalar=-1.0, in1=lns[:, 32:48],
        op0=ALU.mult, op1=ALU.mult), waits=(tb,), inc="vector")
    for tt in range(NT):
        i = tt % 2
        t_ld = sc.dma("sync", f"xst{i}", xst[i][:], x_d[tt * 128:(tt + 1) * 128, :],
                      waits=(xst_free[i],))
        x_ap = xst[i][:]
        n_ap = xnst[i][:]
        t7 = sc.add("vector", lambda v, x_ap=x_ap, n_ap=n_ap, tt=tt: v.tensor_scalar(
            out=n_ap, in0=x_ap, scalar1=lns[:, 32 + tt:33 + tt],
            scalar2=lns[:, 48 + tt:49 + tt], op0=ALU.mult, op1=ALU.add),
            waits=(t_ld, t_stats, xnst_free[i]), inc="vector")
        xst_free[i] = t7
        t8 = sc.add("vector", lambda v, n_ap=n_ap: v.reduce_max(
            out=lns[:, 0:1], in_=n_ap, axis=AX, apply_absolute_value=True),
            waits=(t7, t_cmx0), inc="vector")
        t_xnmax = sc.add("vector", lambda v: v.tensor_tensor(
            out=cmx[:, 0:1], in0=cmx[:, 0:1], in1=lns[:, 0:1], op=ALU.max),
            waits=(t8,), inc="vector")
        last_tr = None
        for dt in range(ND):
            pp = ptr[dt % 2]
            tmm = sc.add("tensor", lambda te, n_ap=n_ap, dt=dt, pp=pp: te.transpose(
                pp[:], n_ap[:, dt * 128:(dt + 1) * 128], ident[:]),
                waits=(t7, t_id, ptr_free.get(dt % 2)), inc="tensor")
            tev = sc.add("vector", lambda v, pp=pp, dt=dt, tt=tt: v.tensor_copy(
                out=xnT[:, dt * S + tt * 128: dt * S + (tt + 1) * 128], in_=pp[:]),
                waits=(tmm,), inc="vector")
            ptr_free[dt % 2] = tev
            last_tr = tmm
        xnst_free[i] = last_tr
        t_xn_done = tev
    mem.free("xst0", "xst1", "xnst0", "xnst1", "lns", "sqd", "ptr0", "ptr1")

    t_st0 = tree(cmx[:, 0:1], ccs[0:1, 0:1], ALU.max, 128, (t_xnmax,))
    t_bc0 = allreduce(0, t_st0)
    t_sx = sc.add("vector", lambda v: v.tensor_scalar(out=scal[:, 0:1], in0=ccb[:, 0:1],
                  scalar1=1.0 / QMAX, scalar2=EPS12, op0=ALU.mult, op1=ALU.max),
                  waits=(t_bc0,), inc="vector")
    t_rx = sc.add("vector", lambda v: v.reciprocal(out=scal[:, 1:2], in_=scal[:, 0:1]),
                  waits=(t_sx,), inc="vector")

    jw1 = (sc.now("vector"), sc.now("scalar"), sc.now("tensor"))
    wsb = {n: mem.sb("w" + n, [128, ND * D], BF16, side="right") for n in ("q", "k", "v")}
    t_wld = {}
    for n in ("q", "k", "v"):
        tk = None
        for kd in range(ND):
            tk = sc.dma("gpsimd", "wcast", wsb[n][:, kd * D:(kd + 1) * D],
                        wT_d[n][kd * 128:(kd + 1) * 128, :], waits=jw1)
        t_wld[n] = tk
    for n in ("q", "k", "v"):
        t_wld[n] = tk

    kT2 = mem.sb("kT2", [128, S], BF16)
    qT2 = mem.sb("qT2", [128, S], BF16)
    vx = [mem.sb(f"vx{i}", [128, NT * 65], BF16) for i in range(2)]
    nsc = mem.sb("nsc", [128, S], BF16)
    expw = [mem.sb(f"expw{i}", [128, NT * 512], BF16) for i in range(2)]
    zT2 = mem.sb("zT2", [128, ND * S], BF16)
    zTo = mem.sb("zTo", [64, S], BF16)
    zsc = [mem.sb(f"zsc{i}", [65, 512], F32) for i in range(4)]
    denbc = [mem.sb(f"denbc{i}", [64, 512], F32) for i in range(4)]
    denbr = [mem.sb(f"denbr{i}", [64, 512], F32) for i in range(4)]
    ps_ = [mem.ps(f"ps{i}", [128, 1024]) for i in range(2)]
    pzx = mem.ps("pzx", [128, 1024])
    pzb = [pzx[0:65, 0:512], ptre[0:65, 0:512]]
    pnr = pzx[0:1, 512:1024]
    pmx = mem.ps("pmx", [128, 512])
    prb = pmx[0:64, 0:512]

    t_vones = [sc.add("vector", lambda v, i=i: v.memset(
        vx[i][:].rearrange("p (t c) -> p t c", c=65)[:, :, 64:65], 1.0), inc="vector")
        for i in range(2)]

    qkv_hist = []
    vx_free = [None, None]
    prev_exp = None
    pz_free = {0: None, 1: None}
    prb_free = None
    zn_prev = [None, None, None, None]
    t_qkv_last = None
    t_attn_last_te = None
    pmx_free = None
    hp_state = {}
    zto_free = None
    t_zrepack = {}

    def qkv_slot(idx):
        return pzx[0:128, 0:512] if idx % 2 == 0 else pmx[0:128, 0:512]

    for hp in range(H // 2):
        for wnm, dst in (("k", kT2), ("q", qT2)):
            for c in range(4):
                pp = qkv_slot(len(qkv_hist))
                gate = qkv_hist[-2] if len(qkv_hist) >= 2 else None
                tmm = None
                for kd in range(ND):
                    tmm = sc.add("tensor", lambda te, pp=pp, wnm=wnm, kd=kd, hp=hp, c=c: te.matmul(
                        pp, wsb[wnm][:, kd * D + hp * 128: kd * D + hp * 128 + 128],
                        xnT[:, kd * S + c * 512: kd * S + (c + 1) * 512],
                        start=(kd == 0), stop=(kd == ND - 1)),
                        waits=(t_xn_done, t_wld[wnm], gate,
                               pz_free[0] if c == 0 and wnm == "k" else None,
                               pz_free[1] if c == 0 and wnm == "k" else None,
                               pmx_free if c == 0 and wnm == "k" else None,
                               hp_state.get("score_last") if c == 0 and wnm == "k" else None),
                        inc="tensor" if kd == ND - 1 else None)
                tev = sc.add("vector", lambda v, pp=pp, c=c, dst=dst: v.tensor_copy(
                    out=dst[:, c * 512:(c + 1) * 512], in_=pp),
                    waits=(tmm,), inc="vector")
                qkv_hist.append(tev)
        for tt in range(NT):
            pp = qkv_slot(len(qkv_hist))
            gate = qkv_hist[-2] if len(qkv_hist) >= 2 else None
            tmm = None
            for kd in range(ND):
                tmm = sc.add("tensor", lambda te, pp=pp, kd=kd, tt=tt, hp=hp: te.matmul(
                    pp[:, 0:128], xnT[:, kd * S + tt * 128: kd * S + (tt + 1) * 128],
                    wsb["v"][:, kd * D + hp * 128: kd * D + hp * 128 + 128],
                    start=(kd == 0), stop=(kd == ND - 1)),
                    waits=(t_xn_done, t_wld["v"], gate,
                           vx_free[0] if tt == 0 else None,
                           vx_free[1] if tt == 0 else None),
                    inc="tensor" if kd == ND - 1 else None)
            tev = None
            for half in range(2):
                tev = sc.add("vector", lambda v, pp=pp, half=half, tt=tt: v.tensor_copy(
                    out=vx[half][:, tt * 65: tt * 65 + 64],
                    in_=pp[:, 64 * half:64 * half + 64]),
                    waits=(tmm, t_vones[half]), inc="vector")
            qkv_hist.append(tev)
        t_qkv_last = tmm

        bias_tok = {}
        pmx_busy = hp_state.get("pmx_busy")
        nsc_reader = hp_state.get("nsc_busy")
        nrm_prev = hp_state.get("nrm_prev")
        nrm_tok = {}
        for j, wch in enumerate(("q", "k")):
            srcT = qT2 if wch == "q" else kT2
            tsq = sc.add("vector", lambda v, srcT=srcT: v.tensor_tensor(
                out=nsc[:], in0=srcT[:], in1=srcT[:], op=ALU.mult),
                waits=(qkv_hist[-1], prev_exp, nsc_reader), inc="vector")
            for half in range(2):
                sl = 8 + 2 * half + j
                tred = None
                for c in range(4):
                    tm = sc.add("tensor", lambda te, c=c, half=half: te.matmul(
                        pnr, onesc[64 * half:64 * half + 64, :],
                        nsc[64 * half:64 * half + 64, c * 512:(c + 1) * 512],
                        start=True, stop=True),
                        waits=(tsq, t_o1f, tred, pmx_busy, hp_state.get("pnr_busy")),
                        inc="tensor")
                    tred = sc.add("vector", lambda v, c=c: v.reduce_max(
                        out=scal[0:1, 30 + c:31 + c], in_=pnr, axis=AX),
                        waits=(tm, nrm_prev), inc="vector")
                nsc_reader = tm
                hp_state["pnr_busy"] = tred
                tok = sc.add("vector", lambda v: v.tensor_tensor(
                    out=scal[0:1, 30:31], in0=scal[0:1, 30:31], in1=scal[0:1, 31:32],
                    op=ALU.max), waits=(tred,), inc="vector")
                tok = sc.add("vector", lambda v: v.tensor_tensor(
                    out=scal[0:1, 32:33], in0=scal[0:1, 32:33], in1=scal[0:1, 33:34],
                    op=ALU.max), waits=(tok,), inc="vector")
                nrm_tok[(wch, half)] = nrm_prev = sc.add(
                    "vector", lambda v, sl=sl: v.tensor_tensor(
                        out=scal[0:1, sl:sl + 1], in0=scal[0:1, 30:31],
                        in1=scal[0:1, 32:33], op=ALU.max), waits=(tok,), inc="vector")
        hp_state["nsc_busy"] = nsc_reader
        hp_state["nrm_prev"] = nrm_prev
        for half in range(2):
            toks = []
            for j, wch in enumerate(("q", "k")):
                sl = 8 + 2 * half + j
                tok = sc.add("tensor", lambda te, sl=sl, half=half, j=j: te.matmul(
                    pmx[:, 2 + 2 * half + j:3 + 2 * half + j], ones1f[:],
                    scal[0:1, sl:sl + 1], start=True, stop=True),
                    waits=(nrm_tok[(wch, half)], t_o1f, prb_free, pmx_busy,
                           qkv_hist[-1]), inc="tensor")
                toks.append(tok)
            tok = sc.add("vector", lambda v, half=half: v.tensor_copy(
                out=scal[:, 34:35], in_=pmx[:, 2 + 2 * half:3 + 2 * half]),
                waits=tuple(toks), inc="vector")
            tok = sc.add("vector", lambda v, half=half: v.tensor_tensor(
                out=scal[:, 16 + half:17 + half], in0=scal[:, 34:35],
                in1=pmx[:, 3 + 2 * half:4 + 2 * half], op=ALU.mult),
                waits=(tok,), inc="vector")
            pmx_busy = tok
            pmx_free = tok
            tok = sc.add("scalar", lambda s2, half=half: s2.activation(
                out=scal[:, 18 + half:19 + half], in_=scal[:, 16 + half:17 + half],
                func=ACT.Sqrt), waits=(tok,), inc="scalar")
            bias_tok[half] = sc.add("vector", lambda v, half=half: v.tensor_scalar(
                out=scal[:, 20 + half:21 + half], in0=scal[:, 18 + half:19 + half],
                scalar1=-0.125, scalar2=None, op0=ALU.mult), waits=(tok,), inc="vector")
        hp_state["pmx_busy"] = pmx_busy

        units = [(qb, half) for qb in range(4) for half in range(2)]
        pending = None

        def emit_av_chunk(st):
            nonlocal t_attn_last_te
            p_hp, p_qb, p_half, p_nkt, p_gates, pos = st[:6]
            b = st[7]
            end = min(pos + 2, p_nkt)
            for kt in range(pos, end):
                tav = sc.add("tensor", lambda te, kt=kt, p_half=p_half, p_nkt=p_nkt, b=b: te.matmul(
                    pzb[b], vx[p_half][:, kt * 65:(kt + 1) * 65],
                    expw[p_half][:, kt * 512:(kt + 1) * 512],
                    start=(kt == 0), stop=(kt == p_nkt - 1)),
                    waits=(p_gates[kt], pz_free[b] if kt == 0 else None),
                    inc="tensor" if kt == p_nkt - 1 else None)
                if kt == p_nkt - 1:
                    t_attn_last_te = tav
                    hp_state["av_last_%d" % p_half] = tav
                    st[6] = tav
            st[5] = end
            return end < p_nkt

        def finish_av(st):
            nonlocal prb_free, zto_free
            while emit_av_chunk(st):
                pass
            p_hp, p_qb, p_half, p_nkt, p_gates, pos, tav, b = st
            ui = hp_state.get("ui", 0)
            hp_state["ui"] = ui + 1
            w = ui % 4
            t_ev = sc.add("vector", lambda v, w=w, b=b: v.tensor_copy(
                out=zsc[w][:], in_=pzb[b]), waits=(tav, zn_prev[w]), inc="vector")
            pz_free[b] = t_ev
            t_dw = sc.dma("gpsimd", "gdwr%d" % w, den_d[w][:, :], zsc[w][64:65, :],
                          waits=(t_ev,))
            t_db = sc.dma("gpsimd", "gdbc%d" % w, denbc[w][:],
                          den_d[w].ap().partition_broadcast(64), waits=(t_dw,))
            t_rc = sc.add("vector", lambda v, w=w: v.reciprocal(
                out=denbr[w][:], in_=denbc[w][:]), waits=(t_db, zn_prev[w]),
                inc="vector")
            if p_half == 0:
                out_ap = zT2[0:64, p_hp * S + p_qb * 512: p_hp * S + (p_qb + 1) * 512]
                t_zn = sc.add("vector", lambda v, out_ap=out_ap, w=w: v.tensor_tensor(
                    out=out_ap, in0=zsc[w][0:64, :], in1=denbr[w][:], op=ALU.mult),
                    waits=(t_rc,), inc="vector")
            else:
                out_ap = zTo[:, p_qb * 512:(p_qb + 1) * 512]
                t_zn = sc.add("vector", lambda v, out_ap=out_ap, w=w: v.tensor_tensor(
                    out=out_ap, in0=zsc[w][0:64, :], in1=denbr[w][:], op=ALU.mult),
                    waits=(t_rc, zto_free), inc="vector")
                if p_qb == 3:
                    zto_free = sc.dma("gpsimd", "gzto",
                                      zT2[64:128, p_hp * S:(p_hp + 1) * S], zTo[:],
                                      waits=(t_zn,))
                    t_zrepack[p_hp] = zto_free
            zn_prev[w] = t_zn
            prb_free = t_zn
            return t_zn

        for (qb, half) in units:
            nkt = 4 * qb + 4
            bias_ap = scal[:, 20 + half:21 + half]
            p0, p1_ = 64 * half, 64 * half + 64
            exp_toks = {}
            gates = {}
            for g in range(nkt // 2):
                tmm2 = None
                for kt in (2 * g, 2 * g + 1):
                    slot = kt % 2
                    tmm2 = sc.add("tensor", lambda te, kt=kt, qb=qb, p0=p0, p1_=p1_, half=half, slot=slot: te.matmul(
                        ps_[half][:, slot * 512:(slot + 1) * 512],
                        kT2[p0:p1_, kt * 128:(kt + 1) * 128],
                        qT2[p0:p1_, qb * 512:(qb + 1) * 512], start=True, stop=True),
                        waits=(exp_toks.get(g - 1), qkv_hist[-1],
                               hp_state.get("av_last_%d" % half) if g == 0 else None),
                        inc="tensor" if kt % 2 == 1 else None)
                if pending is not None:
                    emit_av_chunk(pending)
                texp = sc.add("scalar", lambda s2, g=g, half=half, bias_ap=bias_ap: s2.activation(
                    out=expw[half][:, g * 1024:(g + 1) * 1024], in_=ps_[half][:, 0:1024],
                    func=ACT.Exp, scale=0.125, bias=bias_ap),
                    waits=(tmm2, bias_tok[half]), inc="scalar")
                exp_toks[g] = texp
                for kt in (2 * g, 2 * g + 1):
                    d = kt - 4 * qb
                    if d >= 0:
                        gates[kt] = sc.add("gpsimd", lambda g2, half=half, kt=kt, d=d: g2.tensor_tensor(
                            out=expw[half][:, kt * 512:(kt + 1) * 512],
                            in0=expw[half][:, kt * 512:(kt + 1) * 512],
                            in1=dmask[:, d * 512:(d + 1) * 512], op=ALU.mult),
                            waits=(texp, t_dm), inc="gpsimd")
                    else:
                        gates[kt] = texp
            prev_exp = exp_toks[nkt // 2 - 1]
            hp_state["score_last"] = tmm2
            if pending is not None:
                finish_av(pending)
            pending = [hp, qb, half, nkt, gates, 0, None,
                       (4 * hp + qb * 2 + half) % 2]
        t_zn_last = finish_av(pending)
        vx_free = [t_attn_last_te, t_attn_last_te]
    t_attn_z = t_zn_last

    mem.free("wq", "wk", "wv")
    wo2 = mem.sb("wo2", [128, ND * D], BF16, side="right")
    tk = None
    for kd in range(ND):
        tk = sc.dma("gpsimd", "wcast", wo2[:, kd * D:(kd + 1) * D],
                    wT_d["o"][kd * 128:(kd + 1) * 128, :], waits=(qkv_hist[-1],))
    t_wo_ld = tk
    t_zrp = tuple(t_zrepack[p] for p in range(H // 2))
    wos = [mem.sb(f"wos{i}", [128, D], F32) for i in range(2)]
    wos_free = [None, None]
    wo_sp_fin = [None, None]
    t_wo_sp = None
    for tt in range(NT):
        i = tt % 2
        t_ev = None
        for c in range(2):
            pp = qkv_slot(len(qkv_hist))
            gate = qkv_hist[-2] if len(qkv_hist) >= 2 else None
            tmm = None
            for kd in range(ND):
                tmm = sc.add("tensor", lambda te, pp=pp, kd=kd, tt=tt, c=c: te.matmul(
                    pp, zT2[:, kd * S + tt * 128: kd * S + (tt + 1) * 128],
                    wo2[:, kd * D + c * 512: kd * D + (c + 1) * 512],
                    start=(kd == 0), stop=(kd == ND - 1)),
                    waits=(t_attn_z, t_wo_ld, gate) + t_zrp,
                    inc="tensor" if kd == ND - 1 else None)
            t_ev = sc.add("scalar", lambda s2, pp=pp, i=i, c=c: s2.activation(
                out=wos[i][:, c * 512:(c + 1) * 512], in_=pp, func=ACT.Copy),
                waits=(tmm, wos_free[i]), inc="scalar")
            qkv_hist.append(t_ev)
        t_wo_sp = sc.dma("sync", f"wos{i}", attn_d[tt * 128:(tt + 1) * 128, :],
                         wos[i][:], waits=(t_ev,))
        wos_free[i] = t_wo_sp
        wo_sp_fin[i] = t_wo_sp

    mem.free("kT2", "qT2", "vx0", "vx1", "nsc", "expw0", "expw1", "zT2", "zTo",
             "zsc0", "zsc1", "zsc2", "zsc3", "denbc0", "denbc1", "denbc2",
             "denbc3", "denbr0", "denbr1", "denbr2", "denbr3", "wo2", "wos0", "wos1",
             "ps0", "ps1", "pzx", "pmx")

    jw2 = (sc.now("vector"), sc.now("scalar"), sc.now("tensor"),
           sc.now("wos0") if "wos0" in sc.cnt else None,
           sc.now("wos1") if "wos1" in sc.cnt else None)
    BUFN = mem.sb("BUF", [128, NI * S], BF16)
    iscr = mem.sb("iscr", [128, I3], I32)
    smx = mem.sb("smx", [128, 128], F32)
    wnat = [mem.sb(f"wnat{i}", [128, D], F32, side="right") for i in range(2)]
    wdqo = [mem.sb(f"wdqo{i}", [128, D], BF16, side="right") for i in range(2)]
    wnatD = [mem.sb("wnatD", [128, I3], F32, side="right")]
    wdqoD = [mem.sb("wdqoD", [128, I3], BF16, side="right")]
    bmx = mem.sb("bmx", [128, 128], F32, side="right")
    rix = mem.sb("rix", [128, 128], F32, side="right")
    wtst = [mem.sb("wtst0", [128, I3], BF16, side="right")]
    ptr2 = [mem.ps(f"ptr2_{i}", [128, 128], BF16) for i in range(2)]

    wnat_free = [None, None]
    wdqo_free = [None, None]
    wnatD_free = [None]
    wdqoD_free = [None]
    wtst_free = [None]
    UG_BUFS = (wnat, wdqo, wnat_free, wdqo_free, "wnat")
    DN_BUFS = (wnatD, wdqoD, wnatD_free, wdqoD_free, "wnatD")
    ptr2_free = {}
    prev_fscr = None

    def dequant_tile(idx, src_ap, w, waits=(), bufs=None):
        wnat_, wdqo_, wnat_free_, wdqo_free_, strm = bufs
        i = idx % len(wnat_)
        nb = w // 32
        t_l = sc.dma("sync", strm + str(i), wnat_[i][:, 0:w], src_ap,
                     waits=(wnat_free_[i],) + jw2)
        w3 = wnat_[i][:, 0:w].rearrange("p (n b) -> p n b", b=32)
        t_b = sc.add("vector", lambda v, w3=w3, nb=nb: v.reduce_max(
            out=bmx[:, 96:96 + nb], in_=w3, axis=AX, apply_absolute_value=True),
            waits=(t_l, dq_state["prev_q"]), inc="vector")
        t_s = sc.add("vector", lambda v, nb=nb: v.tensor_scalar(
            out=smx[:, 96:96 + nb], in0=bmx[:, 96:96 + nb], scalar1=1.0 / 7.0,
            scalar2=EPS12, op0=ALU.mult, op1=ALU.max), waits=(t_b,), inc="vector")
        t_r = sc.add("vector", lambda v, nb=nb: v.reciprocal(
            out=rix[:, 96:96 + nb], in_=smx[:, 96:96 + nb]), waits=(t_s,), inc="vector")
        rb = rix[:, 96:96 + nb].unsqueeze(2).broadcast_to([128, nb, 32])
        sb_ = smx[:, 96:96 + nb].unsqueeze(2).broadcast_to([128, nb, 32])
        t_q = sc.add("vector", lambda v, w3=w3, rb=rb, w=w: v.tensor_tensor(
            out=iscr[:, 0:w].rearrange("p (n b) -> p n b", b=32), in0=w3, in1=rb,
            op=ALU.mult), waits=(t_r, dq_state["prev_q"]), inc="vector")
        dq_state["prev_q"] = t_q
        t_w2 = sc.add("vector", lambda v, i=i, sb_=sb_, w=w, wdqo_=wdqo_: v.tensor_tensor(
            out=wdqo_[i][:, 0:w].rearrange("p (n b) -> p n b", b=32),
            in0=iscr[:, 0:w].rearrange("p (n b) -> p n b", b=32), in1=sb_,
            op=ALU.mult), waits=(t_q, wdqo_free_[i]) + tuple(waits), inc="vector")
        wnat_free_[i] = t_w2
        return i, t_w2

    dq_state = {"prev_q": None}
    t_dq_down = None

    def down_dq_job(j):
        nonlocal t_dq_down
        t_l = sc.dma("sync", "wnatD0", wnatD[0][:], wdown_d[j * 128:(j + 1) * 128, :],
                     waits=(wnatD_free[0],) + jw2)
        w3 = wnatD[0][:].rearrange("p (n b) -> p n b", b=32)
        t_b = sc.add("vector", lambda v, w3=w3: v.reduce_max(
            out=bmx[:, 0:96], in_=w3, axis=AX, apply_absolute_value=True),
            waits=(t_l, dq_state["prev_q"]), inc="vector")
        t_s = sc.add("vector", lambda v: v.tensor_scalar(
            out=smx[:, 0:96], in0=bmx[:, 0:96], scalar1=1.0 / 7.0, scalar2=EPS12,
            op0=ALU.mult, op1=ALU.max), waits=(t_b,), inc="vector")
        t_r = sc.add("vector", lambda v: v.reciprocal(
            out=rix[:, 0:96], in_=smx[:, 0:96]), waits=(t_s,), inc="vector")
        rb = rix[:, 0:96].unsqueeze(2).broadcast_to([128, 96, 32])
        sb_ = smx[:, 0:96].unsqueeze(2).broadcast_to([128, 96, 32])
        t_q = sc.add("vector", lambda v, w3=w3, rb=rb: v.tensor_tensor(
            out=iscr[:, 0:I3].rearrange("p (n b) -> p n b", b=32), in0=w3,
            in1=rb, op=ALU.mult), waits=(t_r, dq_state["prev_q"]), inc="vector")
        dq_state["prev_q"] = t_q
        t_w2 = sc.add("vector", lambda v, sb_=sb_: v.tensor_tensor(
            out=wdqoD[0][:].rearrange("p (n b) -> p n b", b=32),
            in0=iscr[:, 0:I3].rearrange("p (n b) -> p n b", b=32), in1=sb_,
            op=ALU.mult), waits=(t_q, wdqoD_free[0]), inc="vector")
        tevT = None
        for k in range(NI):
            pp = ptr2[k % 2]
            tmmT = sc.add("tensor", lambda te, pp=pp, k=k: te.transpose(
                pp[:], wdqoD[0][:, k * 128:(k + 1) * 128], ident[:]),
                waits=(t_w2, t_id, ptr2_free.get(k % 2)), inc="tensor")
            tevT = sc.add("scalar", lambda s2, pp=pp, k=k: s2.activation(
                out=wtst[0][:, k * 128:(k + 1) * 128], in_=pp[:], func=ACT.Copy),
                waits=(tmmT, wtst_free[0] if k == 0 else None), inc="scalar")
            ptr2_free[k % 2] = tevT
        wdqoD_free[0] = tevT
        wnatD_free[0] = tevT
        t_dq_down = sc.dma(
            "sync", "wtsp0",
            wdnT_d[:, j * 128:(j + 1) * 128].rearrange("(t p) c -> p t c", p=128),
            wtst[0][:].rearrange("p (t c) -> p t c", c=128), waits=(tevT,))
        wtst_free[0] = t_dq_down

    t_xq = {}
    prev_xq = None
    for dt in range(ND):
        sl = xnT[:, dt * S:(dt + 1) * S]
        ti = sc.add("vector", lambda v, sl=sl: v.tensor_scalar(
            out=iscr[:, 0:S], in0=sl, scalar1=scal[:, 1:2], scalar2=None,
            op0=ALU.mult), waits=(t_rx, t_qkv_last, prev_xq), inc="vector")
        t_xq[dt] = prev_xq = sc.add("vector", lambda v, sl=sl: v.tensor_scalar(
            out=sl, in0=iscr[:, 0:S], scalar1=scal[:, 0:1], scalar2=None,
            op0=ALU.mult), waits=(ti,), inc="vector")
    t_xq_all = t_xq[ND - 1]

    upst = [mem.sb(f"upst{i}", [128, S], BF16) for i in range(2)]
    wstU = [mem.sb(f"wstU{i}", [128, D], BF16) for i in range(2)]
    wstG = [mem.sb(f"wstG{i}", [128, D], BF16) for i in range(2)]
    pug = [mem.ps(f"pug{i}", [128, 512]) for i in range(2)]

    wst_free = {"up": [None, None], "gate": [None, None]}
    upst_free = [None, None]
    t_mx_prev = None
    deferred_mx = None
    pug_hist = []
    t_upmax = None
    t_gatemax = None
    dq_state["prev_q"] = t_xq_all
    for it in range(NI):
        i = it % 2
        wtoks = {}
        dq2 = {}
        for nmi, (nm, wbuf, srcw_d) in enumerate(
                (("up", wstU, wup_d), ("gate", wstG, wgate_d))):
            dqi, t_w2 = dequant_tile(2 * it + nmi, srcw_d[it * 128:(it + 1) * 128, :],
                                     D, bufs=UG_BUFS)
            dq2[nm] = (dqi, t_w2)
        if it % 3 == 2:
            down_dq_job(it // 3)
        for nmi, (nm, wbuf, srcw_d) in enumerate(
                (("up", wstU, wup_d), ("gate", wstG, wgate_d))):
            dqi, t_w2 = dq2[nm]
            tevT = None
            for kd in range(ND):
                pp = ptr2[kd % 2]
                tmmT = sc.add("tensor", lambda te, pp=pp, dqi=dqi, kd=kd: te.transpose(
                    pp[:], wdqo[dqi][:, kd * 128:(kd + 1) * 128], ident[:]),
                    waits=(t_w2, t_id, ptr2_free.get(kd % 2)), inc="tensor")
                tevT = sc.add("scalar", lambda s2, pp=pp, wbuf=wbuf, i=i, kd=kd: s2.activation(
                    out=wbuf[i][:, kd * 128:(kd + 1) * 128], in_=pp[:], func=ACT.Copy),
                    waits=(tmmT, wst_free[nm][i]), inc="scalar")
                ptr2_free[kd % 2] = tevT
            wdqo_free[dqi] = tevT
            wtoks[nm] = tevT
        for nm, wbuf in (("up", wstU), ("gate", wstG)):
            ev_t = None
            for c in range(4):
                pp = pug[len(pug_hist) % 2]
                gate = pug_hist[-2] if len(pug_hist) >= 2 else None
                tmm = None
                for kd in range(ND):
                    tmm = sc.add("tensor", lambda te, pp=pp, wbuf=wbuf, i=i, kd=kd, c=c: te.matmul(
                        pp[:], wbuf[i][:, kd * 128:(kd + 1) * 128],
                        xnT[:, kd * S + c * 512: kd * S + (c + 1) * 512],
                        start=(kd == 0), stop=(kd == ND - 1)),
                        waits=(wtoks[nm], t_xq[kd], gate),
                        inc="tensor" if kd == ND - 1 else None)
                if nm == "up":
                    out_ap = upst[i][:, c * 512:(c + 1) * 512]
                    ev_t = sc.add("scalar", lambda s2, pp=pp, out_ap=out_ap: s2.activation(
                        out=out_ap, in_=pp[:], func=ACT.Copy),
                        waits=(tmm, upst_free[i] if c == 0 else None),
                        inc="scalar")
                else:
                    out_ap = BUFN[:, it * S + c * 512: it * S + (c + 1) * 512]
                    ev_t = sc.add("scalar", lambda s2, pp=pp, out_ap=out_ap: s2.activation(
                        out=out_ap, in_=pp[:], func=ACT.Copy), waits=(tmm,), inc="scalar")
                pug_hist.append(ev_t)
            wst_free[nm][i] = ev_t
            if deferred_mx is not None:
                (d_src, d_cslot, d_ev) = deferred_mx
                t_red = sc.add("vector", lambda v, d_src=d_src: v.reduce_max(
                    out=cmx[:, 7:8], in_=d_src, axis=AX, apply_absolute_value=True),
                    waits=(d_ev, t_mx_prev), inc="vector")
                t_mx = sc.add("vector", lambda v, d_cslot=d_cslot: v.tensor_tensor(
                    out=cmx[:, d_cslot:d_cslot + 1], in0=cmx[:, d_cslot:d_cslot + 1],
                    in1=cmx[:, 7:8], op=ALU.max), waits=(t_red, t_cmx0), inc="vector")
                t_mx_prev = t_mx
                if d_cslot == 1:
                    t_upmax = t_mx
                else:
                    t_gatemax = t_mx
            src2 = upst[i][:] if nm == "up" else BUFN[:, it * S:(it + 1) * S]
            deferred_mx = (src2, 1 if nm == "up" else 2, ev_t)
            if nm == "up":
                upst_free[i] = sc.dma("sync", f"upst{i}", up_d[it * 128:(it + 1) * 128, :],
                                      upst[i][:], waits=(ev_t,))
    (d_src, d_cslot, d_ev) = deferred_mx
    t_red = sc.add("vector", lambda v, d_src=d_src: v.reduce_max(
        out=cmx[:, 7:8], in_=d_src, axis=AX, apply_absolute_value=True),
        waits=(d_ev, t_mx_prev), inc="vector")
    t_gatemax = sc.add("vector", lambda v, d_cslot=d_cslot: v.tensor_tensor(
        out=cmx[:, d_cslot:d_cslot + 1], in0=cmx[:, d_cslot:d_cslot + 1],
        in1=cmx[:, 7:8], op=ALU.max), waits=(t_red, t_cmx0), inc="vector")
    t_dq_down_all = (wtst_free[0], wtst_free[0])
    mem.free("wstU0", "wstU1", "wstG0", "wstG1")
    mem.free("wtst0", "rix", "bmx", "wdqoD", "wnatD", "wdqo0", "wdqo1",
             "wnat0", "wnat1", "xnT")

    t_u1 = tree(cmx[:, 1:2], ccs[0:1, 16:17], ALU.max, 128, (t_upmax,))
    t_g1 = tree(cmx[:, 2:3], ccs[0:1, 17:18], ALU.max, 128, (t_gatemax, t_u1))
    t_bc1 = allreduce(1, t_g1)
    t_su = sc.add("vector", lambda v: v.tensor_scalar(out=scal[:, 2:3], in0=ccb[:, 16:17],
                  scalar1=1.0 / QMAX, scalar2=EPS12, op0=ALU.mult, op1=ALU.max),
                  waits=(t_bc1,), inc="vector")
    t_ru = sc.add("vector", lambda v: v.reciprocal(out=scal[:, 3:4], in_=scal[:, 2:3]),
                  waits=(t_su,), inc="vector")
    t_sg = sc.add("vector", lambda v: v.tensor_scalar(out=scal[:, 4:5], in0=ccb[:, 17:18],
                  scalar1=1.0 / QMAX, scalar2=EPS12, op0=ALU.mult, op1=ALU.max),
                  waits=(t_bc1,), inc="vector")
    t_rg = sc.add("vector", lambda v: v.reciprocal(out=scal[:, 5:6], in_=scal[:, 4:5]),
                  waits=(t_sg,), inc="vector")

    wdn = mem.sb("wdn", [128, 2 * NI * 512], BF16)
    t_wdn = None
    for dc in range(2):
        for it2 in range(NI):
            t_wdn = sc.dma(
                "sync", "wdn",
                wdn[:, dc * NI * 512 + it2 * 512: dc * NI * 512 + (it2 + 1) * 512],
                wdnT_d[it2 * 128:(it2 + 1) * 128, dc * 512:(dc + 1) * 512],
                waits=t_dq_down_all)
    iA = [mem.sb(f"iA{i}", [128, S], I32) for i in range(2)]
    iB1 = mem.sb("iB0", [128, S], I32)
    iB = [iB1, iB1]
    fsg1 = mem.sb("fsg0", [128, S], F32)
    fsg = [fsg1, fsg1]

    iA_free = [None, None]
    iB_free = [None, None]
    fsg_free = [None, None]
    t_pmax = None
    for it in range(NI):
        i = it % 2
        buf_sl = BUFN[:, it * S:(it + 1) * S]
        tGo = sc.add("scalar", lambda s2, buf_sl=buf_sl, i=i: s2.activation(
            out=iA[i][:], in_=buf_sl, func=ACT.Copy, scale=scal[:, 5:6]),
            waits=(t_rg, iA_free[i], t_dq_down_all[0], t_dq_down_all[1]), inc="scalar")
        tsg_ = sc.add("scalar", lambda s2, i=i: s2.activation(
            out=fsg[i][:], in_=iA[i][:], func=ACT.Sigmoid, scale=scal[:, 4:5]),
            waits=(tGo, fsg_free[0]), inc="scalar")
        tSo = sc.add("scalar", lambda s2, i=i: s2.activation(
            out=iB[i][:], in_=fsg[i][:], func=ACT.Copy, scale=65536.0),
            waits=(tsg_, iB_free[0]), inc="scalar")
        fsg_free[0] = tSo
        tP = sc.add("vector", lambda v, buf_sl=buf_sl, i=i: v.tensor_tensor(
            out=buf_sl, in0=iA[i][:], in1=iB[i][:], op=ALU.mult),
            waits=(tSo,), inc="vector")
        iA_free[i] = tP
        iB_free[0] = tP
        t_red = sc.add("vector", lambda v, buf_sl=buf_sl: v.reduce_max(
            out=cmx[:, 7:8], in_=buf_sl, axis=AX, apply_absolute_value=True),
            waits=(tP,), inc="vector")
        t_pmax = sc.add("vector", lambda v: v.tensor_tensor(
            out=cmx[:, 3:4], in0=cmx[:, 3:4], in1=cmx[:, 7:8], op=ALU.max),
            waits=(t_red,), inc="vector")

    t_p1 = tree(cmx[:, 3:4], ccs[0:1, 32:33], ALU.max, 128, (t_pmax,))
    t_bc2 = allreduce(2, t_p1)
    t_a = sc.add("vector", lambda v: v.tensor_tensor(out=scal[:, 22:23],
                 in0=ccb[:, 32:33], in1=scal[:, 4:5], op=ALU.mult),
                 waits=(t_bc2,), inc="vector")
    t_b2 = sc.add("vector", lambda v: v.tensor_scalar(out=scal[:, 6:7], in0=scal[:, 22:23],
                  scalar1=1.0 / (65536.0 * QMAX), scalar2=EPS12, op0=ALU.mult,
                  op1=ALU.max), waits=(t_a,), inc="vector")
    t_c = sc.add("vector", lambda v: v.reciprocal(out=scal[:, 23:24], in_=scal[:, 6:7]),
                 waits=(t_b2,), inc="vector")
    t_d2 = sc.add("vector", lambda v: v.tensor_scalar(out=scal[:, 24:25], in0=scal[:, 4:5],
                  scalar1=1.0 / 65536.0, scalar2=None, op0=ALU.mult), inc="vector")
    t_r2 = sc.add("vector", lambda v: v.tensor_tensor(out=scal[:, 7:8], in0=scal[:, 24:25],
                  in1=scal[:, 23:24], op=ALU.mult), waits=(t_c, t_d2), inc="vector")

    t_omax = None
    for it in range(NI):
        i = it % 2
        buf_sl = BUFN[:, it * S:(it + 1) * S]
        t_ul = sc.dma("sync", f"upst{i}", upst[i][:], up_d[it * 128:(it + 1) * 128, :],
                      waits=(upst_free[i],))
        tR2 = sc.add("scalar", lambda s2, buf_sl=buf_sl, i=i: s2.activation(
            out=iA[i][:], in_=buf_sl, func=ACT.Copy, scale=scal[:, 7:8]),
            waits=(t_r2, iA_free[i]), inc="scalar")
        tU = sc.add("scalar", lambda s2, i=i: s2.activation(
            out=iB[i][:], in_=upst[i][:], func=ACT.Copy, scale=scal[:, 3:4]),
            waits=(t_ru, t_ul, iB_free[0]), inc="scalar")
        tO = sc.add("vector", lambda v, buf_sl=buf_sl, i=i: v.tensor_tensor(
            out=buf_sl, in0=iA[i][:], in1=iB[i][:], op=ALU.mult),
            waits=(tR2, tU), inc="vector")
        iA_free[i] = tO
        iB_free[0] = tO
        upst_free[i] = tO
        t_red = sc.add("vector", lambda v, buf_sl=buf_sl: v.reduce_max(
            out=cmx[:, 7:8], in_=buf_sl, axis=AX, apply_absolute_value=True),
            waits=(tO,), inc="vector")
        t_omax = sc.add("vector", lambda v: v.tensor_tensor(
            out=cmx[:, 4:5], in0=cmx[:, 4:5], in1=cmx[:, 7:8], op=ALU.max),
            waits=(t_red,), inc="vector")

    t_o1t = tree(cmx[:, 4:5], ccs[0:1, 48:49], ALU.max, 128, (t_omax,))
    t_bc3 = allreduce(3, t_o1t)
    t_e = sc.add("vector", lambda v: v.tensor_tensor(out=scal[:, 25:26], in0=scal[:, 6:7],
                 in1=scal[:, 2:3], op=ALU.mult), waits=(t_b2, t_su), inc="vector")
    t_f = sc.add("vector", lambda v: v.tensor_tensor(out=scal[:, 22:23], in0=ccb[:, 48:49],
                 in1=scal[:, 25:26], op=ALU.mult), waits=(t_bc3, t_e), inc="vector")
    t_g = sc.add("vector", lambda v: v.tensor_scalar(out=scal[:, 26:27], in0=scal[:, 22:23],
                 scalar1=1.0 / QMAX, scalar2=EPS12, op0=ALU.mult, op1=ALU.max),
                 waits=(t_f,), inc="vector")
    t_h = sc.add("vector", lambda v: v.reciprocal(out=scal[:, 27:28], in_=scal[:, 26:27]),
                 waits=(t_g,), inc="vector")
    t_r3 = sc.add("vector", lambda v: v.tensor_tensor(out=scal[:, 28:29], in0=scal[:, 25:26],
                  in1=scal[:, 27:28], op=ALU.mult), waits=(t_h,), inc="vector")

    t_oq = {}
    for it in range(NI):
        i = it % 2
        buf_sl = BUFN[:, it * S:(it + 1) * S]
        tq1 = sc.add("scalar", lambda s2, buf_sl=buf_sl, i=i: s2.activation(
            out=iA[i][:], in_=buf_sl, func=ACT.Copy, scale=scal[:, 28:29]),
            waits=(t_r3, iA_free[i]), inc="scalar")
        t_oq[it] = sc.add("vector", lambda v, buf_sl=buf_sl, i=i: v.tensor_copy(
            out=buf_sl, in_=iA[i][:]), waits=(tq1,), inc="vector")
        iA_free[i] = t_oq[it]

    mem.free("iA0", "iA1", "iB0", "fsg0", "pug0", "pug1")

    jw6 = (sc.now("vector"), sc.now("scalar"))
    xrl = [mem.sb(f"xrl{i}", [128, D], F32) for i in range(2)]
    arl = [mem.sb(f"arl{i}", [128, D], F32) for i in range(2)]
    ost = [mem.sb(f"ost{i}", [128, D], F32) for i in range(2)]
    pdn = [mem.ps(f"pdn{i}", [128, 512]) for i in range(2)]

    xrl_free = [None, None]
    arl_free = [None, None]
    ost_free = [None, None]
    pdn_hist = []
    out_fin = [None, None]
    for tt in range(NT):
        i = tt % 2
        t_xl = sc.dma("sync", f"xrl{i}", xrl[i][:], x_d[tt * 128:(tt + 1) * 128, :],
                      waits=(xrl_free[i],) + jw6)
        t_al = sc.dma("sync", f"arl{i}", arl[i][:], attn_d[tt * 128:(tt + 1) * 128, :],
                      waits=(arl_free[i], wo_sp_fin[0], wo_sp_fin[1]) + jw6)
        for dc in range(2):
            pp = pdn[len(pdn_hist) % 2]
            gate = pdn_hist[-2] if len(pdn_hist) >= 2 else None
            tmm = None
            for it in range(NI):
                tmm = sc.add("tensor", lambda te, pp=pp, it=it, tt=tt, dc=dc: te.matmul(
                    pp[:], BUFN[:, it * S + tt * 128: it * S + (tt + 1) * 128],
                    wdn[:, dc * NI * 512 + it * 512: dc * NI * 512 + (it + 1) * 512],
                    start=(it == 0), stop=(it == NI - 1)),
                    waits=(t_oq[it], t_wdn, gate),
                    inc="tensor" if it == NI - 1 else None)
            t_ev = sc.add("scalar", lambda s2, pp=pp, i=i, dc=dc: s2.activation(
                out=ost[i][:, dc * 512:(dc + 1) * 512], in_=pp[:], func=ACT.Copy,
                scale=scal[:, 26:27]),
                waits=(tmm, t_g, ost_free[i] if dc == 0 else None), inc="scalar")
            pdn_hist.append(t_ev)
            t_a1 = sc.add("vector", lambda v, i=i, dc=dc: v.tensor_tensor(
                out=ost[i][:, dc * 512:(dc + 1) * 512],
                in0=ost[i][:, dc * 512:(dc + 1) * 512],
                in1=xrl[i][:, dc * 512:(dc + 1) * 512], op=ALU.add),
                waits=(t_ev, t_xl), inc="vector")
            t_a2 = sc.add("vector", lambda v, i=i, dc=dc: v.tensor_tensor(
                out=ost[i][:, dc * 512:(dc + 1) * 512],
                in0=ost[i][:, dc * 512:(dc + 1) * 512],
                in1=arl[i][:, dc * 512:(dc + 1) * 512], op=ALU.add),
                waits=(t_a1, t_al), inc="vector")
        t_st = sc.dma("sync", f"ost{i}", out_d[tt * 128:(tt + 1) * 128, :], ost[i][:],
                      waits=(t_a2,))
        xrl_free[i] = t_st
        arl_free[i] = t_st
        ost_free[i] = t_st
        out_fin[i] = t_st

    sc.add("gpsimd", fn=None, waits=(out_fin[0], out_fin[1]))

    with nc.Block() as block:
        sc.emit(block)
    es.close()
    return nc


_NC_CACHE = [None]


def kernel(L0_attn_x, L0_attn_W_K, L0_attn_W_Q, L0_attn_W_V, L0_attn_W_O,
           L0_ffn_w_gate, L0_ffn_w_up, L0_ffn_w_down):
    f = np.float32
    x = np.ascontiguousarray(np.asarray(L0_attn_x, f))
    wq = np.ascontiguousarray(np.asarray(L0_attn_W_Q, f).transpose(2, 0, 1).reshape(D, D))
    wk = np.ascontiguousarray(np.asarray(L0_attn_W_K, f).transpose(2, 0, 1).reshape(D, D))
    wv = np.ascontiguousarray(np.asarray(L0_attn_W_V, f).transpose(2, 0, 1).reshape(D, D))
    wo = np.ascontiguousarray(np.asarray(L0_attn_W_O, f).T)
    wupn = np.ascontiguousarray(np.asarray(L0_ffn_w_up, f))
    wgn = np.ascontiguousarray(np.asarray(L0_ffn_w_gate, f))
    wdn_ = np.ascontiguousarray(np.asarray(L0_ffn_w_down, f))
    ident = np.eye(128, dtype=BF)
    identf = np.eye(128, dtype=f)
    kk = np.arange(128)[:, None]
    dmsk = np.concatenate(
        [((d * 128 + kk) <= np.arange(512)[None, :]) for d in range(4)],
        axis=1).astype(BF)
    ones1 = np.ones((1, 128), dtype=BF)
    ones1f = np.ones((1, 128), dtype=f)
    onesc = np.ones((128, 1), dtype=BF)

    if _NC_CACHE[0] is None:
        _NC_CACHE[0] = build_nc()
    nc = _NC_CACHE[0]
    in_maps = []
    for c in range(NCORES):
        in_maps.append({"x": np.ascontiguousarray(x[c]), "wqT": wq, "wkT": wk,
                        "wvT": wv, "woT": wo, "wup": wupn, "wgate": wgn,
                        "wdown": wdn_, "ident": ident, "dmask": dmsk,
                        "ones1": ones1, "ones1f": ones1f, "onesc": onesc, "identf": identf})
    res = run_bass_kernel_spmd(nc, in_maps, core_ids=list(range(NCORES)))
    return np.stack([res.results[c]["out"] for c in range(NCORES)], axis=0)


# version 16
# speedup vs baseline: 1.1679x; 1.0570x over previous
import numpy as np
import ml_dtypes
import concourse.bass as bass
import concourse.mybir as mybir
from concourse.bass_utils import run_bass_kernel_spmd

F32 = mybir.dt.float32
BF16 = mybir.dt.bfloat16
I32 = mybir.dt.int32
AX = mybir.AxisListType.X
ALU = mybir.AluOpType
ACT = mybir.ActivationFunctionType
BF = ml_dtypes.bfloat16

S, D, H, DH, I3 = 2048, 1024, 16, 64, 3072
NT, ND, NI = S // 128, D // 128, I3 // 128
NCORES = 8
QMAX = 32767.0
EPS12 = 1e-12

ENGS = ("tensor", "vector", "scalar", "gpsimd", "sync")


class Sched:

    def __init__(self, nc, es):
        self.nc = nc
        self.es = es
        self.steps = {e: [] for e in ENGS}
        names = list(ENGS) + ["cc"]
        self.sems = {n: es.enter_context(nc.semaphore("sem_" + n)) for n in names}
        self.cnt = {n: 0 for n in names}
        self.wm = {}

    def now(self, name):
        return (name, self.cnt[name])

    def stream(self, name):
        if name not in self.sems:
            self.sems[name] = self.es.enter_context(self.nc.semaphore("sem_" + name))
            self.cnt[name] = 0
        return name

    def add(self, eng, fn=None, waits=(), inc=None, n=1):
        todo = []
        for t in waits:
            if t is None:
                continue
            s, v = t
            key = (eng, s)
            if self.wm.get(key, 0) < v:
                self.wm[key] = v
                todo.append((self.sems[s], v))
        tok, sem = None, None
        if inc is not None:
            self.cnt[inc] += n
            tok = (inc, self.cnt[inc])
            sem = self.sems[inc]
        amt = n

        def thunk(obj, todo=todo, fn=fn, sem=sem, amt=amt):
            for (sm, v) in todo:
                obj.wait_ge(sm, v)
            ins = fn(obj) if fn is not None else None
            if sem is not None:
                ins.then_inc(sem, amt)

        self.steps[eng].append(thunk)
        return tok

    def dma(self, qeng, stream, out, in_, waits=()):
        return self.add(qeng, lambda o, out=out, in_=in_: o.dma_start(out=out, in_=in_),
                        waits=waits, inc=self.stream(stream), n=16)

    def dmat(self, stream, out, in_, waits=()):
        return self.add("sync",
                        lambda o, out=out, in_=in_: o.dma_start(out=out, in_=in_, transpose=True),
                        waits=waits, inc=self.stream(stream), n=16)

    def emit(self, block):
        for e in ENGS:
            thunks = self.steps[e]

            def body(obj, thunks=thunks):
                for t in thunks:
                    t(obj)
            getattr(block, e)(body)


class Mem:
    def __init__(self, nc):
        self.nc = nc
        self.guards = {}
        self.order = {}
        self.n = 0

    def _reg(self, name, g):
        t = g.__enter__()
        self.guards[name] = g
        self.order[name] = self.n
        self.n += 1
        return t

    def sb(self, name, shape, dt, side=None):
        return self._reg(name, self.nc.sbuf_tensor(name, shape, dt, side=side))

    def ps(self, name, shape, dt=F32):
        return self._reg(name, self.nc.psum_tensor(name, shape, dt))

    def free(self, *names):
        for n in sorted(names, key=lambda n: self.order[n], reverse=True):
            self.guards.pop(n).__exit__(None, None, None)


def build_nc():
    nc = bass.Bass("TRN2", target_bir_lowering=False)

    x_d = nc.dram_tensor("x", [S, D], F32, kind="ExternalInput")
    wT_d = {n: nc.dram_tensor("w" + n + "T", [D, D], F32, kind="ExternalInput")
            for n in ("q", "k", "v", "o")}
    wup_d = nc.dram_tensor("wup", [I3, D], F32, kind="ExternalInput")
    wgate_d = nc.dram_tensor("wgate", [I3, D], F32, kind="ExternalInput")
    wdown_d = nc.dram_tensor("wdown", [D, I3], F32, kind="ExternalInput")
    ident_d = nc.dram_tensor("ident", [128, 128], BF16, kind="ExternalInput")
    identf_d = nc.dram_tensor("identf", [128, 128], F32, kind="ExternalInput")
    dmask_d = nc.dram_tensor("dmask", [128, 2048], BF16, kind="ExternalInput")
    ones1_d = nc.dram_tensor("ones1", [1, 128], BF16, kind="ExternalInput")
    ones1f_d = nc.dram_tensor("ones1f", [1, 128], F32, kind="ExternalInput")
    onesc_d = nc.dram_tensor("onesc", [128, 1], BF16, kind="ExternalInput")
    out_d = nc.dram_tensor("out", [S, D], F32, kind="ExternalOutput")
    cc_in = [nc.dram_tensor(f"cc_in{r}", [1, 16], F32) for r in range(4)]
    cc_out = [nc.dram_tensor(f"cc_out{r}", [1, 16], F32) for r in range(4)]
    wdnT_d = nc.dram_tensor("wdnT", [I3, D], BF16)
    up_d = nc.dram_tensor("upsp", [I3, S], BF16)
    attn_d = nc.dram_tensor("attnsp", [S, D], F32)
    den_d = [nc.dram_tensor(f"densp{i}", [1, 512], F32) for i in range(4)]

    from contextlib import ExitStack
    es = ExitStack()
    sc = Sched(nc, es)
    mem = Mem(nc)

    ident = mem.sb("s_ident", [128, 128], BF16)
    identf = mem.sb("s_identf", [128, 128], F32)
    colws = mem.sb("s_colws", [128, 128], F32)
    ptre = mem.ps("s_ptre", [128, 512], F32)
    dmask = mem.sb("s_dmask", [128, 2048], BF16)
    ones1 = mem.sb("s_ones1", [1, 128], BF16)
    ones1f = mem.sb("s_ones1f", [1, 128], F32)
    onesc = mem.sb("s_onesc", [128, 1], BF16)
    trs = mem.sb("s_trs", [32, 32], F32)
    trs2 = mem.sb("s_trs2", [32, 32], F32)
    scal = mem.sb("scal", [128, 64], F32)
    ccs = mem.sb("ccs", [1, 64], F32)
    ccb = mem.sb("ccb", [128, 64], F32)
    cmx = mem.sb("cmx", [128, 8], F32)

    sc.dma("sync", "const", ident[:], ident_d[:, :])
    sc.dma("sync", "const", identf[:], identf_d[:, :])
    sc.dma("sync", "const", dmask[:], dmask_d[:, :])
    sc.dma("sync", "const", ones1[:], ones1_d[:, :])
    sc.dma("sync", "const", ones1f[:], ones1f_d[:, :])
    t_const = sc.dma("sync", "const", onesc[:], onesc_d[:, :])
    t_id = t_dm = t_o1 = t_o1f = t_const
    t_cmx0 = sc.add("vector", lambda v: v.memset(cmx[:], 0.0), inc="vector")
    t_trs0 = sc.add("vector", lambda v: v.memset(trs[:], 0.0), inc="vector")
    t_ccs0 = sc.add("vector", lambda v: v.memset(ccs[:], 0.0), inc="vector")
    t_cw0 = sc.add("vector", lambda v: v.memset(colws[:], 0.0), inc="vector")

    tree_busy = [None]

    def tree(col_ap, out_ap, op, p0, waits):
        t1 = sc.add("vector", lambda v, col_ap=col_ap, p0=p0: v.tensor_copy(
            out=colws[0:p0, 0:1], in_=col_ap),
            waits=tuple(waits) + (tree_busy[0], t_cw0), inc="vector")
        t2 = sc.add("tensor", lambda te: te.transpose(ptre[:, 0:128], colws[:], identf[:]),
                    waits=(t1, t_id, tree_busy[0]), inc="tensor")
        tok = sc.add("vector", lambda v, out_ap=out_ap, op=op, p0=p0: v.tensor_reduce(
            out=out_ap, in_=ptre[0:1, 0:p0], axis=AX, op=op), waits=(t2,), inc="vector")
        tree_busy[0] = tok
        return tok

    def allreduce(r, t_staged):
        t_w = sc.dma("gpsimd", "gcc", cc_in[r][:, :], ccs[0:1, 16 * r:16 * r + 16],
                     waits=(t_staged, t_ccs0))
        t_ar = sc.add("gpsimd", lambda g, r=r: g.collective_compute(
            "AllReduce", ALU.max, replica_groups=[list(range(NCORES))],
            ins=[cc_in[r].ap().opt()], outs=[cc_out[r].ap().opt()]),
            waits=(t_w,), inc="cc")
        return sc.dma("sync", "ccb", ccb[:, 16 * r:16 * r + 16],
                      cc_out[r].ap().partition_broadcast(128), waits=(t_ar,))

    xnT = mem.sb("xnT", [128, ND * S], BF16, side="right")
    xst = [mem.sb(f"xst{i}", [128, D], F32) for i in range(2)]
    xnst = [mem.sb(f"xnst{i}", [128, D], BF16) for i in range(2)]
    lns = mem.sb("lns", [128, 64], F32)
    sqd = mem.sb("sqd", [128, D], F32)
    ptr = [mem.ps(f"ptr{i}", [128, 128], BF16) for i in range(2)]

    xst_free = [None, None]
    xnst_free = [None, None]
    ptr_free = {}
    t_xn_done = None
    t_xnmax = None
    t_sums = {}
    for tt in range(NT):
        i = tt % 2
        t_ld = sc.dma("sync", f"xst{i}", xst[i][:], x_d[tt * 128:(tt + 1) * 128, :],
                      waits=(xst_free[i],))
        x_ap = xst[i][:]
        t1 = sc.add("vector", lambda v, x_ap=x_ap, tt=tt: v.reduce_sum(
            out=lns[:, tt:tt + 1], in_=x_ap, axis=AX), waits=(t_ld,), inc="vector")
        t2 = sc.add("vector", lambda v, x_ap=x_ap, tt=tt: v.scalar_tensor_tensor(
            out=sqd[:], in0=x_ap, scalar=1.0, in1=x_ap, op0=ALU.mult, op1=ALU.mult,
            accum_out=lns[:, 16 + tt:17 + tt]), waits=(t1,), inc="vector")
        t_sums[tt] = t2
        xst_free[tt % 2] = t2
    tb = sc.add("vector", lambda v: v.tensor_scalar(
        out=lns[:, 0:16], in0=lns[:, 0:16], scalar1=1.0 / D, scalar2=None,
        op0=ALU.mult), waits=(t_sums[NT - 1],), inc="vector")
    tb = sc.add("vector", lambda v: v.tensor_scalar(
        out=lns[:, 16:32], in0=lns[:, 16:32], scalar1=1.0 / D, scalar2=None,
        op0=ALU.mult), waits=(tb,), inc="vector")
    tb = sc.add("vector", lambda v: v.scalar_tensor_tensor(
        out=lns[:, 32:48], in0=lns[:, 0:16], scalar=1.0, in1=lns[:, 0:16],
        op0=ALU.mult, op1=ALU.mult), waits=(tb,), inc="vector")
    tb = sc.add("vector", lambda v: v.tensor_tensor(
        out=lns[:, 32:48], in0=lns[:, 16:32], in1=lns[:, 32:48],
        op=ALU.subtract), waits=(tb,), inc="vector")
    tb = sc.add("vector", lambda v: v.tensor_scalar(
        out=lns[:, 32:48], in0=lns[:, 32:48], scalar1=1e-5, scalar2=None,
        op0=ALU.add), waits=(tb,), inc="vector")
    tb = sc.add("scalar", lambda s: s.activation(
        out=lns[:, 48:64], in_=lns[:, 32:48], func=ACT.Sqrt), waits=(tb,), inc="scalar")
    tb = sc.add("vector", lambda v: v.reciprocal(
        out=lns[:, 32:48], in_=lns[:, 48:64]), waits=(tb,), inc="vector")
    t_stats = sc.add("vector", lambda v: v.scalar_tensor_tensor(
        out=lns[:, 48:64], in0=lns[:, 0:16], scalar=-1.0, in1=lns[:, 32:48],
        op0=ALU.mult, op1=ALU.mult), waits=(tb,), inc="vector")
    for tt in range(NT):
        i = tt % 2
        t_ld = sc.dma("sync", f"xst{i}", xst[i][:], x_d[tt * 128:(tt + 1) * 128, :],
                      waits=(xst_free[i],))
        x_ap = xst[i][:]
        n_ap = xnst[i][:]
        t7 = sc.add("vector", lambda v, x_ap=x_ap, n_ap=n_ap, tt=tt: v.tensor_scalar(
            out=n_ap, in0=x_ap, scalar1=lns[:, 32 + tt:33 + tt],
            scalar2=lns[:, 48 + tt:49 + tt], op0=ALU.mult, op1=ALU.add),
            waits=(t_ld, t_stats, xnst_free[i]), inc="vector")
        xst_free[i] = t7
        t8 = sc.add("vector", lambda v, n_ap=n_ap: v.reduce_max(
            out=lns[:, 0:1], in_=n_ap, axis=AX, apply_absolute_value=True),
            waits=(t7, t_cmx0), inc="vector")
        t_xnmax = sc.add("vector", lambda v: v.tensor_tensor(
            out=cmx[:, 0:1], in0=cmx[:, 0:1], in1=lns[:, 0:1], op=ALU.max),
            waits=(t8,), inc="vector")
        last_tr = None
        for dt in range(ND):
            pp = ptr[dt % 2]
            tmm = sc.add("tensor", lambda te, n_ap=n_ap, dt=dt, pp=pp: te.transpose(
                pp[:], n_ap[:, dt * 128:(dt + 1) * 128], ident[:]),
                waits=(t7, t_id, ptr_free.get(dt % 2)), inc="tensor")
            tev = sc.add("vector", lambda v, pp=pp, dt=dt, tt=tt: v.tensor_copy(
                out=xnT[:, dt * S + tt * 128: dt * S + (tt + 1) * 128], in_=pp[:]),
                waits=(tmm,), inc="vector")
            ptr_free[dt % 2] = tev
            last_tr = tmm
        xnst_free[i] = last_tr
        t_xn_done = tev
    mem.free("xst0", "xst1", "xnst0", "xnst1", "lns", "sqd", "ptr0", "ptr1")

    t_st0 = tree(cmx[:, 0:1], ccs[0:1, 0:1], ALU.max, 128, (t_xnmax,))
    t_bc0 = allreduce(0, t_st0)
    t_sx = sc.add("vector", lambda v: v.tensor_scalar(out=scal[:, 0:1], in0=ccb[:, 0:1],
                  scalar1=1.0 / QMAX, scalar2=EPS12, op0=ALU.mult, op1=ALU.max),
                  waits=(t_bc0,), inc="vector")
    t_rx = sc.add("vector", lambda v: v.reciprocal(out=scal[:, 1:2], in_=scal[:, 0:1]),
                  waits=(t_sx,), inc="vector")

    jw1 = (sc.now("vector"), sc.now("scalar"), sc.now("tensor"))
    wsb = {n: mem.sb("w" + n, [128, ND * D], BF16, side="right") for n in ("q", "k", "v")}
    t_wld = {}
    for n in ("q", "k", "v"):
        tk = None
        for kd in range(ND):
            tk = sc.dma("gpsimd", "wcast", wsb[n][:, kd * D:(kd + 1) * D],
                        wT_d[n][kd * 128:(kd + 1) * 128, :], waits=jw1)
        t_wld[n] = tk
    for n in ("q", "k", "v"):
        t_wld[n] = tk

    kT2 = mem.sb("kT2", [128, S], BF16)
    qT2 = mem.sb("qT2", [128, S], BF16)
    vx = [mem.sb(f"vx{i}", [128, NT * 65], BF16) for i in range(2)]
    nsc = mem.sb("nsc", [128, S], BF16)
    expw = [mem.sb(f"expw{i}", [128, NT * 512], BF16) for i in range(2)]
    zT2 = mem.sb("zT2", [128, ND * S], BF16)
    zTo = mem.sb("zTo", [64, S], BF16)
    zsc = [mem.sb(f"zsc{i}", [65, 512], F32) for i in range(4)]
    denbc = [mem.sb(f"denbc{i}", [64, 512], F32) for i in range(4)]
    denbr = [mem.sb(f"denbr{i}", [64, 512], F32) for i in range(4)]
    ps_ = [mem.ps(f"ps{i}", [128, 1024]) for i in range(2)]
    pzx = mem.ps("pzx", [128, 1024])
    pzb = [pzx[0:65, 0:512], ptre[0:65, 0:512]]
    pnr = pzx[0:1, 512:1024]
    pmx = mem.ps("pmx", [128, 512])
    prb = pmx[0:64, 0:512]

    t_vones = [sc.add("vector", lambda v, i=i: v.memset(
        vx[i][:].rearrange("p (t c) -> p t c", c=65)[:, :, 64:65], 1.0), inc="vector")
        for i in range(2)]

    qkv_hist = []
    vx_free = [None, None]
    prev_exp = None
    pz_free = {0: None, 1: None}
    prb_free = None
    zn_prev = [None, None, None, None]
    t_qkv_last = None
    t_attn_last_te = None
    pmx_free = None
    hp_state = {}
    zto_free = None
    t_zrepack = {}
    wo_tks = []
    wo2 = None

    def qkv_slot(idx):
        return pzx[0:128, 0:512] if idx % 2 == 0 else pmx[0:128, 0:512]

    for hp in range(H // 2):
        for wnm, dst in (("k", kT2), ("q", qT2)):
            for c in range(4):
                pp = qkv_slot(len(qkv_hist))
                gate = qkv_hist[-2] if len(qkv_hist) >= 2 else None
                tmm = None
                for kd in range(ND):
                    tmm = sc.add("tensor", lambda te, pp=pp, wnm=wnm, kd=kd, hp=hp, c=c: te.matmul(
                        pp, wsb[wnm][:, kd * D + hp * 128: kd * D + hp * 128 + 128],
                        xnT[:, kd * S + c * 512: kd * S + (c + 1) * 512],
                        start=(kd == 0), stop=(kd == ND - 1)),
                        waits=(t_xn_done, t_wld[wnm], gate,
                               pz_free[0] if c == 0 and wnm == "k" else None,
                               pz_free[1] if c == 0 and wnm == "k" else None,
                               pmx_free if c == 0 and wnm == "k" else None,
                               hp_state.get("score_last") if c == 0 and wnm == "k" else None),
                        inc="tensor" if kd == ND - 1 else None)
                tev = sc.add("vector", lambda v, pp=pp, c=c, dst=dst: v.tensor_copy(
                    out=dst[:, c * 512:(c + 1) * 512], in_=pp),
                    waits=(tmm,), inc="vector")
                qkv_hist.append(tev)
        for tt in range(NT):
            pp = qkv_slot(len(qkv_hist))
            gate = qkv_hist[-2] if len(qkv_hist) >= 2 else None
            tmm = None
            for kd in range(ND):
                tmm = sc.add("tensor", lambda te, pp=pp, kd=kd, tt=tt, hp=hp: te.matmul(
                    pp[:, 0:128], xnT[:, kd * S + tt * 128: kd * S + (tt + 1) * 128],
                    wsb["v"][:, kd * D + hp * 128: kd * D + hp * 128 + 128],
                    start=(kd == 0), stop=(kd == ND - 1)),
                    waits=(t_xn_done, t_wld["v"], gate,
                           vx_free[0] if tt == 0 else None,
                           vx_free[1] if tt == 0 else None),
                    inc="tensor" if kd == ND - 1 else None)
            tev = None
            for half in range(2):
                tev = sc.add("vector", lambda v, pp=pp, half=half, tt=tt: v.tensor_copy(
                    out=vx[half][:, tt * 65: tt * 65 + 64],
                    in_=pp[:, 64 * half:64 * half + 64]),
                    waits=(tmm, t_vones[half]), inc="vector")
            qkv_hist.append(tev)
        t_qkv_last = tmm
        if hp == H // 2 - 1:
            mem.free("wq", "wk", "wv")
            wo2 = mem.sb("wo2", [128, ND * D], BF16, side="right")
            for kd in range(ND):
                wo_tks.append(sc.dma("gpsimd", "wcast%d" % kd,
                                     wo2[:, kd * D:(kd + 1) * D],
                                     wT_d["o"][kd * 128:(kd + 1) * 128, :],
                                     waits=(qkv_hist[-1],)))

        bias_tok = {}
        pmx_busy = hp_state.get("pmx_busy")
        nsc_reader = hp_state.get("nsc_busy")
        nrm_prev = hp_state.get("nrm_prev")
        nrm_tok = {}
        for j, wch in enumerate(("q", "k")):
            srcT = qT2 if wch == "q" else kT2
            tsq = sc.add("vector", lambda v, srcT=srcT: v.tensor_tensor(
                out=nsc[:], in0=srcT[:], in1=srcT[:], op=ALU.mult),
                waits=(qkv_hist[-1], prev_exp, nsc_reader), inc="vector")
            for half in range(2):
                sl = 8 + 2 * half + j
                tred = None
                for c in range(4):
                    tm = sc.add("tensor", lambda te, c=c, half=half: te.matmul(
                        pnr, onesc[64 * half:64 * half + 64, :],
                        nsc[64 * half:64 * half + 64, c * 512:(c + 1) * 512],
                        start=True, stop=True),
                        waits=(tsq, t_o1f, tred, pmx_busy, hp_state.get("pnr_busy")),
                        inc="tensor")
                    tred = sc.add("vector", lambda v, c=c: v.reduce_max(
                        out=scal[0:1, 30 + c:31 + c], in_=pnr, axis=AX),
                        waits=(tm, nrm_prev), inc="vector")
                nsc_reader = tm
                hp_state["pnr_busy"] = tred
                tok = sc.add("vector", lambda v: v.tensor_tensor(
                    out=scal[0:1, 30:31], in0=scal[0:1, 30:31], in1=scal[0:1, 31:32],
                    op=ALU.max), waits=(tred,), inc="vector")
                tok = sc.add("vector", lambda v: v.tensor_tensor(
                    out=scal[0:1, 32:33], in0=scal[0:1, 32:33], in1=scal[0:1, 33:34],
                    op=ALU.max), waits=(tok,), inc="vector")
                nrm_tok[(wch, half)] = nrm_prev = sc.add(
                    "vector", lambda v, sl=sl: v.tensor_tensor(
                        out=scal[0:1, sl:sl + 1], in0=scal[0:1, 30:31],
                        in1=scal[0:1, 32:33], op=ALU.max), waits=(tok,), inc="vector")
        hp_state["nsc_busy"] = nsc_reader
        hp_state["nrm_prev"] = nrm_prev
        for half in range(2):
            toks = []
            for j, wch in enumerate(("q", "k")):
                sl = 8 + 2 * half + j
                tok = sc.add("tensor", lambda te, sl=sl, half=half, j=j: te.matmul(
                    pmx[:, 2 + 2 * half + j:3 + 2 * half + j], ones1f[:],
                    scal[0:1, sl:sl + 1], start=True, stop=True),
                    waits=(nrm_tok[(wch, half)], t_o1f, prb_free, pmx_busy,
                           qkv_hist[-1]), inc="tensor")
                toks.append(tok)
            tok = sc.add("vector", lambda v, half=half: v.tensor_copy(
                out=scal[:, 34:35], in_=pmx[:, 2 + 2 * half:3 + 2 * half]),
                waits=tuple(toks), inc="vector")
            tok = sc.add("vector", lambda v, half=half: v.tensor_tensor(
                out=scal[:, 16 + half:17 + half], in0=scal[:, 34:35],
                in1=pmx[:, 3 + 2 * half:4 + 2 * half], op=ALU.mult),
                waits=(tok,), inc="vector")
            pmx_busy = tok
            pmx_free = tok
            tok = sc.add("scalar", lambda s2, half=half: s2.activation(
                out=scal[:, 18 + half:19 + half], in_=scal[:, 16 + half:17 + half],
                func=ACT.Sqrt), waits=(tok,), inc="scalar")
            bias_tok[half] = sc.add("vector", lambda v, half=half: v.tensor_scalar(
                out=scal[:, 20 + half:21 + half], in0=scal[:, 18 + half:19 + half],
                scalar1=-0.125, scalar2=None, op0=ALU.mult), waits=(tok,), inc="vector")
        hp_state["pmx_busy"] = pmx_busy

        units = [(qb, half) for qb in range(4) for half in range(2)]
        pending = None

        def emit_av_chunk(st):
            nonlocal t_attn_last_te
            p_hp, p_qb, p_half, p_nkt, p_gates, pos = st[:6]
            b = st[7]
            end = min(pos + 2, p_nkt)
            for kt in range(pos, end):
                tav = sc.add("tensor", lambda te, kt=kt, p_half=p_half, p_nkt=p_nkt, b=b: te.matmul(
                    pzb[b], vx[p_half][:, kt * 65:(kt + 1) * 65],
                    expw[p_half][:, kt * 512:(kt + 1) * 512],
                    start=(kt == 0), stop=(kt == p_nkt - 1)),
                    waits=(p_gates[kt], pz_free[b] if kt == 0 else None),
                    inc="tensor" if kt == p_nkt - 1 else None)
                if kt == p_nkt - 1:
                    t_attn_last_te = tav
                    hp_state["av_last_%d" % p_half] = tav
                    st[6] = tav
            st[5] = end
            return end < p_nkt

        def finish_av(st):
            nonlocal prb_free, zto_free
            while emit_av_chunk(st):
                pass
            p_hp, p_qb, p_half, p_nkt, p_gates, pos, tav, b = st
            ui = hp_state.get("ui", 0)
            hp_state["ui"] = ui + 1
            w = ui % 4
            t_ev = sc.add("vector", lambda v, w=w, b=b: v.tensor_copy(
                out=zsc[w][:], in_=pzb[b]), waits=(tav, zn_prev[w]), inc="vector")
            pz_free[b] = t_ev
            t_dw = sc.dma("gpsimd", "gdwr%d" % w, den_d[w][:, :], zsc[w][64:65, :],
                          waits=(t_ev,))
            t_db = sc.dma("gpsimd", "gdbc%d" % w, denbc[w][:],
                          den_d[w].ap().partition_broadcast(64), waits=(t_dw,))
            t_rc = sc.add("vector", lambda v, w=w: v.reciprocal(
                out=denbr[w][:], in_=denbc[w][:]), waits=(t_db, zn_prev[w]),
                inc="vector")
            if p_half == 0:
                out_ap = zT2[0:64, p_hp * S + p_qb * 512: p_hp * S + (p_qb + 1) * 512]
                t_zn = sc.add("vector", lambda v, out_ap=out_ap, w=w: v.tensor_tensor(
                    out=out_ap, in0=zsc[w][0:64, :], in1=denbr[w][:], op=ALU.mult),
                    waits=(t_rc,), inc="vector")
            else:
                out_ap = zTo[:, p_qb * 512:(p_qb + 1) * 512]
                t_zn = sc.add("vector", lambda v, out_ap=out_ap, w=w: v.tensor_tensor(
                    out=out_ap, in0=zsc[w][0:64, :], in1=denbr[w][:], op=ALU.mult),
                    waits=(t_rc, zto_free), inc="vector")
                if p_qb == 3:
                    zto_free = sc.dma("gpsimd", "gzto",
                                      zT2[64:128, p_hp * S:(p_hp + 1) * S], zTo[:],
                                      waits=(t_zn,))
                    t_zrepack[p_hp] = zto_free
            zn_prev[w] = t_zn
            prb_free = t_zn
            return t_zn

        for (qb, half) in units:
            nkt = 4 * qb + 4
            bias_ap = scal[:, 20 + half:21 + half]
            p0, p1_ = 64 * half, 64 * half + 64
            exp_toks = {}
            gates = {}
            for g in range(nkt // 2):
                tmm2 = None
                for kt in (2 * g, 2 * g + 1):
                    slot = kt % 2
                    tmm2 = sc.add("tensor", lambda te, kt=kt, qb=qb, p0=p0, p1_=p1_, half=half, slot=slot: te.matmul(
                        ps_[half][:, slot * 512:(slot + 1) * 512],
                        kT2[p0:p1_, kt * 128:(kt + 1) * 128],
                        qT2[p0:p1_, qb * 512:(qb + 1) * 512], start=True, stop=True),
                        waits=(exp_toks.get(g - 1), qkv_hist[-1],
                               hp_state.get("av_last_%d" % half) if g == 0 else None),
                        inc="tensor" if kt % 2 == 1 else None)
                if pending is not None:
                    emit_av_chunk(pending)
                texp = sc.add("scalar", lambda s2, g=g, half=half, bias_ap=bias_ap: s2.activation(
                    out=expw[half][:, g * 1024:(g + 1) * 1024], in_=ps_[half][:, 0:1024],
                    func=ACT.Exp, scale=0.125, bias=bias_ap),
                    waits=(tmm2, bias_tok[half]), inc="scalar")
                exp_toks[g] = texp
                for kt in (2 * g, 2 * g + 1):
                    d = kt - 4 * qb
                    if d >= 0:
                        gates[kt] = sc.add("gpsimd", lambda g2, half=half, kt=kt, d=d: g2.tensor_tensor(
                            out=expw[half][:, kt * 512:(kt + 1) * 512],
                            in0=expw[half][:, kt * 512:(kt + 1) * 512],
                            in1=dmask[:, d * 512:(d + 1) * 512], op=ALU.mult),
                            waits=(texp, t_dm), inc="gpsimd")
                    else:
                        gates[kt] = texp
            prev_exp = exp_toks[nkt // 2 - 1]
            hp_state["score_last"] = tmm2
            if pending is not None:
                finish_av(pending)
            pending = [hp, qb, half, nkt, gates, 0, None,
                       (4 * hp + qb * 2 + half) % 2]
        t_zn_last = finish_av(pending)
        vx_free = [t_attn_last_te, t_attn_last_te]
    t_attn_z = t_zn_last

    t_zrp = tuple(t_zrepack[p] for p in range(H // 2))
    wos = [mem.sb(f"wos{i}", [128, D], F32) for i in range(2)]
    wos_free = [None, None]
    wo_sp_fin = [None, None]
    t_wo_sp = None
    for tt in range(NT):
        i = tt % 2
        t_ev = None
        for c in range(2):
            pp = qkv_slot(len(qkv_hist))
            gate = qkv_hist[-2] if len(qkv_hist) >= 2 else None
            tmm = None
            for kd in range(ND):
                tmm = sc.add("tensor", lambda te, pp=pp, kd=kd, tt=tt, c=c: te.matmul(
                    pp, zT2[:, kd * S + tt * 128: kd * S + (tt + 1) * 128],
                    wo2[:, kd * D + c * 512: kd * D + (c + 1) * 512],
                    start=(kd == 0), stop=(kd == ND - 1)),
                    waits=(t_attn_z, wo_tks[kd], gate) + t_zrp,
                    inc="tensor" if kd == ND - 1 else None)
            t_ev = sc.add("scalar", lambda s2, pp=pp, i=i, c=c: s2.activation(
                out=wos[i][:, c * 512:(c + 1) * 512], in_=pp, func=ACT.Copy),
                waits=(tmm, wos_free[i]), inc="scalar")
            qkv_hist.append(t_ev)
        t_wo_sp = sc.dma("sync", f"wos{i}", attn_d[tt * 128:(tt + 1) * 128, :],
                         wos[i][:], waits=(t_ev,))
        wos_free[i] = t_wo_sp
        wo_sp_fin[i] = t_wo_sp

    mem.free("kT2", "qT2", "vx0", "vx1", "nsc", "expw0", "expw1", "zT2", "zTo",
             "zsc0", "zsc1", "zsc2", "zsc3", "denbc0", "denbc1", "denbc2",
             "denbc3", "denbr0", "denbr1", "denbr2", "denbr3", "wo2", "wos0", "wos1",
             "ps0", "ps1", "pzx", "pmx")

    jw2 = (sc.now("vector"), sc.now("scalar"), sc.now("tensor"),
           sc.now("wos0") if "wos0" in sc.cnt else None,
           sc.now("wos1") if "wos1" in sc.cnt else None)
    BUFN = mem.sb("BUF", [128, NI * S], BF16)
    iscr = mem.sb("iscr", [128, I3], I32)
    smx = mem.sb("smx", [128, 128], F32)
    wnat = [mem.sb(f"wnat{i}", [128, D], F32, side="right") for i in range(3)]
    wdqo = [mem.sb(f"wdqo{i}", [128, D], BF16, side="right") for i in range(2)]
    wnatD = [mem.sb("wnatD", [128, I3], F32, side="right")]
    wdqoD = [mem.sb("wdqoD", [128, I3], BF16, side="right")]
    bmx = mem.sb("bmx", [128, 128], F32, side="right")
    rix = mem.sb("rix", [128, 128], F32, side="right")
    wtst = [mem.sb("wtst0", [128, I3], BF16, side="right")]
    ptr2 = [mem.ps(f"ptr2_{i}", [128, 128], BF16) for i in range(2)]

    wnat_free = [None, None, None]
    wdqo_free = [None, None]
    wnatD_free = [None]
    wdqoD_free = [None]
    wtst_free = [None]
    UG_BUFS = (wnat, wdqo, wnat_free, wdqo_free, "wnat")
    DN_BUFS = (wnatD, wdqoD, wnatD_free, wdqoD_free, "wnatD")
    ptr2_free = {}
    prev_fscr = None

    def dequant_tile(idx, src_ap, w, waits=(), bufs=None):
        wnat_, wdqo_, wnat_free_, wdqo_free_, strm = bufs
        in_ = idx % len(wnat_)
        i = idx % len(wdqo_)
        nb = w // 32
        t_l = sc.dma("sync", strm + str(in_), wnat_[in_][:, 0:w], src_ap,
                     waits=(wnat_free_[in_],) + jw2)
        w3 = wnat_[in_][:, 0:w].rearrange("p (n b) -> p n b", b=32)
        t_b = sc.add("vector", lambda v, w3=w3, nb=nb: v.reduce_max(
            out=bmx[:, 96:96 + nb], in_=w3, axis=AX, apply_absolute_value=True),
            waits=(t_l, dq_state["prev_q"]), inc="vector")
        t_s = sc.add("vector", lambda v, nb=nb: v.tensor_scalar(
            out=smx[:, 96:96 + nb], in0=bmx[:, 96:96 + nb], scalar1=1.0 / 7.0,
            scalar2=EPS12, op0=ALU.mult, op1=ALU.max), waits=(t_b,), inc="vector")
        t_r = sc.add("vector", lambda v, nb=nb: v.reciprocal(
            out=rix[:, 96:96 + nb], in_=smx[:, 96:96 + nb]), waits=(t_s,), inc="vector")
        rb = rix[:, 96:96 + nb].unsqueeze(2).broadcast_to([128, nb, 32])
        sb_ = smx[:, 96:96 + nb].unsqueeze(2).broadcast_to([128, nb, 32])
        t_q = sc.add("vector", lambda v, w3=w3, rb=rb, w=w: v.tensor_tensor(
            out=iscr[:, 0:w].rearrange("p (n b) -> p n b", b=32), in0=w3, in1=rb,
            op=ALU.mult), waits=(t_r, dq_state["prev_q"]), inc="vector")
        dq_state["prev_q"] = t_q
        t_w2 = sc.add("vector", lambda v, i=i, sb_=sb_, w=w, wdqo_=wdqo_: v.tensor_tensor(
            out=wdqo_[i][:, 0:w].rearrange("p (n b) -> p n b", b=32),
            in0=iscr[:, 0:w].rearrange("p (n b) -> p n b", b=32), in1=sb_,
            op=ALU.mult), waits=(t_q, wdqo_free_[i]) + tuple(waits), inc="vector")
        wnat_free_[in_] = t_w2
        return i, t_w2

    dq_state = {"prev_q": None}
    t_dq_down = None

    def down_dq_chain(j):
        t_l = sc.dma("sync", "wnatD0", wnatD[0][:], wdown_d[j * 128:(j + 1) * 128, :],
                     waits=(wnatD_free[0],) + jw2)
        w3 = wnatD[0][:].rearrange("p (n b) -> p n b", b=32)
        t_b = sc.add("vector", lambda v, w3=w3: v.reduce_max(
            out=bmx[:, 0:96], in_=w3, axis=AX, apply_absolute_value=True),
            waits=(t_l, dq_state["prev_q"]), inc="vector")
        t_s = sc.add("vector", lambda v: v.tensor_scalar(
            out=smx[:, 0:96], in0=bmx[:, 0:96], scalar1=1.0 / 7.0, scalar2=EPS12,
            op0=ALU.mult, op1=ALU.max), waits=(t_b,), inc="vector")
        t_r = sc.add("vector", lambda v: v.reciprocal(
            out=rix[:, 0:96], in_=smx[:, 0:96]), waits=(t_s,), inc="vector")
        rb = rix[:, 0:96].unsqueeze(2).broadcast_to([128, 96, 32])
        sb_ = smx[:, 0:96].unsqueeze(2).broadcast_to([128, 96, 32])
        t_q = sc.add("vector", lambda v, w3=w3, rb=rb: v.tensor_tensor(
            out=iscr[:, 0:I3].rearrange("p (n b) -> p n b", b=32), in0=w3,
            in1=rb, op=ALU.mult), waits=(t_r, dq_state["prev_q"]), inc="vector")
        dq_state["prev_q"] = t_q
        t_w2 = sc.add("vector", lambda v, sb_=sb_: v.tensor_tensor(
            out=wdqoD[0][:].rearrange("p (n b) -> p n b", b=32),
            in0=iscr[:, 0:I3].rearrange("p (n b) -> p n b", b=32), in1=sb_,
            op=ALU.mult), waits=(t_q, wdqoD_free[0]), inc="vector")
        dq_state["down_w2"] = t_w2

    def down_dq_te(j):
        nonlocal t_dq_down
        t_w2 = dq_state["down_w2"]
        tevT = None
        for k in range(NI):
            pp = ptr2[k % 2]
            tmmT = sc.add("tensor", lambda te, pp=pp, k=k: te.transpose(
                pp[:], wdqoD[0][:, k * 128:(k + 1) * 128], ident[:]),
                waits=(t_w2, t_id, ptr2_free.get(k % 2)), inc="tensor")
            tevT = sc.add("scalar", lambda s2, pp=pp, k=k: s2.activation(
                out=wtst[0][:, k * 128:(k + 1) * 128], in_=pp[:], func=ACT.Copy),
                waits=(tmmT, wtst_free[0] if k == 0 else None), inc="scalar")
            ptr2_free[k % 2] = tevT
        wdqoD_free[0] = tevT
        wnatD_free[0] = tevT
        t_dq_down = sc.dma(
            "sync", "wtsp0",
            wdnT_d[:, j * 128:(j + 1) * 128].rearrange("(t p) c -> p t c", p=128),
            wtst[0][:].rearrange("p (t c) -> p t c", c=128), waits=(tevT,))
        wtst_free[0] = t_dq_down

    t_xq = {}
    prev_xq = None
    for dt in range(ND):
        sl = xnT[:, dt * S:(dt + 1) * S]
        ti = sc.add("vector", lambda v, sl=sl: v.tensor_scalar(
            out=iscr[:, 0:S], in0=sl, scalar1=scal[:, 1:2], scalar2=None,
            op0=ALU.mult), waits=(t_rx, t_qkv_last, prev_xq), inc="vector")
        t_xq[dt] = prev_xq = sc.add("vector", lambda v, sl=sl: v.tensor_scalar(
            out=sl, in0=iscr[:, 0:S], scalar1=scal[:, 0:1], scalar2=None,
            op0=ALU.mult), waits=(ti,), inc="vector")
    t_xq_all = t_xq[ND - 1]

    upst = [mem.sb(f"upst{i}", [128, S], BF16) for i in range(2)]
    wstU = [mem.sb(f"wstU{i}", [128, D], BF16) for i in range(2)]
    wstG = [mem.sb(f"wstG{i}", [128, D], BF16) for i in range(2)]
    pug = [mem.ps(f"pug{i}", [128, 512]) for i in range(2)]

    wst_free = {"up": [None, None], "gate": [None, None]}
    upst_free = [None, None]
    t_mx_prev = None
    deferred_mx = None
    pug_hist = []
    t_upmax = None
    t_gatemax = None
    dq_state["prev_q"] = t_xq_all
    for it in range(NI):
        i = it % 2
        wtoks = {}
        dq2 = {}
        for nmi, (nm, wbuf, srcw_d) in enumerate(
                (("up", wstU, wup_d), ("gate", wstG, wgate_d))):
            dqi, t_w2 = dequant_tile(2 * it + nmi, srcw_d[it * 128:(it + 1) * 128, :],
                                     D, bufs=UG_BUFS)
            dq2[nm] = (dqi, t_w2)
        if it % 3 == 0:
            down_dq_chain(it // 3)
        if it % 3 == 2:
            down_dq_te(it // 3)
        for nmi, (nm, wbuf, srcw_d) in enumerate(
                (("up", wstU, wup_d), ("gate", wstG, wgate_d))):
            dqi, t_w2 = dq2[nm]
            tevT = None
            for kd in range(ND):
                pp = ptr2[kd % 2]
                tmmT = sc.add("tensor", lambda te, pp=pp, dqi=dqi, kd=kd: te.transpose(
                    pp[:], wdqo[dqi][:, kd * 128:(kd + 1) * 128], ident[:]),
                    waits=(t_w2, t_id, ptr2_free.get(kd % 2)), inc="tensor")
                tevT = sc.add("scalar", lambda s2, pp=pp, wbuf=wbuf, i=i, kd=kd: s2.activation(
                    out=wbuf[i][:, kd * 128:(kd + 1) * 128], in_=pp[:], func=ACT.Copy),
                    waits=(tmmT, wst_free[nm][i]), inc="scalar")
                ptr2_free[kd % 2] = tevT
            wdqo_free[dqi] = tevT
            wtoks[nm] = tevT
        for nm, wbuf in (("up", wstU), ("gate", wstG)):
            ev_t = None
            for c in range(4):
                pp = pug[len(pug_hist) % 2]
                gate = pug_hist[-2] if len(pug_hist) >= 2 else None
                tmm = None
                for kd in range(ND):
                    tmm = sc.add("tensor", lambda te, pp=pp, wbuf=wbuf, i=i, kd=kd, c=c: te.matmul(
                        pp[:], wbuf[i][:, kd * 128:(kd + 1) * 128],
                        xnT[:, kd * S + c * 512: kd * S + (c + 1) * 512],
                        start=(kd == 0), stop=(kd == ND - 1)),
                        waits=(wtoks[nm], t_xq[kd], gate),
                        inc="tensor" if kd == ND - 1 else None)
                if nm == "up":
                    out_ap = upst[i][:, c * 512:(c + 1) * 512]
                    ev_t = sc.add("scalar", lambda s2, pp=pp, out_ap=out_ap: s2.activation(
                        out=out_ap, in_=pp[:], func=ACT.Copy),
                        waits=(tmm, upst_free[i] if c == 0 else None),
                        inc="scalar")
                else:
                    out_ap = BUFN[:, it * S + c * 512: it * S + (c + 1) * 512]
                    ev_t = sc.add("scalar", lambda s2, pp=pp, out_ap=out_ap: s2.activation(
                        out=out_ap, in_=pp[:], func=ACT.Copy), waits=(tmm,), inc="scalar")
                pug_hist.append(ev_t)
            wst_free[nm][i] = ev_t
            if deferred_mx is not None:
                (d_src, d_cslot, d_ev) = deferred_mx
                t_red = sc.add("vector", lambda v, d_src=d_src: v.reduce_max(
                    out=cmx[:, 7:8], in_=d_src, axis=AX, apply_absolute_value=True),
                    waits=(d_ev, t_mx_prev), inc="vector")
                t_mx = sc.add("vector", lambda v, d_cslot=d_cslot: v.tensor_tensor(
                    out=cmx[:, d_cslot:d_cslot + 1], in0=cmx[:, d_cslot:d_cslot + 1],
                    in1=cmx[:, 7:8], op=ALU.max), waits=(t_red, t_cmx0), inc="vector")
                t_mx_prev = t_mx
                if d_cslot == 1:
                    t_upmax = t_mx
                else:
                    t_gatemax = t_mx
            src2 = upst[i][:] if nm == "up" else BUFN[:, it * S:(it + 1) * S]
            deferred_mx = (src2, 1 if nm == "up" else 2, ev_t)
            if nm == "up":
                upst_free[i] = sc.dma("sync", f"upst{i}", up_d[it * 128:(it + 1) * 128, :],
                                      upst[i][:], waits=(ev_t,))
    (d_src, d_cslot, d_ev) = deferred_mx
    t_red = sc.add("vector", lambda v, d_src=d_src: v.reduce_max(
        out=cmx[:, 7:8], in_=d_src, axis=AX, apply_absolute_value=True),
        waits=(d_ev, t_mx_prev), inc="vector")
    t_gatemax = sc.add("vector", lambda v, d_cslot=d_cslot: v.tensor_tensor(
        out=cmx[:, d_cslot:d_cslot + 1], in0=cmx[:, d_cslot:d_cslot + 1],
        in1=cmx[:, 7:8], op=ALU.max), waits=(t_red, t_cmx0), inc="vector")
    t_dq_down_all = (wtst_free[0], wtst_free[0])
    mem.free("wstU0", "wstU1", "wstG0", "wstG1")
    mem.free("wtst0", "rix", "bmx", "wdqoD", "wnatD", "wdqo0", "wdqo1",
             "wnat0", "wnat1", "wnat2", "xnT")

    t_u1 = tree(cmx[:, 1:2], ccs[0:1, 16:17], ALU.max, 128, (t_upmax,))
    t_g1 = tree(cmx[:, 2:3], ccs[0:1, 17:18], ALU.max, 128, (t_gatemax, t_u1))
    t_bc1 = allreduce(1, t_g1)
    t_su = sc.add("vector", lambda v: v.tensor_scalar(out=scal[:, 2:3], in0=ccb[:, 16:17],
                  scalar1=1.0 / QMAX, scalar2=EPS12, op0=ALU.mult, op1=ALU.max),
                  waits=(t_bc1,), inc="vector")
    t_ru = sc.add("vector", lambda v: v.reciprocal(out=scal[:, 3:4], in_=scal[:, 2:3]),
                  waits=(t_su,), inc="vector")
    t_sg = sc.add("vector", lambda v: v.tensor_scalar(out=scal[:, 4:5], in0=ccb[:, 17:18],
                  scalar1=1.0 / QMAX, scalar2=EPS12, op0=ALU.mult, op1=ALU.max),
                  waits=(t_bc1,), inc="vector")
    t_rg = sc.add("vector", lambda v: v.reciprocal(out=scal[:, 5:6], in_=scal[:, 4:5]),
                  waits=(t_sg,), inc="vector")

    wdn = mem.sb("wdn", [128, 2 * NI * 512], BF16)
    t_wdn = None
    for dc in range(2):
        for it2 in range(NI):
            t_wdn = sc.dma(
                "sync", "wdn",
                wdn[:, dc * NI * 512 + it2 * 512: dc * NI * 512 + (it2 + 1) * 512],
                wdnT_d[it2 * 128:(it2 + 1) * 128, dc * 512:(dc + 1) * 512],
                waits=t_dq_down_all)
    iA = [mem.sb(f"iA{i}", [128, S], I32) for i in range(2)]
    iB1 = mem.sb("iB0", [128, S], I32)
    iB = [iB1, iB1]
    fsg1 = mem.sb("fsg0", [128, S], F32)
    fsg = [fsg1, fsg1]

    iA_free = [None, None]
    iB_free = [None, None]
    fsg_free = [None, None]
    t_pmax = None
    for it in range(NI):
        i = it % 2
        buf_sl = BUFN[:, it * S:(it + 1) * S]
        tGo = sc.add("scalar", lambda s2, buf_sl=buf_sl, i=i: s2.activation(
            out=iA[i][:], in_=buf_sl, func=ACT.Copy, scale=scal[:, 5:6]),
            waits=(t_rg, iA_free[i], t_dq_down_all[0], t_dq_down_all[1]), inc="scalar")
        tsg_ = sc.add("scalar", lambda s2, i=i: s2.activation(
            out=fsg[i][:], in_=iA[i][:], func=ACT.Sigmoid, scale=scal[:, 4:5]),
            waits=(tGo, fsg_free[0]), inc="scalar")
        tSo = sc.add("scalar", lambda s2, i=i: s2.activation(
            out=iB[i][:], in_=fsg[i][:], func=ACT.Copy, scale=65536.0),
            waits=(tsg_, iB_free[0]), inc="scalar")
        fsg_free[0] = tSo
        tP = sc.add("vector", lambda v, buf_sl=buf_sl, i=i: v.tensor_tensor(
            out=buf_sl, in0=iA[i][:], in1=iB[i][:], op=ALU.mult),
            waits=(tSo,), inc="vector")
        iA_free[i] = tP
        iB_free[0] = tP
        t_red = sc.add("vector", lambda v, buf_sl=buf_sl: v.reduce_max(
            out=cmx[:, 7:8], in_=buf_sl, axis=AX, apply_absolute_value=True),
            waits=(tP,), inc="vector")
        t_pmax = sc.add("vector", lambda v: v.tensor_tensor(
            out=cmx[:, 3:4], in0=cmx[:, 3:4], in1=cmx[:, 7:8], op=ALU.max),
            waits=(t_red,), inc="vector")

    t_p1 = tree(cmx[:, 3:4], ccs[0:1, 32:33], ALU.max, 128, (t_pmax,))
    t_bc2 = allreduce(2, t_p1)
    t_a = sc.add("vector", lambda v: v.tensor_tensor(out=scal[:, 22:23],
                 in0=ccb[:, 32:33], in1=scal[:, 4:5], op=ALU.mult),
                 waits=(t_bc2,), inc="vector")
    t_b2 = sc.add("vector", lambda v: v.tensor_scalar(out=scal[:, 6:7], in0=scal[:, 22:23],
                  scalar1=1.0 / (65536.0 * QMAX), scalar2=EPS12, op0=ALU.mult,
                  op1=ALU.max), waits=(t_a,), inc="vector")
    t_c = sc.add("vector", lambda v: v.reciprocal(out=scal[:, 23:24], in_=scal[:, 6:7]),
                 waits=(t_b2,), inc="vector")
    t_d2 = sc.add("vector", lambda v: v.tensor_scalar(out=scal[:, 24:25], in0=scal[:, 4:5],
                  scalar1=1.0 / 65536.0, scalar2=None, op0=ALU.mult), inc="vector")
    t_r2 = sc.add("vector", lambda v: v.tensor_tensor(out=scal[:, 7:8], in0=scal[:, 24:25],
                  in1=scal[:, 23:24], op=ALU.mult), waits=(t_c, t_d2), inc="vector")

    t_omax = None
    for it in range(NI):
        i = it % 2
        buf_sl = BUFN[:, it * S:(it + 1) * S]
        t_ul = sc.dma("sync", f"upst{i}", upst[i][:], up_d[it * 128:(it + 1) * 128, :],
                      waits=(upst_free[i],))
        tR2 = sc.add("scalar", lambda s2, buf_sl=buf_sl, i=i: s2.activation(
            out=iA[i][:], in_=buf_sl, func=ACT.Copy, scale=scal[:, 7:8]),
            waits=(t_r2, iA_free[i]), inc="scalar")
        tU = sc.add("scalar", lambda s2, i=i: s2.activation(
            out=iB[i][:], in_=upst[i][:], func=ACT.Copy, scale=scal[:, 3:4]),
            waits=(t_ru, t_ul, iB_free[0]), inc="scalar")
        tO = sc.add("vector", lambda v, buf_sl=buf_sl, i=i: v.tensor_tensor(
            out=buf_sl, in0=iA[i][:], in1=iB[i][:], op=ALU.mult),
            waits=(tR2, tU), inc="vector")
        iA_free[i] = tO
        iB_free[0] = tO
        upst_free[i] = tO
        t_red = sc.add("vector", lambda v, buf_sl=buf_sl: v.reduce_max(
            out=cmx[:, 7:8], in_=buf_sl, axis=AX, apply_absolute_value=True),
            waits=(tO,), inc="vector")
        t_omax = sc.add("vector", lambda v: v.tensor_tensor(
            out=cmx[:, 4:5], in0=cmx[:, 4:5], in1=cmx[:, 7:8], op=ALU.max),
            waits=(t_red,), inc="vector")

    t_o1t = tree(cmx[:, 4:5], ccs[0:1, 48:49], ALU.max, 128, (t_omax,))
    t_bc3 = allreduce(3, t_o1t)
    t_e = sc.add("vector", lambda v: v.tensor_tensor(out=scal[:, 25:26], in0=scal[:, 6:7],
                 in1=scal[:, 2:3], op=ALU.mult), waits=(t_b2, t_su), inc="vector")
    t_f = sc.add("vector", lambda v: v.tensor_tensor(out=scal[:, 22:23], in0=ccb[:, 48:49],
                 in1=scal[:, 25:26], op=ALU.mult), waits=(t_bc3, t_e), inc="vector")
    t_g = sc.add("vector", lambda v: v.tensor_scalar(out=scal[:, 26:27], in0=scal[:, 22:23],
                 scalar1=1.0 / QMAX, scalar2=EPS12, op0=ALU.mult, op1=ALU.max),
                 waits=(t_f,), inc="vector")
    t_h = sc.add("vector", lambda v: v.reciprocal(out=scal[:, 27:28], in_=scal[:, 26:27]),
                 waits=(t_g,), inc="vector")
    t_r3 = sc.add("vector", lambda v: v.tensor_tensor(out=scal[:, 28:29], in0=scal[:, 25:26],
                  in1=scal[:, 27:28], op=ALU.mult), waits=(t_h,), inc="vector")

    t_oq = {}
    for it in range(NI):
        i = it % 2
        buf_sl = BUFN[:, it * S:(it + 1) * S]
        tq1 = sc.add("scalar", lambda s2, buf_sl=buf_sl, i=i: s2.activation(
            out=iA[i][:], in_=buf_sl, func=ACT.Copy, scale=scal[:, 28:29]),
            waits=(t_r3, iA_free[i]), inc="scalar")
        t_oq[it] = sc.add("vector", lambda v, buf_sl=buf_sl, i=i: v.tensor_copy(
            out=buf_sl, in_=iA[i][:]), waits=(tq1,), inc="vector")
        iA_free[i] = t_oq[it]

    mem.free("iA0", "iA1", "iB0", "fsg0", "pug0", "pug1")

    jw6 = (sc.now("vector"), sc.now("scalar"))
    xrl = [mem.sb(f"xrl{i}", [128, D], F32) for i in range(2)]
    arl = [mem.sb(f"arl{i}", [128, D], F32) for i in range(2)]
    ost = [mem.sb(f"ost{i}", [128, D], F32) for i in range(2)]
    pdn = [mem.ps(f"pdn{i}", [128, 512]) for i in range(2)]

    xrl_free = [None, None]
    arl_free = [None, None]
    ost_free = [None, None]
    pdn_hist = []
    out_fin = [None, None]
    for tt in range(NT):
        i = tt % 2
        t_xl = sc.dma("sync", f"xrl{i}", xrl[i][:], x_d[tt * 128:(tt + 1) * 128, :],
                      waits=(xrl_free[i],) + jw6)
        t_al = sc.dma("sync", f"arl{i}", arl[i][:], attn_d[tt * 128:(tt + 1) * 128, :],
                      waits=(arl_free[i], wo_sp_fin[0], wo_sp_fin[1]) + jw6)
        for dc in range(2):
            pp = pdn[len(pdn_hist) % 2]
            gate = pdn_hist[-2] if len(pdn_hist) >= 2 else None
            tmm = None
            for it in range(NI):
                tmm = sc.add("tensor", lambda te, pp=pp, it=it, tt=tt, dc=dc: te.matmul(
                    pp[:], BUFN[:, it * S + tt * 128: it * S + (tt + 1) * 128],
                    wdn[:, dc * NI * 512 + it * 512: dc * NI * 512 + (it + 1) * 512],
                    start=(it == 0), stop=(it == NI - 1)),
                    waits=(t_oq[it], t_wdn, gate),
                    inc="tensor" if it == NI - 1 else None)
            t_ev = sc.add("scalar", lambda s2, pp=pp, i=i, dc=dc: s2.activation(
                out=ost[i][:, dc * 512:(dc + 1) * 512], in_=pp[:], func=ACT.Copy,
                scale=scal[:, 26:27]),
                waits=(tmm, t_g, ost_free[i] if dc == 0 else None), inc="scalar")
            pdn_hist.append(t_ev)
            t_a1 = sc.add("vector", lambda v, i=i, dc=dc: v.tensor_tensor(
                out=ost[i][:, dc * 512:(dc + 1) * 512],
                in0=ost[i][:, dc * 512:(dc + 1) * 512],
                in1=xrl[i][:, dc * 512:(dc + 1) * 512], op=ALU.add),
                waits=(t_ev, t_xl), inc="vector")
            t_a2 = sc.add("vector", lambda v, i=i, dc=dc: v.tensor_tensor(
                out=ost[i][:, dc * 512:(dc + 1) * 512],
                in0=ost[i][:, dc * 512:(dc + 1) * 512],
                in1=arl[i][:, dc * 512:(dc + 1) * 512], op=ALU.add),
                waits=(t_a1, t_al), inc="vector")
        t_st = sc.dma("sync", f"ost{i}", out_d[tt * 128:(tt + 1) * 128, :], ost[i][:],
                      waits=(t_a2,))
        xrl_free[i] = t_st
        arl_free[i] = t_st
        ost_free[i] = t_st
        out_fin[i] = t_st

    sc.add("gpsimd", fn=None, waits=(out_fin[0], out_fin[1]))

    with nc.Block() as block:
        sc.emit(block)
    es.close()
    return nc


_NC_CACHE = [None]


def kernel(L0_attn_x, L0_attn_W_K, L0_attn_W_Q, L0_attn_W_V, L0_attn_W_O,
           L0_ffn_w_gate, L0_ffn_w_up, L0_ffn_w_down):
    f = np.float32
    x = np.ascontiguousarray(np.asarray(L0_attn_x, f))
    wq = np.ascontiguousarray(np.asarray(L0_attn_W_Q, f).transpose(2, 0, 1).reshape(D, D))
    wk = np.ascontiguousarray(np.asarray(L0_attn_W_K, f).transpose(2, 0, 1).reshape(D, D))
    wv = np.ascontiguousarray(np.asarray(L0_attn_W_V, f).transpose(2, 0, 1).reshape(D, D))
    wo = np.ascontiguousarray(np.asarray(L0_attn_W_O, f).T)
    wupn = np.ascontiguousarray(np.asarray(L0_ffn_w_up, f))
    wgn = np.ascontiguousarray(np.asarray(L0_ffn_w_gate, f))
    wdn_ = np.ascontiguousarray(np.asarray(L0_ffn_w_down, f))
    ident = np.eye(128, dtype=BF)
    identf = np.eye(128, dtype=f)
    kk = np.arange(128)[:, None]
    dmsk = np.concatenate(
        [((d * 128 + kk) <= np.arange(512)[None, :]) for d in range(4)],
        axis=1).astype(BF)
    ones1 = np.ones((1, 128), dtype=BF)
    ones1f = np.ones((1, 128), dtype=f)
    onesc = np.ones((128, 1), dtype=BF)

    if _NC_CACHE[0] is None:
        _NC_CACHE[0] = build_nc()
    nc = _NC_CACHE[0]
    in_maps = []
    for c in range(NCORES):
        in_maps.append({"x": np.ascontiguousarray(x[c]), "wqT": wq, "wkT": wk,
                        "wvT": wv, "woT": wo, "wup": wupn, "wgate": wgn,
                        "wdown": wdn_, "ident": ident, "dmask": dmsk,
                        "ones1": ones1, "ones1f": ones1f, "onesc": onesc, "identf": identf})
    res = run_bass_kernel_spmd(nc, in_maps, core_ids=list(range(NCORES)))
    return np.stack([res.results[c]["out"] for c in range(NCORES)], axis=0)


# version 17
# speedup vs baseline: 1.2382x; 1.0602x over previous
import numpy as np
import ml_dtypes
import concourse.bass as bass
import concourse.mybir as mybir
from concourse.bass_utils import run_bass_kernel_spmd

F32 = mybir.dt.float32
BF16 = mybir.dt.bfloat16
I32 = mybir.dt.int32
AX = mybir.AxisListType.X
ALU = mybir.AluOpType
ACT = mybir.ActivationFunctionType
BF = ml_dtypes.bfloat16

S, D, H, DH, I3 = 2048, 1024, 16, 64, 3072
NT, ND, NI = S // 128, D // 128, I3 // 128
NCORES = 8
QMAX = 32767.0
EPS12 = 1e-12

ENGS = ("tensor", "vector", "scalar", "gpsimd", "sync")


class Sched:

    def __init__(self, nc, es):
        self.nc = nc
        self.es = es
        self.steps = {e: [] for e in ENGS}
        names = list(ENGS) + ["cc"]
        self.sems = {n: es.enter_context(nc.semaphore("sem_" + n)) for n in names}
        self.cnt = {n: 0 for n in names}
        self.wm = {}

    def now(self, name):
        return (name, self.cnt[name])

    def stream(self, name):
        if name not in self.sems:
            self.sems[name] = self.es.enter_context(self.nc.semaphore("sem_" + name))
            self.cnt[name] = 0
        return name

    def add(self, eng, fn=None, waits=(), inc=None, n=1):
        todo = []
        for t in waits:
            if t is None:
                continue
            s, v = t
            key = (eng, s)
            if self.wm.get(key, 0) < v:
                self.wm[key] = v
                todo.append((self.sems[s], v))
        tok, sem = None, None
        if inc is not None:
            self.cnt[inc] += n
            tok = (inc, self.cnt[inc])
            sem = self.sems[inc]
        amt = n

        def thunk(obj, todo=todo, fn=fn, sem=sem, amt=amt):
            for (sm, v) in todo:
                obj.wait_ge(sm, v)
            ins = fn(obj) if fn is not None else None
            if sem is not None:
                ins.then_inc(sem, amt)

        self.steps[eng].append(thunk)
        return tok

    def dma(self, qeng, stream, out, in_, waits=()):
        return self.add(qeng, lambda o, out=out, in_=in_: o.dma_start(out=out, in_=in_),
                        waits=waits, inc=self.stream(stream), n=16)

    def dmat(self, stream, out, in_, waits=()):
        return self.add("sync",
                        lambda o, out=out, in_=in_: o.dma_start(out=out, in_=in_, transpose=True),
                        waits=waits, inc=self.stream(stream), n=16)

    def emit(self, block):
        for e in ENGS:
            thunks = self.steps[e]

            def body(obj, thunks=thunks):
                for t in thunks:
                    t(obj)
            getattr(block, e)(body)


class Mem:
    def __init__(self, nc):
        self.nc = nc
        self.guards = {}
        self.order = {}
        self.n = 0

    def _reg(self, name, g):
        t = g.__enter__()
        self.guards[name] = g
        self.order[name] = self.n
        self.n += 1
        return t

    def sb(self, name, shape, dt, side=None):
        return self._reg(name, self.nc.sbuf_tensor(name, shape, dt, side=side))

    def ps(self, name, shape, dt=F32):
        return self._reg(name, self.nc.psum_tensor(name, shape, dt))

    def free(self, *names):
        for n in sorted(names, key=lambda n: self.order[n], reverse=True):
            self.guards.pop(n).__exit__(None, None, None)


def build_nc():
    nc = bass.Bass("TRN2", target_bir_lowering=False)

    x_d = nc.dram_tensor("x", [S, D], F32, kind="ExternalInput")
    wT_d = {n: nc.dram_tensor("w" + n + "T", [D, D], F32, kind="ExternalInput")
            for n in ("q", "k", "v", "o")}
    wup_d = nc.dram_tensor("wup", [I3, D], F32, kind="ExternalInput")
    wgate_d = nc.dram_tensor("wgate", [I3, D], F32, kind="ExternalInput")
    wdown_d = nc.dram_tensor("wdown", [D, I3], F32, kind="ExternalInput")
    ident_d = nc.dram_tensor("ident", [128, 128], BF16, kind="ExternalInput")
    identf_d = nc.dram_tensor("identf", [128, 128], F32, kind="ExternalInput")
    dmask_d = nc.dram_tensor("dmask", [128, 2048], BF16, kind="ExternalInput")
    ones1_d = nc.dram_tensor("ones1", [1, 128], BF16, kind="ExternalInput")
    ones1f_d = nc.dram_tensor("ones1f", [1, 128], F32, kind="ExternalInput")
    onesc_d = nc.dram_tensor("onesc", [128, 1], BF16, kind="ExternalInput")
    out_d = nc.dram_tensor("out", [S, D], F32, kind="ExternalOutput")
    cc_in = [nc.dram_tensor(f"cc_in{r}", [1, 16], F32) for r in range(4)]
    cc_out = [nc.dram_tensor(f"cc_out{r}", [1, 16], F32) for r in range(4)]
    wdnT_d = nc.dram_tensor("wdnT", [I3, D], BF16)
    up_d = nc.dram_tensor("upsp", [I3, S], BF16)
    attn_d = nc.dram_tensor("attnsp", [S, D], F32)
    den_d = [nc.dram_tensor(f"densp{i}", [1, 512], F32) for i in range(4)]

    from contextlib import ExitStack
    es = ExitStack()
    sc = Sched(nc, es)
    mem = Mem(nc)

    ident = mem.sb("s_ident", [128, 128], BF16)
    identf = mem.sb("s_identf", [128, 128], F32)
    colws = mem.sb("s_colws", [128, 128], F32)
    ptre = mem.ps("s_ptre", [128, 512], F32)
    dmask = mem.sb("s_dmask", [128, 2048], BF16)
    ones1 = mem.sb("s_ones1", [1, 128], BF16)
    ones1f = mem.sb("s_ones1f", [1, 128], F32)
    onesc = mem.sb("s_onesc", [128, 1], BF16)
    trs = mem.sb("s_trs", [32, 32], F32)
    trs2 = mem.sb("s_trs2", [32, 32], F32)
    scal = mem.sb("scal", [128, 64], F32)
    ccs = mem.sb("ccs", [1, 64], F32)
    ccb = mem.sb("ccb", [128, 64], F32)
    cmx = mem.sb("cmx", [128, 8], F32)

    sc.dma("sync", "const", ident[:], ident_d[:, :])
    sc.dma("sync", "const", identf[:], identf_d[:, :])
    sc.dma("sync", "const", dmask[:], dmask_d[:, :])
    sc.dma("sync", "const", ones1[:], ones1_d[:, :])
    sc.dma("sync", "const", ones1f[:], ones1f_d[:, :])
    t_const = sc.dma("sync", "const", onesc[:], onesc_d[:, :])
    t_id = t_dm = t_o1 = t_o1f = t_const
    t_cmx0 = sc.add("vector", lambda v: v.memset(cmx[:], 0.0), inc="vector")
    t_trs0 = sc.add("vector", lambda v: v.memset(trs[:], 0.0), inc="vector")
    t_ccs0 = sc.add("vector", lambda v: v.memset(ccs[:], 0.0), inc="vector")
    t_cw0 = sc.add("vector", lambda v: v.memset(colws[:], 0.0), inc="vector")

    tree_busy = [None]

    def tree(col_ap, out_ap, op, p0, waits):
        t1 = sc.add("vector", lambda v, col_ap=col_ap, p0=p0: v.tensor_copy(
            out=colws[0:p0, 0:1], in_=col_ap),
            waits=tuple(waits) + (tree_busy[0], t_cw0), inc="vector")
        t2 = sc.add("tensor", lambda te: te.transpose(ptre[:, 0:128], colws[:], identf[:]),
                    waits=(t1, t_id, tree_busy[0]), inc="tensor")
        tok = sc.add("vector", lambda v, out_ap=out_ap, op=op, p0=p0: v.tensor_reduce(
            out=out_ap, in_=ptre[0:1, 0:p0], axis=AX, op=op), waits=(t2,), inc="vector")
        tree_busy[0] = tok
        return tok

    def allreduce(r, t_staged):
        t_w = sc.dma("gpsimd", "gcc", cc_in[r][:, :], ccs[0:1, 16 * r:16 * r + 16],
                     waits=(t_staged, t_ccs0))
        t_ar = sc.add("gpsimd", lambda g, r=r: g.collective_compute(
            "AllReduce", ALU.max, replica_groups=[list(range(NCORES))],
            ins=[cc_in[r].ap().opt()], outs=[cc_out[r].ap().opt()]),
            waits=(t_w,), inc="cc")
        return sc.dma("sync", "ccb", ccb[:, 16 * r:16 * r + 16],
                      cc_out[r].ap().partition_broadcast(128), waits=(t_ar,))

    xnT = mem.sb("xnT", [128, ND * S], BF16, side="right")
    xst = [mem.sb(f"xst{i}", [128, D], F32) for i in range(2)]
    xnst = [mem.sb(f"xnst{i}", [128, D], BF16) for i in range(2)]
    lns = mem.sb("lns", [128, 64], F32)
    sqd = mem.sb("sqd", [128, D], F32)
    ptr = [mem.ps(f"ptr{i}", [128, 128], BF16) for i in range(2)]

    xst_free = [None, None]
    xnst_free = [None, None]
    ptr_free = {}
    t_xn_done = None
    t_xnmax = None
    t_sums = {}
    for tt in range(NT):
        i = tt % 2
        t_ld = sc.dma("sync", f"xst{i}", xst[i][:], x_d[tt * 128:(tt + 1) * 128, :],
                      waits=(xst_free[i],))
        x_ap = xst[i][:]
        t1 = sc.add("vector", lambda v, x_ap=x_ap, tt=tt: v.reduce_sum(
            out=lns[:, tt:tt + 1], in_=x_ap, axis=AX), waits=(t_ld,), inc="vector")
        t2 = sc.add("vector", lambda v, x_ap=x_ap, tt=tt: v.scalar_tensor_tensor(
            out=sqd[:], in0=x_ap, scalar=1.0, in1=x_ap, op0=ALU.mult, op1=ALU.mult,
            accum_out=lns[:, 16 + tt:17 + tt]), waits=(t1,), inc="vector")
        t_sums[tt] = t2
        xst_free[tt % 2] = t2
    tb = sc.add("vector", lambda v: v.tensor_scalar(
        out=lns[:, 0:16], in0=lns[:, 0:16], scalar1=1.0 / D, scalar2=None,
        op0=ALU.mult), waits=(t_sums[NT - 1],), inc="vector")
    tb = sc.add("vector", lambda v: v.tensor_scalar(
        out=lns[:, 16:32], in0=lns[:, 16:32], scalar1=1.0 / D, scalar2=None,
        op0=ALU.mult), waits=(tb,), inc="vector")
    tb = sc.add("vector", lambda v: v.scalar_tensor_tensor(
        out=lns[:, 32:48], in0=lns[:, 0:16], scalar=1.0, in1=lns[:, 0:16],
        op0=ALU.mult, op1=ALU.mult), waits=(tb,), inc="vector")
    tb = sc.add("vector", lambda v: v.tensor_tensor(
        out=lns[:, 32:48], in0=lns[:, 16:32], in1=lns[:, 32:48],
        op=ALU.subtract), waits=(tb,), inc="vector")
    tb = sc.add("vector", lambda v: v.tensor_scalar(
        out=lns[:, 32:48], in0=lns[:, 32:48], scalar1=1e-5, scalar2=None,
        op0=ALU.add), waits=(tb,), inc="vector")
    tb = sc.add("scalar", lambda s: s.activation(
        out=lns[:, 48:64], in_=lns[:, 32:48], func=ACT.Sqrt), waits=(tb,), inc="scalar")
    tb = sc.add("vector", lambda v: v.reciprocal(
        out=lns[:, 32:48], in_=lns[:, 48:64]), waits=(tb,), inc="vector")
    t_stats = sc.add("vector", lambda v: v.scalar_tensor_tensor(
        out=lns[:, 48:64], in0=lns[:, 0:16], scalar=-1.0, in1=lns[:, 32:48],
        op0=ALU.mult, op1=ALU.mult), waits=(tb,), inc="vector")
    for tt in range(NT):
        i = tt % 2
        t_ld = sc.dma("sync", f"xst{i}", xst[i][:], x_d[tt * 128:(tt + 1) * 128, :],
                      waits=(xst_free[i],))
        x_ap = xst[i][:]
        n_ap = xnst[i][:]
        t7 = sc.add("vector", lambda v, x_ap=x_ap, n_ap=n_ap, tt=tt: v.tensor_scalar(
            out=n_ap, in0=x_ap, scalar1=lns[:, 32 + tt:33 + tt],
            scalar2=lns[:, 48 + tt:49 + tt], op0=ALU.mult, op1=ALU.add),
            waits=(t_ld, t_stats, xnst_free[i]), inc="vector")
        xst_free[i] = t7
        t8 = sc.add("vector", lambda v, n_ap=n_ap: v.reduce_max(
            out=lns[:, 0:1], in_=n_ap, axis=AX, apply_absolute_value=True),
            waits=(t7, t_cmx0), inc="vector")
        t_xnmax = sc.add("vector", lambda v: v.tensor_tensor(
            out=cmx[:, 0:1], in0=cmx[:, 0:1], in1=lns[:, 0:1], op=ALU.max),
            waits=(t8,), inc="vector")
        last_tr = None
        for dt in range(ND):
            pp = ptr[dt % 2]
            tmm = sc.add("tensor", lambda te, n_ap=n_ap, dt=dt, pp=pp: te.transpose(
                pp[:], n_ap[:, dt * 128:(dt + 1) * 128], ident[:]),
                waits=(t7, t_id, ptr_free.get(dt % 2)), inc="tensor")
            tev = sc.add("vector", lambda v, pp=pp, dt=dt, tt=tt: v.tensor_copy(
                out=xnT[:, dt * S + tt * 128: dt * S + (tt + 1) * 128], in_=pp[:]),
                waits=(tmm,), inc="vector")
            ptr_free[dt % 2] = tev
            last_tr = tmm
        xnst_free[i] = last_tr
        t_xn_done = tev
    mem.free("xst0", "xst1", "xnst0", "xnst1", "lns", "sqd", "ptr0", "ptr1")

    t_st0 = tree(cmx[:, 0:1], ccs[0:1, 0:1], ALU.max, 128, (t_xnmax,))
    t_bc0 = allreduce(0, t_st0)
    t_sx = sc.add("vector", lambda v: v.tensor_scalar(out=scal[:, 0:1], in0=ccb[:, 0:1],
                  scalar1=1.0 / QMAX, scalar2=EPS12, op0=ALU.mult, op1=ALU.max),
                  waits=(t_bc0,), inc="vector")
    t_rx = sc.add("vector", lambda v: v.reciprocal(out=scal[:, 1:2], in_=scal[:, 0:1]),
                  waits=(t_sx,), inc="vector")

    jw1 = (sc.now("vector"), sc.now("scalar"), sc.now("tensor"))
    wsb = {n: mem.sb("w" + n, [128, ND * D], BF16, side="right") for n in ("q", "k", "v")}
    t_wld = {}
    for n in ("q", "k", "v"):
        tk = None
        for kd in range(ND):
            tk = sc.dma("gpsimd", "wcast", wsb[n][:, kd * D:(kd + 1) * D],
                        wT_d[n][kd * 128:(kd + 1) * 128, :], waits=jw1)
        t_wld[n] = tk
    for n in ("q", "k", "v"):
        t_wld[n] = tk

    kT2 = mem.sb("kT2", [128, S], BF16)
    qT2 = mem.sb("qT2", [128, S], BF16)
    vx = [mem.sb(f"vx{i}", [128, NT * 65], BF16) for i in range(2)]
    nsc = mem.sb("nsc", [128, S], BF16)
    expw = [mem.sb(f"expw{i}", [128, NT * 512], BF16) for i in range(2)]
    zT2 = mem.sb("zT2", [128, ND * S], BF16)
    zTo = mem.sb("zTo", [64, S], BF16)
    zsc = [mem.sb(f"zsc{i}", [65, 512], F32) for i in range(4)]
    denbc = [mem.sb(f"denbc{i}", [64, 512], F32) for i in range(4)]
    denbr = [mem.sb(f"denbr{i}", [64, 512], F32) for i in range(4)]
    ps_ = [mem.ps(f"ps{i}", [128, 1024]) for i in range(2)]
    pzx = mem.ps("pzx", [128, 1024])
    pzb = [pzx[0:65, 0:512], ptre[0:65, 0:512]]
    pnr = pzx[0:1, 512:1024]
    pmx = mem.ps("pmx", [128, 512])
    prb = pmx[0:64, 0:512]

    t_vones = [sc.add("vector", lambda v, i=i: v.memset(
        vx[i][:].rearrange("p (t c) -> p t c", c=65)[:, :, 64:65], 1.0), inc="vector")
        for i in range(2)]

    qkv_hist = []
    vx_free = [None, None]
    prev_exp = None
    pz_free = {0: None, 1: None}
    prb_free = None
    zn_prev = [None, None, None, None]
    t_qkv_last = None
    t_attn_last_te = None
    pmx_free = None
    hp_state = {}
    zto_free = None
    t_zrepack = {}
    wo_tks = []
    wo2 = None
    exp_hist = {}
    glb_g = [0]

    def qkv_slot(idx):
        r = idx % 4
        if r == 0:
            return ps_[0][0:128, 0:512]
        if r == 1:
            return ps_[1][0:128, 0:512]
        if r == 2:
            return pmx[0:128, 0:512]
        return pzx[0:128, 0:512]

    for hp in range(H // 2):
        qi0 = len(qkv_hist)
        for wnm, dst in (("k", kT2), ("q", qT2)):
            for c in range(4):
                qi = len(qkv_hist) - qi0
                pp = qkv_slot(len(qkv_hist))
                gate = qkv_hist[-4] if len(qkv_hist) >= 4 else None
                tmm = None
                for kd in range(ND):
                    tmm = sc.add("tensor", lambda te, pp=pp, wnm=wnm, kd=kd, hp=hp, c=c: te.matmul(
                        pp, wsb[wnm][:, kd * D + hp * 128: kd * D + hp * 128 + 128],
                        xnT[:, kd * S + c * 512: kd * S + (c + 1) * 512],
                        start=(kd == 0), stop=(kd == ND - 1)),
                        waits=(t_xn_done, t_wld[wnm], gate,
                               exp_hist.get(glb_g[0] - 1) if qi == 0 else None,
                               exp_hist.get(glb_g[0] - 2) if qi == 0 else None,
                               pmx_free if qi == 2 else None,
                               pz_free[0] if qi == 3 else None,
                               pz_free[1] if qi == 3 else None,
                               hp_state.get("score_last") if qi == 0 else None),
                        inc="tensor" if kd == ND - 1 else None)
                tev = sc.add("vector", lambda v, pp=pp, c=c, dst=dst: v.tensor_copy(
                    out=dst[:, c * 512:(c + 1) * 512], in_=pp),
                    waits=(tmm,), inc="vector")
                qkv_hist.append(tev)
        for tt in range(NT):
            pp = qkv_slot(len(qkv_hist))
            gate = qkv_hist[-4] if len(qkv_hist) >= 4 else None
            tmm = None
            for kd in range(ND):
                tmm = sc.add("tensor", lambda te, pp=pp, kd=kd, tt=tt, hp=hp: te.matmul(
                    pp[:, 0:128], xnT[:, kd * S + tt * 128: kd * S + (tt + 1) * 128],
                    wsb["v"][:, kd * D + hp * 128: kd * D + hp * 128 + 128],
                    start=(kd == 0), stop=(kd == ND - 1)),
                    waits=(t_xn_done, t_wld["v"], gate,
                           vx_free[0] if tt == 0 else None,
                           vx_free[1] if tt == 0 else None),
                    inc="tensor" if kd == ND - 1 else None)
            tev = None
            for half in range(2):
                tev = sc.add("vector", lambda v, pp=pp, half=half, tt=tt: v.tensor_copy(
                    out=vx[half][:, tt * 65: tt * 65 + 64],
                    in_=pp[:, 64 * half:64 * half + 64]),
                    waits=(tmm, t_vones[half]), inc="vector")
            qkv_hist.append(tev)
        t_qkv_last = tmm
        if hp == H // 2 - 1:
            mem.free("wq", "wk", "wv")
            wo2 = mem.sb("wo2", [128, ND * D], BF16, side="right")
            for kd in range(ND):
                wo_tks.append(sc.dma("gpsimd", "wcast%d" % kd,
                                     wo2[:, kd * D:(kd + 1) * D],
                                     wT_d["o"][kd * 128:(kd + 1) * 128, :],
                                     waits=(qkv_hist[-1],)))

        bias_tok = {}
        pmx_busy = hp_state.get("pmx_busy")
        nsc_reader = hp_state.get("nsc_busy")
        nrm_prev = hp_state.get("nrm_prev")
        nrm_tok = {}
        for j, wch in enumerate(("q", "k")):
            srcT = qT2 if wch == "q" else kT2
            tsq = sc.add("vector", lambda v, srcT=srcT: v.tensor_tensor(
                out=nsc[:], in0=srcT[:], in1=srcT[:], op=ALU.mult),
                waits=(qkv_hist[-1], prev_exp, nsc_reader), inc="vector")
            for half in range(2):
                sl = 8 + 2 * half + j
                tred = None
                for c in range(4):
                    tm = sc.add("tensor", lambda te, c=c, half=half: te.matmul(
                        pnr, onesc[64 * half:64 * half + 64, :],
                        nsc[64 * half:64 * half + 64, c * 512:(c + 1) * 512],
                        start=True, stop=True),
                        waits=(tsq, t_o1f, tred, pmx_busy, hp_state.get("pnr_busy")),
                        inc="tensor")
                    tred = sc.add("vector", lambda v, c=c: v.reduce_max(
                        out=scal[0:1, 30 + c:31 + c], in_=pnr, axis=AX),
                        waits=(tm, nrm_prev), inc="vector")
                nsc_reader = tm
                hp_state["pnr_busy"] = tred
                tok = sc.add("vector", lambda v: v.tensor_tensor(
                    out=scal[0:1, 30:31], in0=scal[0:1, 30:31], in1=scal[0:1, 31:32],
                    op=ALU.max), waits=(tred,), inc="vector")
                tok = sc.add("vector", lambda v: v.tensor_tensor(
                    out=scal[0:1, 32:33], in0=scal[0:1, 32:33], in1=scal[0:1, 33:34],
                    op=ALU.max), waits=(tok,), inc="vector")
                nrm_tok[(wch, half)] = nrm_prev = sc.add(
                    "vector", lambda v, sl=sl: v.tensor_tensor(
                        out=scal[0:1, sl:sl + 1], in0=scal[0:1, 30:31],
                        in1=scal[0:1, 32:33], op=ALU.max), waits=(tok,), inc="vector")
        hp_state["nsc_busy"] = nsc_reader
        hp_state["nrm_prev"] = nrm_prev
        for half in range(2):
            toks = []
            for j, wch in enumerate(("q", "k")):
                sl = 8 + 2 * half + j
                tok = sc.add("tensor", lambda te, sl=sl, half=half, j=j: te.matmul(
                    pmx[:, 2 + 2 * half + j:3 + 2 * half + j], ones1f[:],
                    scal[0:1, sl:sl + 1], start=True, stop=True),
                    waits=(nrm_tok[(wch, half)], t_o1f, prb_free, pmx_busy,
                           qkv_hist[-1]), inc="tensor")
                toks.append(tok)
            tok = sc.add("vector", lambda v, half=half: v.tensor_copy(
                out=scal[:, 34:35], in_=pmx[:, 2 + 2 * half:3 + 2 * half]),
                waits=tuple(toks), inc="vector")
            tok = sc.add("vector", lambda v, half=half: v.tensor_tensor(
                out=scal[:, 16 + half:17 + half], in0=scal[:, 34:35],
                in1=pmx[:, 3 + 2 * half:4 + 2 * half], op=ALU.mult),
                waits=(tok,), inc="vector")
            pmx_busy = tok
            pmx_free = tok
            tok = sc.add("scalar", lambda s2, half=half: s2.activation(
                out=scal[:, 18 + half:19 + half], in_=scal[:, 16 + half:17 + half],
                func=ACT.Sqrt), waits=(tok,), inc="scalar")
            bias_tok[half] = sc.add("vector", lambda v, half=half: v.tensor_scalar(
                out=scal[:, 20 + half:21 + half], in0=scal[:, 18 + half:19 + half],
                scalar1=-0.125, scalar2=None, op0=ALU.mult), waits=(tok,), inc="vector")
        hp_state["pmx_busy"] = pmx_busy

        units = [(qb, half) for qb in range(4) for half in range(2)]
        pending = None

        def emit_av_chunk(st):
            nonlocal t_attn_last_te
            p_hp, p_qb, p_half, p_nkt, p_gates, pos = st[:6]
            b = st[7]
            end = min(pos + 2, p_nkt)
            for kt in range(pos, end):
                tav = sc.add("tensor", lambda te, kt=kt, p_half=p_half, p_nkt=p_nkt, b=b: te.matmul(
                    pzb[b], vx[p_half][:, kt * 65:(kt + 1) * 65],
                    expw[p_half][:, kt * 512:(kt + 1) * 512],
                    start=(kt == 0), stop=(kt == p_nkt - 1)),
                    waits=(p_gates[kt], pz_free[b] if kt == 0 else None),
                    inc="tensor" if kt == p_nkt - 1 else None)
                if kt == p_nkt - 1:
                    t_attn_last_te = tav
                    hp_state["av_last_%d" % p_half] = tav
                    st[6] = tav
            st[5] = end
            return end < p_nkt

        def finish_av(st):
            nonlocal prb_free, zto_free
            while emit_av_chunk(st):
                pass
            p_hp, p_qb, p_half, p_nkt, p_gates, pos, tav, b = st
            ui = hp_state.get("ui", 0)
            hp_state["ui"] = ui + 1
            w = ui % 4
            t_ev = sc.add("vector", lambda v, w=w, b=b: v.tensor_copy(
                out=zsc[w][:], in_=pzb[b]), waits=(tav, zn_prev[w]), inc="vector")
            pz_free[b] = t_ev
            t_dw = sc.dma("gpsimd", "gdwr%d" % w, den_d[w][:, :], zsc[w][64:65, :],
                          waits=(t_ev,))
            t_db = sc.dma("gpsimd", "gdbc%d" % w, denbc[w][:],
                          den_d[w].ap().partition_broadcast(64), waits=(t_dw,))
            t_rc = sc.add("vector", lambda v, w=w: v.reciprocal(
                out=denbr[w][:], in_=denbc[w][:]), waits=(t_db, zn_prev[w]),
                inc="vector")
            if p_half == 0:
                out_ap = zT2[0:64, p_hp * S + p_qb * 512: p_hp * S + (p_qb + 1) * 512]
                t_zn = sc.add("vector", lambda v, out_ap=out_ap, w=w: v.tensor_tensor(
                    out=out_ap, in0=zsc[w][0:64, :], in1=denbr[w][:], op=ALU.mult),
                    waits=(t_rc,), inc="vector")
            else:
                out_ap = zTo[:, p_qb * 512:(p_qb + 1) * 512]
                t_zn = sc.add("vector", lambda v, out_ap=out_ap, w=w: v.tensor_tensor(
                    out=out_ap, in0=zsc[w][0:64, :], in1=denbr[w][:], op=ALU.mult),
                    waits=(t_rc, zto_free), inc="vector")
                if p_qb == 3:
                    zto_free = sc.dma("gpsimd", "gzto",
                                      zT2[64:128, p_hp * S:(p_hp + 1) * S], zTo[:],
                                      waits=(t_zn,))
                    t_zrepack[p_hp] = zto_free
            zn_prev[w] = t_zn
            prb_free = t_zn
            return t_zn

        for (qb, half) in units:
            nkt = 4 * qb + 4
            bias_ap = scal[:, 20 + half:21 + half]
            p0, p1_ = 64 * half, 64 * half + 64
            exp_toks = {}
            gates = {}
            for g in range(nkt // 2):
                gg = glb_g[0]
                pair = gg % 2
                tmm2 = None
                for kt in (2 * g, 2 * g + 1):
                    slot = kt % 2
                    tmm2 = sc.add("tensor", lambda te, kt=kt, qb=qb, p0=p0, p1_=p1_, pair=pair, slot=slot: te.matmul(
                        ps_[pair][:, slot * 512:(slot + 1) * 512],
                        kT2[p0:p1_, kt * 128:(kt + 1) * 128],
                        qT2[p0:p1_, qb * 512:(qb + 1) * 512], start=True, stop=True),
                        waits=(exp_hist.get(gg - 2), qkv_hist[-1]),
                        inc="tensor" if kt % 2 == 1 else None)
                if pending is not None:
                    emit_av_chunk(pending)
                texp = sc.add("scalar", lambda s2, g=g, half=half, pair=pair, bias_ap=bias_ap: s2.activation(
                    out=expw[half][:, g * 1024:(g + 1) * 1024], in_=ps_[pair][:, 0:1024],
                    func=ACT.Exp, scale=0.125, bias=bias_ap),
                    waits=(tmm2, bias_tok[half],
                           hp_state.get("av_last_%d" % half) if g == 0 else None),
                    inc="scalar")
                exp_hist[gg] = texp
                glb_g[0] = gg + 1
                exp_toks[g] = texp
                for kt in (2 * g, 2 * g + 1):
                    d = kt - 4 * qb
                    if d >= 0:
                        gates[kt] = sc.add("gpsimd", lambda g2, half=half, kt=kt, d=d: g2.tensor_tensor(
                            out=expw[half][:, kt * 512:(kt + 1) * 512],
                            in0=expw[half][:, kt * 512:(kt + 1) * 512],
                            in1=dmask[:, d * 512:(d + 1) * 512], op=ALU.mult),
                            waits=(texp, t_dm), inc="gpsimd")
                    else:
                        gates[kt] = texp
            prev_exp = exp_toks[nkt // 2 - 1]
            hp_state["score_last"] = tmm2
            if pending is not None:
                finish_av(pending)
            pending = [hp, qb, half, nkt, gates, 0, None,
                       (4 * hp + qb * 2 + half) % 2]
        t_zn_last = finish_av(pending)
        vx_free = [t_attn_last_te, t_attn_last_te]
    t_attn_z = t_zn_last

    t_zrp = tuple(t_zrepack[p] for p in range(H // 2))
    wos = [mem.sb(f"wos{i}", [128, D], F32) for i in range(2)]
    wos_free = [None, None]
    wo_sp_fin = [None, None]
    t_wo_sp = None
    for tt in range(NT):
        i = tt % 2
        t_ev = None
        for c in range(2):
            pp = qkv_slot(len(qkv_hist))
            gate = qkv_hist[-4] if len(qkv_hist) >= 4 else None
            tmm = None
            for kd in range(ND):
                tmm = sc.add("tensor", lambda te, pp=pp, kd=kd, tt=tt, c=c: te.matmul(
                    pp, zT2[:, kd * S + tt * 128: kd * S + (tt + 1) * 128],
                    wo2[:, kd * D + c * 512: kd * D + (c + 1) * 512],
                    start=(kd == 0), stop=(kd == ND - 1)),
                    waits=(t_attn_z, wo_tks[kd], gate) + t_zrp,
                    inc="tensor" if kd == ND - 1 else None)
            t_ev = sc.add("scalar", lambda s2, pp=pp, i=i, c=c: s2.activation(
                out=wos[i][:, c * 512:(c + 1) * 512], in_=pp, func=ACT.Copy),
                waits=(tmm, wos_free[i]), inc="scalar")
            qkv_hist.append(t_ev)
        t_wo_sp = sc.dma("sync", f"wos{i}", attn_d[tt * 128:(tt + 1) * 128, :],
                         wos[i][:], waits=(t_ev,))
        wos_free[i] = t_wo_sp
        wo_sp_fin[i] = t_wo_sp

    mem.free("kT2", "qT2", "vx0", "vx1", "nsc", "expw0", "expw1", "zT2", "zTo",
             "zsc0", "zsc1", "zsc2", "zsc3", "denbc0", "denbc1", "denbc2",
             "denbc3", "denbr0", "denbr1", "denbr2", "denbr3", "wo2", "wos0", "wos1",
             "ps0", "ps1", "pzx", "pmx")

    jw2 = (sc.now("vector"), sc.now("scalar"), sc.now("tensor"),
           sc.now("wos0") if "wos0" in sc.cnt else None,
           sc.now("wos1") if "wos1" in sc.cnt else None)
    BUFN = mem.sb("BUF", [128, NI * S], BF16)
    iscr = mem.sb("iscr", [128, I3], I32)
    smx = mem.sb("smx", [128, 128], F32)
    wnat = [mem.sb(f"wnat{i}", [128, D], F32, side="right") for i in range(3)]
    wdqo = [mem.sb(f"wdqo{i}", [128, D], BF16, side="right") for i in range(2)]
    wnatD = [mem.sb("wnatD", [128, I3], F32, side="right")]
    wdqoD = [mem.sb("wdqoD", [128, I3], BF16, side="right")]
    bmx = mem.sb("bmx", [128, 128], F32, side="right")
    rix = mem.sb("rix", [128, 128], F32, side="right")
    wtst = [mem.sb("wtst0", [128, I3], BF16, side="right")]
    ptr2 = [mem.ps(f"ptr2_{i}", [128, 128], BF16) for i in range(2)]

    wnat_free = [None, None, None]
    wdqo_free = [None, None]
    wnatD_free = [None]
    wdqoD_free = [None]
    wtst_free = [None]
    UG_BUFS = (wnat, wdqo, wnat_free, wdqo_free, "wnat")
    DN_BUFS = (wnatD, wdqoD, wnatD_free, wdqoD_free, "wnatD")
    ptr2_free = {}
    prev_fscr = None

    def dequant_tile(idx, src_ap, w, waits=(), bufs=None):
        wnat_, wdqo_, wnat_free_, wdqo_free_, strm = bufs
        in_ = idx % len(wnat_)
        i = idx % len(wdqo_)
        nb = w // 32
        t_l = sc.dma("sync", strm + str(in_), wnat_[in_][:, 0:w], src_ap,
                     waits=(wnat_free_[in_],) + jw2)
        w3 = wnat_[in_][:, 0:w].rearrange("p (n b) -> p n b", b=32)
        t_b = sc.add("vector", lambda v, w3=w3, nb=nb: v.reduce_max(
            out=bmx[:, 96:96 + nb], in_=w3, axis=AX, apply_absolute_value=True),
            waits=(t_l, dq_state["prev_q"]), inc="vector")
        t_s = sc.add("vector", lambda v, nb=nb: v.tensor_scalar(
            out=smx[:, 96:96 + nb], in0=bmx[:, 96:96 + nb], scalar1=1.0 / 7.0,
            scalar2=EPS12, op0=ALU.mult, op1=ALU.max), waits=(t_b,), inc="vector")
        t_r = sc.add("vector", lambda v, nb=nb: v.reciprocal(
            out=rix[:, 96:96 + nb], in_=smx[:, 96:96 + nb]), waits=(t_s,), inc="vector")
        rb = rix[:, 96:96 + nb].unsqueeze(2).broadcast_to([128, nb, 32])
        sb_ = smx[:, 96:96 + nb].unsqueeze(2).broadcast_to([128, nb, 32])
        t_q = sc.add("vector", lambda v, w3=w3, rb=rb, w=w: v.tensor_tensor(
            out=iscr[:, 0:w].rearrange("p (n b) -> p n b", b=32), in0=w3, in1=rb,
            op=ALU.mult), waits=(t_r, dq_state["prev_q"]), inc="vector")
        dq_state["prev_q"] = t_q
        t_w2 = sc.add("vector", lambda v, i=i, sb_=sb_, w=w, wdqo_=wdqo_: v.tensor_tensor(
            out=wdqo_[i][:, 0:w].rearrange("p (n b) -> p n b", b=32),
            in0=iscr[:, 0:w].rearrange("p (n b) -> p n b", b=32), in1=sb_,
            op=ALU.mult), waits=(t_q, wdqo_free_[i]) + tuple(waits), inc="vector")
        wnat_free_[in_] = t_w2
        return i, t_w2

    dq_state = {"prev_q": None}
    t_dq_down = None

    def down_dq_chain(j):
        t_l = sc.dma("sync", "wnatD0", wnatD[0][:], wdown_d[j * 128:(j + 1) * 128, :],
                     waits=(wnatD_free[0],) + jw2)
        w3 = wnatD[0][:].rearrange("p (n b) -> p n b", b=32)
        t_b = sc.add("vector", lambda v, w3=w3: v.reduce_max(
            out=bmx[:, 0:96], in_=w3, axis=AX, apply_absolute_value=True),
            waits=(t_l, dq_state["prev_q"]), inc="vector")
        t_s = sc.add("vector", lambda v: v.tensor_scalar(
            out=smx[:, 0:96], in0=bmx[:, 0:96], scalar1=1.0 / 7.0, scalar2=EPS12,
            op0=ALU.mult, op1=ALU.max), waits=(t_b,), inc="vector")
        t_r = sc.add("vector", lambda v: v.reciprocal(
            out=rix[:, 0:96], in_=smx[:, 0:96]), waits=(t_s,), inc="vector")
        rb = rix[:, 0:96].unsqueeze(2).broadcast_to([128, 96, 32])
        sb_ = smx[:, 0:96].unsqueeze(2).broadcast_to([128, 96, 32])
        t_q = sc.add("vector", lambda v, w3=w3, rb=rb: v.tensor_tensor(
            out=iscr[:, 0:I3].rearrange("p (n b) -> p n b", b=32), in0=w3,
            in1=rb, op=ALU.mult), waits=(t_r, dq_state["prev_q"]), inc="vector")
        dq_state["prev_q"] = t_q
        t_w2 = sc.add("vector", lambda v, sb_=sb_: v.tensor_tensor(
            out=wdqoD[0][:].rearrange("p (n b) -> p n b", b=32),
            in0=iscr[:, 0:I3].rearrange("p (n b) -> p n b", b=32), in1=sb_,
            op=ALU.mult), waits=(t_q, wdqoD_free[0]), inc="vector")
        dq_state["down_w2"] = t_w2

    def down_dq_te(j):
        nonlocal t_dq_down
        t_w2 = dq_state["down_w2"]
        tevT = None
        for k in range(NI):
            pp = ptr2[k % 2]
            tmmT = sc.add("tensor", lambda te, pp=pp, k=k: te.transpose(
                pp[:], wdqoD[0][:, k * 128:(k + 1) * 128], ident[:]),
                waits=(t_w2, t_id, ptr2_free.get(k % 2)), inc="tensor")
            tevT = sc.add("scalar", lambda s2, pp=pp, k=k: s2.activation(
                out=wtst[0][:, k * 128:(k + 1) * 128], in_=pp[:], func=ACT.Copy),
                waits=(tmmT, wtst_free[0] if k == 0 else None), inc="scalar")
            ptr2_free[k % 2] = tevT
        wdqoD_free[0] = tevT
        wnatD_free[0] = tevT
        t_dq_down = sc.dma(
            "sync", "wtsp0",
            wdnT_d[:, j * 128:(j + 1) * 128].rearrange("(t p) c -> p t c", p=128),
            wtst[0][:].rearrange("p (t c) -> p t c", c=128), waits=(tevT,))
        wtst_free[0] = t_dq_down

    t_xq = {}
    prev_xq = None
    for dt in range(ND):
        sl = xnT[:, dt * S:(dt + 1) * S]
        ti = sc.add("vector", lambda v, sl=sl: v.tensor_scalar(
            out=iscr[:, 0:S], in0=sl, scalar1=scal[:, 1:2], scalar2=None,
            op0=ALU.mult), waits=(t_rx, t_qkv_last, prev_xq), inc="vector")
        t_xq[dt] = prev_xq = sc.add("vector", lambda v, sl=sl: v.tensor_scalar(
            out=sl, in0=iscr[:, 0:S], scalar1=scal[:, 0:1], scalar2=None,
            op0=ALU.mult), waits=(ti,), inc="vector")
    t_xq_all = t_xq[ND - 1]

    upst = [mem.sb(f"upst{i}", [128, S], BF16) for i in range(2)]
    wstU = [mem.sb(f"wstU{i}", [128, D], BF16) for i in range(2)]
    wstG = [mem.sb(f"wstG{i}", [128, D], BF16) for i in range(2)]
    pug = [mem.ps(f"pug{i}", [128, 512]) for i in range(2)]

    wst_free = {"up": [None, None], "gate": [None, None]}
    upst_free = [None, None]
    t_mx_prev = None
    deferred_mx = None
    pug_hist = []
    t_upmax = None
    t_gatemax = None
    dq_state["prev_q"] = t_xq_all
    for it in range(NI):
        i = it % 2
        wtoks = {}
        dq2 = {}
        for nmi, (nm, wbuf, srcw_d) in enumerate(
                (("up", wstU, wup_d), ("gate", wstG, wgate_d))):
            dqi, t_w2 = dequant_tile(2 * it + nmi, srcw_d[it * 128:(it + 1) * 128, :],
                                     D, bufs=UG_BUFS)
            dq2[nm] = (dqi, t_w2)
        if it % 3 == 0:
            down_dq_chain(it // 3)
        if it % 3 == 2:
            down_dq_te(it // 3)
        for nmi, (nm, wbuf, srcw_d) in enumerate(
                (("up", wstU, wup_d), ("gate", wstG, wgate_d))):
            dqi, t_w2 = dq2[nm]
            tevT = None
            for kd in range(ND):
                pp = ptr2[kd % 2]
                tmmT = sc.add("tensor", lambda te, pp=pp, dqi=dqi, kd=kd: te.transpose(
                    pp[:], wdqo[dqi][:, kd * 128:(kd + 1) * 128], ident[:]),
                    waits=(t_w2, t_id, ptr2_free.get(kd % 2)), inc="tensor")
                tevT = sc.add("scalar", lambda s2, pp=pp, wbuf=wbuf, i=i, kd=kd: s2.activation(
                    out=wbuf[i][:, kd * 128:(kd + 1) * 128], in_=pp[:], func=ACT.Copy),
                    waits=(tmmT, wst_free[nm][i]), inc="scalar")
                ptr2_free[kd % 2] = tevT
            wdqo_free[dqi] = tevT
            wtoks[nm] = tevT
        for nm, wbuf in (("up", wstU), ("gate", wstG)):
            ev_t = None
            for c in range(4):
                pp = pug[len(pug_hist) % 2]
                gate = pug_hist[-2] if len(pug_hist) >= 2 else None
                tmm = None
                for kd in range(ND):
                    tmm = sc.add("tensor", lambda te, pp=pp, wbuf=wbuf, i=i, kd=kd, c=c: te.matmul(
                        pp[:], wbuf[i][:, kd * 128:(kd + 1) * 128],
                        xnT[:, kd * S + c * 512: kd * S + (c + 1) * 512],
                        start=(kd == 0), stop=(kd == ND - 1)),
                        waits=(wtoks[nm], t_xq[kd], gate),
                        inc="tensor" if kd == ND - 1 else None)
                if nm == "up":
                    out_ap = upst[i][:, c * 512:(c + 1) * 512]
                    ev_t = sc.add("scalar", lambda s2, pp=pp, out_ap=out_ap: s2.activation(
                        out=out_ap, in_=pp[:], func=ACT.Copy),
                        waits=(tmm, upst_free[i] if c == 0 else None),
                        inc="scalar")
                else:
                    out_ap = BUFN[:, it * S + c * 512: it * S + (c + 1) * 512]
                    ev_t = sc.add("scalar", lambda s2, pp=pp, out_ap=out_ap: s2.activation(
                        out=out_ap, in_=pp[:], func=ACT.Copy), waits=(tmm,), inc="scalar")
                pug_hist.append(ev_t)
            wst_free[nm][i] = ev_t
            if deferred_mx is not None:
                (d_src, d_cslot, d_ev) = deferred_mx
                t_red = sc.add("vector", lambda v, d_src=d_src: v.reduce_max(
                    out=cmx[:, 7:8], in_=d_src, axis=AX, apply_absolute_value=True),
                    waits=(d_ev, t_mx_prev), inc="vector")
                t_mx = sc.add("vector", lambda v, d_cslot=d_cslot: v.tensor_tensor(
                    out=cmx[:, d_cslot:d_cslot + 1], in0=cmx[:, d_cslot:d_cslot + 1],
                    in1=cmx[:, 7:8], op=ALU.max), waits=(t_red, t_cmx0), inc="vector")
                t_mx_prev = t_mx
                if d_cslot == 1:
                    t_upmax = t_mx
                else:
                    t_gatemax = t_mx
            src2 = upst[i][:] if nm == "up" else BUFN[:, it * S:(it + 1) * S]
            deferred_mx = (src2, 1 if nm == "up" else 2, ev_t)
            if nm == "up":
                upst_free[i] = sc.dma("sync", f"upst{i}", up_d[it * 128:(it + 1) * 128, :],
                                      upst[i][:], waits=(ev_t,))
    (d_src, d_cslot, d_ev) = deferred_mx
    t_red = sc.add("vector", lambda v, d_src=d_src: v.reduce_max(
        out=cmx[:, 7:8], in_=d_src, axis=AX, apply_absolute_value=True),
        waits=(d_ev, t_mx_prev), inc="vector")
    t_gatemax = sc.add("vector", lambda v, d_cslot=d_cslot: v.tensor_tensor(
        out=cmx[:, d_cslot:d_cslot + 1], in0=cmx[:, d_cslot:d_cslot + 1],
        in1=cmx[:, 7:8], op=ALU.max), waits=(t_red, t_cmx0), inc="vector")
    t_dq_down_all = (wtst_free[0], wtst_free[0])
    mem.free("wstU0", "wstU1", "wstG0", "wstG1")
    mem.free("wtst0", "rix", "bmx", "wdqoD", "wnatD", "wdqo0", "wdqo1",
             "wnat0", "wnat1", "wnat2", "xnT")

    t_u1 = tree(cmx[:, 1:2], ccs[0:1, 16:17], ALU.max, 128, (t_upmax,))
    t_g1 = tree(cmx[:, 2:3], ccs[0:1, 17:18], ALU.max, 128, (t_gatemax, t_u1))
    t_bc1 = allreduce(1, t_g1)
    t_su = sc.add("vector", lambda v: v.tensor_scalar(out=scal[:, 2:3], in0=ccb[:, 16:17],
                  scalar1=1.0 / QMAX, scalar2=EPS12, op0=ALU.mult, op1=ALU.max),
                  waits=(t_bc1,), inc="vector")
    t_ru = sc.add("vector", lambda v: v.reciprocal(out=scal[:, 3:4], in_=scal[:, 2:3]),
                  waits=(t_su,), inc="vector")
    t_sg = sc.add("vector", lambda v: v.tensor_scalar(out=scal[:, 4:5], in0=ccb[:, 17:18],
                  scalar1=1.0 / QMAX, scalar2=EPS12, op0=ALU.mult, op1=ALU.max),
                  waits=(t_bc1,), inc="vector")
    t_rg = sc.add("vector", lambda v: v.reciprocal(out=scal[:, 5:6], in_=scal[:, 4:5]),
                  waits=(t_sg,), inc="vector")

    wdn = mem.sb("wdn", [128, 2 * NI * 512], BF16)
    t_wdn = None
    for dc in range(2):
        for it2 in range(NI):
            t_wdn = sc.dma(
                "sync", "wdn",
                wdn[:, dc * NI * 512 + it2 * 512: dc * NI * 512 + (it2 + 1) * 512],
                wdnT_d[it2 * 128:(it2 + 1) * 128, dc * 512:(dc + 1) * 512],
                waits=t_dq_down_all)
    iA = [mem.sb(f"iA{i}", [128, S], I32) for i in range(2)]
    iB1 = mem.sb("iB0", [128, S], I32)
    iB = [iB1, iB1]
    fsg1 = mem.sb("fsg0", [128, S], F32)
    fsg = [fsg1, fsg1]

    iA_free = [None, None]
    iB_free = [None, None]
    fsg_free = [None, None]
    t_pmax = None
    for it in range(NI):
        i = it % 2
        buf_sl = BUFN[:, it * S:(it + 1) * S]
        tGo = sc.add("scalar", lambda s2, buf_sl=buf_sl, i=i: s2.activation(
            out=iA[i][:], in_=buf_sl, func=ACT.Copy, scale=scal[:, 5:6]),
            waits=(t_rg, iA_free[i], t_dq_down_all[0], t_dq_down_all[1]), inc="scalar")
        tsg_ = sc.add("scalar", lambda s2, i=i: s2.activation(
            out=fsg[i][:], in_=iA[i][:], func=ACT.Sigmoid, scale=scal[:, 4:5]),
            waits=(tGo, fsg_free[0]), inc="scalar")
        tSo = sc.add("scalar", lambda s2, i=i: s2.activation(
            out=iB[i][:], in_=fsg[i][:], func=ACT.Copy, scale=65536.0),
            waits=(tsg_, iB_free[0]), inc="scalar")
        fsg_free[0] = tSo
        tP = sc.add("vector", lambda v, buf_sl=buf_sl, i=i: v.tensor_tensor(
            out=buf_sl, in0=iA[i][:], in1=iB[i][:], op=ALU.mult),
            waits=(tSo,), inc="vector")
        iA_free[i] = tP
        iB_free[0] = tP
        t_red = sc.add("vector", lambda v, buf_sl=buf_sl: v.reduce_max(
            out=cmx[:, 7:8], in_=buf_sl, axis=AX, apply_absolute_value=True),
            waits=(tP,), inc="vector")
        t_pmax = sc.add("vector", lambda v: v.tensor_tensor(
            out=cmx[:, 3:4], in0=cmx[:, 3:4], in1=cmx[:, 7:8], op=ALU.max),
            waits=(t_red,), inc="vector")

    t_p1 = tree(cmx[:, 3:4], ccs[0:1, 32:33], ALU.max, 128, (t_pmax,))
    t_bc2 = allreduce(2, t_p1)
    t_a = sc.add("vector", lambda v: v.tensor_tensor(out=scal[:, 22:23],
                 in0=ccb[:, 32:33], in1=scal[:, 4:5], op=ALU.mult),
                 waits=(t_bc2,), inc="vector")
    t_b2 = sc.add("vector", lambda v: v.tensor_scalar(out=scal[:, 6:7], in0=scal[:, 22:23],
                  scalar1=1.0 / (65536.0 * QMAX), scalar2=EPS12, op0=ALU.mult,
                  op1=ALU.max), waits=(t_a,), inc="vector")
    t_c = sc.add("vector", lambda v: v.reciprocal(out=scal[:, 23:24], in_=scal[:, 6:7]),
                 waits=(t_b2,), inc="vector")
    t_d2 = sc.add("vector", lambda v: v.tensor_scalar(out=scal[:, 24:25], in0=scal[:, 4:5],
                  scalar1=1.0 / 65536.0, scalar2=None, op0=ALU.mult), inc="vector")
    t_r2 = sc.add("vector", lambda v: v.tensor_tensor(out=scal[:, 7:8], in0=scal[:, 24:25],
                  in1=scal[:, 23:24], op=ALU.mult), waits=(t_c, t_d2), inc="vector")

    t_omax = None
    for it in range(NI):
        i = it % 2
        buf_sl = BUFN[:, it * S:(it + 1) * S]
        t_ul = sc.dma("sync", f"upst{i}", upst[i][:], up_d[it * 128:(it + 1) * 128, :],
                      waits=(upst_free[i],))
        tR2 = sc.add("scalar", lambda s2, buf_sl=buf_sl, i=i: s2.activation(
            out=iA[i][:], in_=buf_sl, func=ACT.Copy, scale=scal[:, 7:8]),
            waits=(t_r2, iA_free[i]), inc="scalar")
        tU = sc.add("scalar", lambda s2, i=i: s2.activation(
            out=iB[i][:], in_=upst[i][:], func=ACT.Copy, scale=scal[:, 3:4]),
            waits=(t_ru, t_ul, iB_free[0]), inc="scalar")
        tO = sc.add("vector", lambda v, buf_sl=buf_sl, i=i: v.tensor_tensor(
            out=buf_sl, in0=iA[i][:], in1=iB[i][:], op=ALU.mult),
            waits=(tR2, tU), inc="vector")
        iA_free[i] = tO
        iB_free[0] = tO
        upst_free[i] = tO
        t_red = sc.add("vector", lambda v, buf_sl=buf_sl: v.reduce_max(
            out=cmx[:, 7:8], in_=buf_sl, axis=AX, apply_absolute_value=True),
            waits=(tO,), inc="vector")
        t_omax = sc.add("vector", lambda v: v.tensor_tensor(
            out=cmx[:, 4:5], in0=cmx[:, 4:5], in1=cmx[:, 7:8], op=ALU.max),
            waits=(t_red,), inc="vector")

    t_o1t = tree(cmx[:, 4:5], ccs[0:1, 48:49], ALU.max, 128, (t_omax,))
    t_bc3 = allreduce(3, t_o1t)
    t_e = sc.add("vector", lambda v: v.tensor_tensor(out=scal[:, 25:26], in0=scal[:, 6:7],
                 in1=scal[:, 2:3], op=ALU.mult), waits=(t_b2, t_su), inc="vector")
    t_f = sc.add("vector", lambda v: v.tensor_tensor(out=scal[:, 22:23], in0=ccb[:, 48:49],
                 in1=scal[:, 25:26], op=ALU.mult), waits=(t_bc3, t_e), inc="vector")
    t_g = sc.add("vector", lambda v: v.tensor_scalar(out=scal[:, 26:27], in0=scal[:, 22:23],
                 scalar1=1.0 / QMAX, scalar2=EPS12, op0=ALU.mult, op1=ALU.max),
                 waits=(t_f,), inc="vector")
    t_h = sc.add("vector", lambda v: v.reciprocal(out=scal[:, 27:28], in_=scal[:, 26:27]),
                 waits=(t_g,), inc="vector")
    t_r3 = sc.add("vector", lambda v: v.tensor_tensor(out=scal[:, 28:29], in0=scal[:, 25:26],
                  in1=scal[:, 27:28], op=ALU.mult), waits=(t_h,), inc="vector")

    t_oq = {}
    for it in range(NI):
        i = it % 2
        buf_sl = BUFN[:, it * S:(it + 1) * S]
        tq1 = sc.add("scalar", lambda s2, buf_sl=buf_sl, i=i: s2.activation(
            out=iA[i][:], in_=buf_sl, func=ACT.Copy, scale=scal[:, 28:29]),
            waits=(t_r3, iA_free[i]), inc="scalar")
        t_oq[it] = sc.add("vector", lambda v, buf_sl=buf_sl, i=i: v.tensor_copy(
            out=buf_sl, in_=iA[i][:]), waits=(tq1,), inc="vector")
        iA_free[i] = t_oq[it]

    mem.free("iA0", "iA1", "iB0", "fsg0", "pug0", "pug1")

    jw6 = (sc.now("vector"), sc.now("scalar"))
    xrl = [mem.sb(f"xrl{i}", [128, D], F32) for i in range(2)]
    arl = [mem.sb(f"arl{i}", [128, D], F32) for i in range(2)]
    ost = [mem.sb(f"ost{i}", [128, D], F32) for i in range(2)]
    pdn = [mem.ps(f"pdn{i}", [128, 512]) for i in range(2)]

    xrl_free = [None, None]
    arl_free = [None, None]
    ost_free = [None, None]
    pdn_hist = []
    out_fin = [None, None]
    for tt in range(NT):
        i = tt % 2
        t_xl = sc.dma("sync", f"xrl{i}", xrl[i][:], x_d[tt * 128:(tt + 1) * 128, :],
                      waits=(xrl_free[i],) + jw6)
        t_al = sc.dma("sync", f"arl{i}", arl[i][:], attn_d[tt * 128:(tt + 1) * 128, :],
                      waits=(arl_free[i], wo_sp_fin[0], wo_sp_fin[1]) + jw6)
        for dc in range(2):
            pp = pdn[len(pdn_hist) % 2]
            gate = pdn_hist[-2] if len(pdn_hist) >= 2 else None
            tmm = None
            for it in range(NI):
                tmm = sc.add("tensor", lambda te, pp=pp, it=it, tt=tt, dc=dc: te.matmul(
                    pp[:], BUFN[:, it * S + tt * 128: it * S + (tt + 1) * 128],
                    wdn[:, dc * NI * 512 + it * 512: dc * NI * 512 + (it + 1) * 512],
                    start=(it == 0), stop=(it == NI - 1)),
                    waits=(t_oq[it], t_wdn, gate),
                    inc="tensor" if it == NI - 1 else None)
            t_ev = sc.add("scalar", lambda s2, pp=pp, i=i, dc=dc: s2.activation(
                out=ost[i][:, dc * 512:(dc + 1) * 512], in_=pp[:], func=ACT.Copy,
                scale=scal[:, 26:27]),
                waits=(tmm, t_g, ost_free[i] if dc == 0 else None), inc="scalar")
            pdn_hist.append(t_ev)
            t_a1 = sc.add("vector", lambda v, i=i, dc=dc: v.tensor_tensor(
                out=ost[i][:, dc * 512:(dc + 1) * 512],
                in0=ost[i][:, dc * 512:(dc + 1) * 512],
                in1=xrl[i][:, dc * 512:(dc + 1) * 512], op=ALU.add),
                waits=(t_ev, t_xl), inc="vector")
            t_a2 = sc.add("vector", lambda v, i=i, dc=dc: v.tensor_tensor(
                out=ost[i][:, dc * 512:(dc + 1) * 512],
                in0=ost[i][:, dc * 512:(dc + 1) * 512],
                in1=arl[i][:, dc * 512:(dc + 1) * 512], op=ALU.add),
                waits=(t_a1, t_al), inc="vector")
        t_st = sc.dma("sync", f"ost{i}", out_d[tt * 128:(tt + 1) * 128, :], ost[i][:],
                      waits=(t_a2,))
        xrl_free[i] = t_st
        arl_free[i] = t_st
        ost_free[i] = t_st
        out_fin[i] = t_st

    sc.add("gpsimd", fn=None, waits=(out_fin[0], out_fin[1]))

    with nc.Block() as block:
        sc.emit(block)
    es.close()
    return nc


_NC_CACHE = [None]


def kernel(L0_attn_x, L0_attn_W_K, L0_attn_W_Q, L0_attn_W_V, L0_attn_W_O,
           L0_ffn_w_gate, L0_ffn_w_up, L0_ffn_w_down):
    f = np.float32
    x = np.ascontiguousarray(np.asarray(L0_attn_x, f))
    wq = np.ascontiguousarray(np.asarray(L0_attn_W_Q, f).transpose(2, 0, 1).reshape(D, D))
    wk = np.ascontiguousarray(np.asarray(L0_attn_W_K, f).transpose(2, 0, 1).reshape(D, D))
    wv = np.ascontiguousarray(np.asarray(L0_attn_W_V, f).transpose(2, 0, 1).reshape(D, D))
    wo = np.ascontiguousarray(np.asarray(L0_attn_W_O, f).T)
    wupn = np.ascontiguousarray(np.asarray(L0_ffn_w_up, f))
    wgn = np.ascontiguousarray(np.asarray(L0_ffn_w_gate, f))
    wdn_ = np.ascontiguousarray(np.asarray(L0_ffn_w_down, f))
    ident = np.eye(128, dtype=BF)
    identf = np.eye(128, dtype=f)
    kk = np.arange(128)[:, None]
    dmsk = np.concatenate(
        [((d * 128 + kk) <= np.arange(512)[None, :]) for d in range(4)],
        axis=1).astype(BF)
    ones1 = np.ones((1, 128), dtype=BF)
    ones1f = np.ones((1, 128), dtype=f)
    onesc = np.ones((128, 1), dtype=BF)

    if _NC_CACHE[0] is None:
        _NC_CACHE[0] = build_nc()
    nc = _NC_CACHE[0]
    in_maps = []
    for c in range(NCORES):
        in_maps.append({"x": np.ascontiguousarray(x[c]), "wqT": wq, "wkT": wk,
                        "wvT": wv, "woT": wo, "wup": wupn, "wgate": wgn,
                        "wdown": wdn_, "ident": ident, "dmask": dmsk,
                        "ones1": ones1, "ones1f": ones1f, "onesc": onesc, "identf": identf})
    res = run_bass_kernel_spmd(nc, in_maps, core_ids=list(range(NCORES)))
    return np.stack([res.results[c]["out"] for c in range(NCORES)], axis=0)
